# version 08 (final)
# speedup vs baseline: 1.3571x; 1.0252x over previous
import contextlib
import math
import numpy as np
import ml_dtypes
import concourse.bass as bass
import concourse.mybir as mybir
from concourse.bass import IndirectOffsetOnAxis
from concourse.bass_utils import run_bass_kernel_spmd

F32 = mybir.dt.float32
BF16 = mybir.dt.bfloat16
I32 = mybir.dt.int32
AF = mybir.ActivationFunctionType
AL = mybir.AluOpType
AX = mybir.AxisListType

P = 128
NCORES = 8
BATCH = 2
SEQ = 8192
DM = 1024
NTOK = BATCH * SEQ
TPC = NTOK // NCORES
DC = DM // P
TILES_PC = TPC // P

H0, E0 = 16, 64
LN_EPS0 = 1e-3
NEXP, CAP, DFF0 = 8, 3072, 4096
H1, DK1, DC1, DR1 = 8, 128, 32, 32
DI1, QB = 2816, 16
GLUE_EPS = 1e-5
BIGF = float(1 << 27)

NB_DM = DM // QB
NB_DI = DI1 // QB


class Eng:

    def __init__(self, raw, sem, dsem):
        self.r = raw
        self.sem = sem
        self.dsem = dsem
        self.cnt = 0
        self.dcnt = 0

    def op(self, inst):
        self.cnt += 1
        inst.then_inc(self.sem, 1)
        return (self.sem, self.cnt)

    def dma(self, inst):
        self.dcnt += 16
        inst.then_inc(self.dsem, 16)
        return (self.dsem, self.dcnt)

    def w(self, *ms):
        for m in ms:
            if m is not None:
                self.r.wait_ge(m[0], m[1])

    def mark(self):
        return (self.sem, self.cnt)

    def dmark(self):
        return (self.dsem, self.dcnt)


class K:
    def __init__(self, name):
        self.nc = bass.Bass("TRN2", target_bir_lowering=False, name=name)
        self.ctx = contextlib.ExitStack()

    def start(self):
        nc = self.nc
        self.gp = Eng(nc.gpsimd, self.ctx.enter_context(nc.semaphore("s_gp")),
                      self.ctx.enter_context(nc.semaphore("d_gp")))
        self.te = Eng(nc.tensor, self.ctx.enter_context(nc.semaphore("s_te")),
                      self.ctx.enter_context(nc.semaphore("d_te")))
        self.sc = Eng(nc.scalar, self.ctx.enter_context(nc.semaphore("s_sc")),
                      self.ctx.enter_context(nc.semaphore("d_sc")))
        self.ve = Eng(nc.vector, self.ctx.enter_context(nc.semaphore("s_ve")),
                      self.ctx.enter_context(nc.semaphore("d_ve")))
        self.sy = Eng(nc.sync, self.ctx.enter_context(nc.semaphore("s_sy")),
                      self.ctx.enter_context(nc.semaphore("d_sy")))
        self.ccsems = []

    def ccsem(self):
        s = self.ctx.enter_context(self.nc.semaphore(f"s_cc{len(self.ccsems)}"))
        self.ccsems.append(s)
        return s

    def sbuf(self, name, shape, dtype, ctx=None):
        return (ctx or self.ctx).enter_context(self.nc.sbuf_tensor(name, shape, dtype))

    def psum(self, name, shape, dtype=F32, ctx=None):
        return (ctx or self.ctx).enter_context(self.nc.psum_tensor(name, shape, dtype))



def e_ln(k, x_f32, xln_out, eps, n_feat, sq, st, after=()):
    ve, sc = k.ve, k.sc
    ve.w(*after)
    m_sum = ve.op(ve.r.reduce_sum(st[:, 0:1], x_f32, axis=AX.X))
    sc.w(*after)
    m_sq = sc.op(sc.r.activation(out=sq, in_=x_f32, func=AF.Square))
    ve.w(m_sq)
    m_ss = ve.op(ve.r.reduce_sum(st[:, 1:2], sq, axis=AX.X))
    ve.w(m_sum)
    m_mean = ve.op(ve.r.tensor_scalar_mul(st[:, 2:3], st[:, 0:1], 1.0 / n_feat))
    ve.w(m_ss)
    m_ex2 = ve.op(ve.r.tensor_scalar_mul(st[:, 3:4], st[:, 1:2], 1.0 / n_feat))
    ve.w(m_mean)
    m_m2 = ve.op(ve.r.tensor_tensor(st[:, 4:5], st[:, 2:3], st[:, 2:3], AL.mult))
    ve.w(m_ex2, m_m2)
    m_var = ve.op(ve.r.tensor_tensor(st[:, 5:6], st[:, 3:4], st[:, 4:5], AL.subtract))
    ve.w(m_var)
    m_vpe = ve.op(ve.r.tensor_single_scalar(st[:, 5:6], st[:, 5:6], float(eps), AL.add))
    sc.w(m_vpe)
    m_ln = sc.op(sc.r.activation(out=st[:, 6:7], in_=st[:, 5:6], func=AF.Ln))
    sc.w(m_ln)
    m_rstd = sc.op(sc.r.activation(out=st[:, 7:8], in_=st[:, 6:7], func=AF.Exp, scale=-0.5))
    ve.w(m_rstd)
    m_b2 = ve.op(ve.r.scalar_tensor_tensor(st[:, 4:5], st[:, 2:3], -1.0, st[:, 7:8], AL.mult, AL.mult))
    ve.w(m_b2)
    m_out = ve.op(ve.r.tensor_scalar(xln_out, x_f32, st[:, 7:8], st[:, 4:5], AL.mult, AL.add))
    return m_out


def e_quant(k, x_in, q_out, nb, t, after=()):
    ve, sc = k.ve, k.sc
    t = {kk: (v if isinstance(v, bass.AP) else v[:]) for kk, v in t.items()}
    xv = x_in.rearrange("p (nb b) -> p nb b", b=QB)
    ve.w(*after)
    m_am = ve.op(ve.r.tensor_reduce(t["am"], xv, axis=AX.X, op=AL.max, apply_absolute_value=True))
    ve.w(m_am)
    m_sc = ve.op(ve.r.tensor_scalar(t["scs"], t["am"], 1.0 / 6.0, 1e-12, AL.mult, AL.max))
    ve.w(m_sc)
    m_rc = ve.op(ve.r.reciprocal(t["rcs"], t["scs"]))
    nbv = t["am"].shape[1]
    rb = t["rcs"].unsqueeze(2).broadcast_to([t["rcs"].shape[0], nbv, QB])
    sb = t["scs"].unsqueeze(2).broadcast_to([t["scs"].shape[0], nbv, QB])
    ve.w(m_rc)
    m_xn = ve.op(ve.r.tensor_tensor(t["xn"].rearrange("p (nb b) -> p nb b", b=QB), xv, rb, AL.mult))
    sc.w(m_xn)
    m_sg = sc.op(sc.r.sign(out=t["sgn"], in_=t["xn"]))
    ve.w(m_xn)
    m_a = ve.op(ve.r.scalar_tensor_tensor(t["a"], t["xn"], -1.0, t["xn"], AL.mult, AL.max))
    ve.w(m_a)
    m = ve.op(ve.r.tensor_single_scalar(t["acc"], t["a"], 0.25, AL.is_gt))
    for th in (0.75, 1.25, 1.75):
        ve.w(m)
        m = ve.op(ve.r.scalar_tensor_tensor(t["acc"], t["a"], float(th), t["acc"], AL.is_gt, AL.add))
    m2 = ve.op(ve.r.tensor_single_scalar(t["acc2"], t["a"], 2.5, AL.is_gt))
    for th in (3.5, 5.0, 5.0):
        ve.w(m2)
        m2 = ve.op(ve.r.scalar_tensor_tensor(t["acc2"], t["a"], float(th), t["acc2"], AL.is_gt, AL.add))
    ve.w(m, m2)
    mq = ve.op(ve.r.scalar_tensor_tensor(t["a"], t["acc"], 0.5, t["acc2"], AL.mult, AL.add))
    ve.w(mq, m_sg)
    mqs = ve.op(ve.r.tensor_tensor(t["a"], t["a"], t["sgn"], AL.mult))
    ve.w(mqs)
    m_out = ve.op(ve.r.tensor_tensor(q_out.rearrange("p (nb b) -> p nb b", b=QB),
                                     t["a"].rearrange("p (nb b) -> p nb b", b=QB), sb, AL.mult))
    return m_out


def e_transp(k, src_bf16, dst3, n_chunks, ps2, ident, after=()):
    te, sc = k.te, k.sc
    m_ev = [sc.mark(), sc.mark()]
    m_last = None
    for g0 in range(0, n_chunks, 4):
        ng = min(4, n_chunks - g0)
        bk = (g0 // 4) % 2
        ps = ps2[bk]
        te.w(*after)
        te.w(m_ev[bk])
        for j in range(ng):
            m_mm = te.op(te.r.matmul(ps[:, j * P:(j + 1) * P], src_bf16[:, (g0 + j) * P:(g0 + j + 1) * P],
                                     ident[:], start=True, stop=True))
        sc.w(m_mm)
        m_ev[bk] = sc.op(sc.r.activation(
            out=dst3[:, g0:g0 + ng, :],
            in_=ps[:, 0:ng * P].rearrange("p (g r) -> p g r", r=P), func=AF.Copy))
        m_last = m_ev[bk]
    return m_last


def host_consts():
    ident = np.eye(P, dtype=np.float32)
    triu = np.triu(np.ones((P, P), np.float32), 1)
    masks = np.zeros((4, P, 512), np.float32)
    kk = np.arange(P)[:, None]
    qq = np.arange(512)[None, :]
    for rel in range(4):
        masks[rel] = (qq >= rel * P + kk).astype(np.float32)
    e8 = np.tile(np.arange(8, dtype=np.float32)[None, :], (P, 1))
    return ident, triu, masks, e8



def nc_ffn1_prep():
    RG = DI1 // NCORES
    RD = DM // NCORES
    k = K("ffn1prep")
    nc = k.nc
    wg = nc.dram_tensor("wg", [RG, DM], F32, kind="ExternalInput").ap()
    wu = nc.dram_tensor("wu", [RG, DM], F32, kind="ExternalInput").ap()
    wd = nc.dram_tensor("wd", [RD, DI1], F32, kind="ExternalInput").ap()
    wgq = nc.dram_tensor("wgq", [RG, DM], BF16, kind="ExternalOutput").ap()
    wuq = nc.dram_tensor("wuq", [RG, DM], BF16, kind="ExternalOutput").ap()
    wdq = nc.dram_tensor("wdq", [RD, DI1], BF16, kind="ExternalOutput").ap()
    with k.ctx:
        k.start()
        xs = k.sbuf("xs", [P, DI1], F32)
        qo = k.sbuf("qo", [P, DI1], BF16)
        t = {
            "xn": k.sbuf("t_xn", [P, DI1], F32),
            "a": k.sbuf("t_a", [P, DI1], F32),
            "sgn": k.sbuf("t_sgn", [P, DI1], BF16),
            "acc": k.sbuf("t_acc", [P, DI1], BF16),
            "acc2": k.sbuf("t_acc2", [P, DI1], BF16),
            "am": k.sbuf("t_am", [P, NB_DI], F32),
            "scs": k.sbuf("t_scs", [P, NB_DI], F32),
            "rcs": k.sbuf("t_rcs", [P, NB_DI], F32),
        }
        with nc.Block() as block:
            @block.sync
            def _(_h):
                sy, ve, sc = k.sy, k.ve, k.sc
                jobs = []
                for (src, dst, rows, width, nb) in ((wg, wgq, RG, DM, NB_DM),
                                                    (wu, wuq, RG, DM, NB_DM),
                                                    (wd, wdq, RD, DI1, NB_DI)):
                    r = 0
                    while r < rows:
                        jobs.append((src, dst, r, min(P, rows - r), width, nb))
                        r += P
                for (src, dst, r0, nr, width, nb) in jobs:
                    sy.w(ve.mark())
                    m_ld = sy.dma(sy.r.dma_start(out=xs[0:nr, 0:width], in_=src[r0:r0 + nr, :]))
                    ve.w(sy.dmark())
                    sc.w(sy.dmark())
                    tt = {kk2: (v[0:nr, 0:nb] if kk2 in ("am", "scs", "rcs") else v[0:nr, 0:width])
                          for kk2, v in t.items()}
                    m_q = e_quant(k, xs[0:nr, 0:width], qo[0:nr, 0:width], nb, tt, after=(m_ld,))
                    sy.w(m_q)
                    m_store = sy.dma(sy.r.dma_start(out=dst[r0:r0 + nr, :], in_=qo[0:nr, 0:width]))
                sy.w(sy.dmark())
    return k.nc



def emit_ffn1(k, pfx, sctx, io):
    nc = k.nc
    IC = DI1 // P
    ident = k.ident
    wgT = k.sbuf(pfx + "wgT", [P, DC * DI1], BF16, ctx=sctx)
    wuT = k.sbuf(pfx + "wuT", [P, DC * DI1], BF16, ctx=sctx)
    wdT = k.sbuf(pfx + "wdT", [P, IC * DM], BF16, ctx=sctx)
    x_s = k.sbuf(pfx + "x_s", [P, DM], F32, ctx=sctx)
    xln = k.sbuf(pfx + "xln", [P, DM], F32, ctx=sctx)
    xq = k.sbuf(pfx + "xq", [P, DM], BF16, ctx=sctx)
    xqT = k.sbuf(pfx + "xqT", [P, DM], BF16, ctx=sctx)
    g_s = k.sbuf(pfx + "g_s", [P, DI1], F32, ctx=sctx)
    u_s = k.sbuf(pfx + "u_s", [P, DI1], F32, ctx=sctx)
    h_s = k.sbuf(pfx + "h_s", [P, DI1], F32, ctx=sctx)
    hq = k.sbuf(pfx + "hq", [P, DI1], BF16, ctx=sctx)
    hqT = k.sbuf(pfx + "hqT", [P, DI1], BF16, ctx=sctx)
    wtile = hq
    stat = k.sbuf(pfx + "stat", [P, 8], F32, ctx=sctx)
    tq = {
        "xn": g_s, "a": u_s,
        "sgn": hq,
        "acc": k.sbuf(pfx + "t_acc", [P, DI1], BF16, ctx=sctx),
        "acc2": hqT,
        "am": k.sbuf(pfx + "t_am", [P, NB_DI], F32, ctx=sctx),
        "scs": k.sbuf(pfx + "t_scs", [P, NB_DI], F32, ctx=sctx),
        "rcs": k.sbuf(pfx + "t_rcs", [P, NB_DI], F32, ctx=sctx),
    }
    ps_t2 = [k.psum(pfx + "ps_ta", [P, 512], ctx=sctx), k.psum(pfx + "ps_tb", [P, 512], ctx=sctx)]
    ps_g = k.psum(pfx + "ps_g", [P, 512], ctx=sctx)
    ps_u = k.psum(pfx + "ps_u", [P, 512], ctx=sctx)
    hq_d = io["hq_d"]
    after = tuple(io.get("after", ()))

    gp, te, sc, ve, sy = k.gp, k.te, k.sc, k.ve, k.sy
    for e in (gp, te, sc, ve, sy):
        e.w(*after)

    def load_T(tile_fn, dstT, rows, width):
        nblk = rows // P
        m_fin = None
        dst3_full = dstT[:].rearrange("p (c r) -> p c r", r=rows)
        for tix in range(nblk):
            sy.w(k.te.mark())
            m_ld = sy.dma(sy.r.dma_start(out=wtile[:, 0:width], in_=tile_fn(tix)))
            m_fin = e_transp(k, wtile[:, 0:width], dst3_full[:, :, tix * P:(tix + 1) * P],
                             width // P, ps_t2, ident, after=(m_ld,))
        return m_fin

    m_wg = load_T(io["wgq_t"], wgT, DI1, DM)
    m_wu = load_T(io["wuq_t"], wuT, DI1, DM)
    m_wd = load_T(io["wdq_t"], wdT, DM, DI1)

    NS = [512, 512, 512, 512, 512, 256]
    m_gev = m_uev = None
    for tix in range(TILES_PC):
        sy.w(ve.mark())
        sy.w(sc.mark())
        m_prev_ve = ve.mark()
        m_x = sy.dma(sy.r.dma_start(out=x_s[:], in_=io["x"](tix)))
        ve.w(te.mark())
        ve.w(sy.dmark())
        sc.w(sy.dmark())
        m_ln = e_ln(k, x_s[:], xln[:], GLUE_EPS, DM, sq=h_s[:, 0:DM], st=stat,
                    after=(m_x, m_prev_ve))
        tqx = {kk2: (v[:, 0:NB_DM] if kk2 in ("am", "scs", "rcs") else v[:, 0:DM])
               for kk2, v in tq.items()}
        tqx["xn"] = g_s[:, 0:DM]
        tqx["a"] = u_s[:, 0:DM]
        m_q = e_quant(k, xln[:], xq[:], NB_DM, tqx, after=(m_ln,))
        m_xT = e_transp(k, xq[:], xqT[:].rearrange("p (c r) -> p c r", r=P),
                        DC, ps_t2, ident, after=(m_q, m_wg, m_wu, m_wd))
        n0 = 0
        for ni, nsz in enumerate(NS):
            te.w(m_xT)
            if m_gev is not None:
                te.w(m_gev)
            for d in range(DC):
                mm_g = te.op(te.r.matmul(ps_g[:, 0:nsz], xqT[:, d * P:(d + 1) * P],
                                         wgT[:, d * DI1 + n0: d * DI1 + n0 + nsz],
                                         start=(d == 0), stop=(d == DC - 1)))
            sc.w(mm_g)
            sc.w(ve.mark() if ni == 0 else None)
            m_gev = sc.op(sc.r.activation(out=g_s[:, n0:n0 + nsz], in_=ps_g[:, 0:nsz], func=AF.Silu))
            if m_uev is not None:
                te.w(m_uev)
            for d in range(DC):
                mm_u = te.op(te.r.matmul(ps_u[:, 0:nsz], xqT[:, d * P:(d + 1) * P],
                                         wuT[:, d * DI1 + n0: d * DI1 + n0 + nsz],
                                         start=(d == 0), stop=(d == DC - 1)))
            sc.w(mm_u)
            m_uev = sc.op(sc.r.activation(out=u_s[:, n0:n0 + nsz], in_=ps_u[:, 0:nsz], func=AF.Copy))
            n0 += nsz
        ve.w(m_gev, m_uev)
        m_h = ve.op(ve.r.tensor_tensor(h_s[:], g_s[:], u_s[:], AL.mult))
        m_hq = e_quant(k, h_s[:], hq[:], NB_DI, tq, after=(m_h,))
        sy.w(m_hq)
        sy.dma(sy.r.dma_start(out=hq_d[tix * P:(tix + 1) * P, :], in_=hq[:]))

    m_y = None
    for tix in range(TILES_PC):
        te.w(sc.mark())
        sy.w(ve.mark())
        sy.w(te.mark())
        sy.w(sy.dmark())
        sy.dma(sy.r.dma_start(out=hq[:], in_=hq_d[tix * P:(tix + 1) * P, :]))
        sy.dma(sy.r.dma_start(out=x_s[:], in_=io["x"](tix)))
        m_x = m_hl = sy.dmark()
        m_hT = e_transp(k, hq[:], hqT[:].rearrange("p (c r) -> p c r", r=P),
                        IC, ps_t2, ident, after=(m_hl,))
        for n2 in range(2):
            te.w(m_hT)
            if m_y is not None:
                te.w(m_y)
            for i in range(IC):
                mm_d = te.op(te.r.matmul(ps_g[:, 0:512], hqT[:, i * P:(i + 1) * P],
                                         wdT[:, i * DM + n2 * 512: i * DM + (n2 + 1) * 512],
                                         start=(i == 0), stop=(i == IC - 1)))
            sc.w(mm_d)
            sc.w(ve.mark() if tix > 0 and n2 == 0 else None)
            m_y = sc.op(sc.r.activation(out=g_s[:, n2 * 512:(n2 + 1) * 512], in_=ps_g[:, 0:512],
                                        func=AF.Copy, scale=0.5))
        ve.w(m_y, m_x)
        ve.w(sy.dmark())
        m_o = ve.op(ve.r.tensor_tensor(u_s[:, 0:DM], g_s[:, 0:DM], x_s[:], AL.add))
        sy.w(m_o)
        sy.dma(sy.r.dma_start(out=io["xo"](tix), in_=u_s[:, 0:DM]))


def nc_ffn1():
    k = K("ffn1")
    nc = k.nc
    xs_d = nc.dram_tensor("xs", [TPC, DM], F32, kind="ExternalInput").ap()
    wgq_d = nc.dram_tensor("wgq", [DI1, DM], BF16, kind="ExternalInput").ap()
    wuq_d = nc.dram_tensor("wuq", [DI1, DM], BF16, kind="ExternalInput").ap()
    wdq_d = nc.dram_tensor("wdq", [DM, DI1], BF16, kind="ExternalInput").ap()
    ident_d = nc.dram_tensor("ident", [P, P], F32, kind="ExternalInput").ap()
    xo_d = nc.dram_tensor("xo", [TPC, DM], F32, kind="ExternalOutput").ap()
    hq_d = nc.dram_tensor("hq_scr", [TPC, DI1], BF16).ap()
    with k.ctx:
        k.start()
        k.ident = k.sbuf("ident_s", [P, P], BF16)
        with nc.Block() as block:
            @block.sync
            def _(_h):
                m_id = k.gp.dma(k.gp.r.dma_start(out=k.ident[:], in_=ident_d[:, :]))
                io = {
                    "x": lambda t: xs_d[t * P:(t + 1) * P, :],
                    "xo": lambda t: xo_d[t * P:(t + 1) * P, :],
                    "wgq_t": lambda t: wgq_d[t * P:(t + 1) * P, :],
                    "wuq_t": lambda t: wuq_d[t * P:(t + 1) * P, :],
                    "wdq_t": lambda t: wdq_d[t * P:(t + 1) * P, :],
                    "hq_d": hq_d,
                    "after": (m_id,),
                }
                emit_ffn1(k, "f_", k.ctx, io)
                k.sy.w(k.sy.dmark())
    return k.nc


_cache = {}


def _get(name, builder):
    if name not in _cache:
        _cache[name] = builder()
    return _cache[name]


def emit_attn1(k, pfx, sctx, io):
    nc = k.nc
    NKT = SEQ // P
    s160 = 1.0 / math.sqrt(DK1 + DR1)
    gp, te, sc, ve, sy = k.gp, k.te, k.sc, k.ve, k.sy
    for _e in (gp, te, sc, ve, sy):
        _e.w(*io.get('after', ()))
    ident = k.ident
    ones1f = k.ones1f
    ones1b = k.ones1b
    ones_col = k.ones_col
    kvlat = k.sbuf(pfx + "kvlat", [P, SEQ], BF16, ctx=sctx)
    cqT = k.sbuf(pfx + "cqT", [P, TPC], BF16, ctx=sctx)
    kcT_h = k.sbuf(pfx + "kcT_h", [P, SEQ], BF16, ctx=sctx)
    v_h = k.sbuf(pfx + "v_h", [P, SEQ], BF16, ctx=sctx)
    qcT = k.sbuf(pfx + "qcT", [P, DC * TPC], BF16, ctx=sctx)
    qrT = k.sbuf(pfx + "qrT", [P, 2 * TPC], BF16, ctx=sctx)
    o_all = k.sbuf(pfx + "o_all", [P, DC * TPC], BF16, ctx=sctx)
    at = [k.sbuf(pfx + "at%d" % i, [P, 512], BF16, ctx=sctx) for i in range(4)]
    x_s = k.sbuf(pfx + "x_s", [P, DM], F32, ctx=sctx)
    sq_s = k.sbuf(pfx + "sq_s", [P, 2 * DM], F32, ctx=sctx)
    xln = k.sbuf(pfx + "xln", [P, DM], BF16, ctx=sctx)
    xlnT = k.sbuf(pfx + "xlnT", [P, DM], BF16, ctx=sctx)
    stat = k.sbuf(pfx + "stat", [P, 8], F32, ctx=sctx)
    wdq_s = k.sbuf(pfx + "wdq_s", [P, DC * DC1], BF16, ctx=sctx)
    wdkv_s = k.sbuf(pfx + "wdkv_s", [P, DC * DC1], BF16, ctx=sctx)
    wkr_s = k.sbuf(pfx + "wkr_s", [P, DC * DR1], BF16, ctx=sctx)
    wuq_s = k.sbuf(pfx + "wuq_s", [P, DM], BF16, ctx=sctx)
    wuk_s = k.sbuf(pfx + "wuk_s", [P, DM], BF16, ctx=sctx)
    wuv_s = k.sbuf(pfx + "wuv_s", [P, DM], BF16, ctx=sctx)
    wqr_s = k.sbuf(pfx + "wqr_s", [P, 256], BF16, ctx=sctx)
    wo_s = k.sbuf(pfx + "wo_s", [P, DC * DM], BF16, ctx=sctx)
    bo_row = k.sbuf(pfx + "bo_row", [1, DM], BF16, ctx=sctx)
    bdq_s = k.sbuf(pfx + "bdq_s", [P, 2], F32, ctx=sctx)
    bkr_s = k.sbuf(pfx + "bkr_s", [P, 1], F32, ctx=sctx)
    kr2 = k.sbuf(pfx + "kr2", [1, SEQ], BF16, ctx=sctx)
    qn2 = k.sbuf(pfx + "qn2", [1, TPC], BF16, ctx=sctx)
    mrow = k.sbuf(pfx + "mrow", [1, TPC], BF16, ctx=sctx)
    mred = k.sbuf(pfx + "mred", [1, 8], BF16, ctx=sctx)
    kmx = k.sbuf(pfx + "kmx", [1, 18], F32, ctx=sctx)
    kmax2 = k.sbuf(pfx + "kmax2", [1, 1], F32, ctx=sctx)
    negm_t = k.sbuf(pfx + "negm_t", [P, 32], F32, ctx=sctx)
    sqk = k.sbuf(pfx + "sqk", [P, 1024], BF16, ctx=sctx)
    qr_h = k.sbuf(pfx + "qr_h", [P, TPC], BF16, ctx=sctx)
    lrow = sq_s[0:1, 1024:1536]
    rl_s = sq_s[0:1, 1536:2048]
    rlb_s = sq_s[:, 512:1024]
    ps_t2 = [k.psum(pfx + "ps_ta", [P, 512], ctx=sctx), k.psum(pfx + "ps_tb", [P, 512], ctx=sctx)]
    ps_s2 = [k.psum(pfx + "ps_sa", [P, 512], ctx=sctx), k.psum(pfx + "ps_sb", [P, 512], ctx=sctx)]
    ps_o = k.psum(pfx + "ps_o", [P, 512], ctx=sctx)
    ps_l = k.psum(pfx + "ps_l", [P, 512], ctx=sctx)
    ps_s4 = ps_s2 + [k.psum(pfx + "ps_sc", [P, 512], ctx=sctx),
                     k.psum(pfx + "ps_sd", [P, 512], ctx=sctx)]

    gp.dma(gp.r.dma_start(out=wuq_s[0:DC1, :], in_=io["w_uq"][:, :]))
    gp.dma(gp.r.dma_start(out=wuq_s[DC1:DC1 + 1, :], in_=io["b_uq"][:].unsqueeze(0)))
    gp.dma(gp.r.dma_start(out=wuk_s[0:DC1, :], in_=io["w_uk"][:, :]))
    gp.dma(gp.r.dma_start(out=wuk_s[DC1:DC1 + 1, :], in_=io["b_uk"][:].unsqueeze(0)))
    gp.dma(gp.r.dma_start(out=wuv_s[0:DC1, :], in_=io["w_uv"][:, :]))
    gp.dma(gp.r.dma_start(out=wuv_s[DC1:DC1 + 1, :], in_=io["b_uv"][:].unsqueeze(0)))
    gp.dma(gp.r.dma_start(out=wqr_s[0:DC1, :], in_=io["w_qr"][:, :]))
    gp.dma(gp.r.dma_start(out=wqr_s[DC1:DC1 + 1, :], in_=io["b_qr"][:].unsqueeze(0)))
    gp.dma(gp.r.dma_start(out=bo_row[:], in_=io["b_o"][:].unsqueeze(0)))
    for d in range(DC):
        gp.dma(gp.r.dma_start(out=wdq_s[:, d * DC1:(d + 1) * DC1], in_=io["w_dq"][d * P:(d + 1) * P, :]))
        gp.dma(gp.r.dma_start(out=wdkv_s[:, d * DC1:(d + 1) * DC1], in_=io["w_dkv"][d * P:(d + 1) * P, :]))
        gp.dma(gp.r.dma_start(out=wkr_s[:, d * DR1:(d + 1) * DR1], in_=io["w_kr"][d * P:(d + 1) * P, :]))
        gp.dma(gp.r.dma_start(out=wo_s[:, d * DM:(d + 1) * DM], in_=io["w_o"][d * P:(d + 1) * P, :]))
    gp.dma(gp.r.dma_start(out=bdq_s[0:DC1, 0:1], in_=io["b_dq"][:].unsqueeze(1)))
    gp.dma(gp.r.dma_start(out=bdq_s[0:DC1, 1:2], in_=io["b_dkv"][:].unsqueeze(1)))
    gp.dma(gp.r.dma_start(out=bkr_s[0:DR1, 0:1], in_=io["b_kr"][:].unsqueeze(1)))
    m_wload = gp.dmark()
    ve.op(ve.r.memset(kvlat[DC1:DC1 + 1, :], 1.0))
    m_ones = ve.op(ve.r.memset(cqT[DC1:DC1 + 1, :], 1.0))

    xlnT3 = xlnT[:].rearrange("p (c r) -> p c r", r=P)

    def lat_tile(src_ap, out_desc):
        sy.w(ve.mark())
        sy.w(sc.mark())
        m_x = sy.dma(sy.r.dma_start(out=x_s[:], in_=src_ap))
        ve.w(te.mark())
        m_ln = e_ln(k, x_s[:], xln[:], GLUE_EPS, DM, sq=sq_s[:, 0:DM], st=stat,
                    after=(m_x,))
        m_tr = e_transp(k, xln[:], xlnT3, DC, ps_t2, ident, after=(m_ln, m_wload))
        for (w_sb, wstep, bias_ap, dst_ap, bank) in out_desc:
            te.w(m_tr)
            te.w(sc.mark())
            for d in range(DC):
                mm = te.op(te.r.matmul(bank[0:DC1, 0:P],
                                       w_sb[:, d * wstep:(d + 1) * wstep],
                                       xlnT3[:, d, :],
                                       start=(d == 0), stop=(d == DC - 1)))
            sc.w(mm)
            sc.op(sc.r.activation(out=dst_ap, in_=bank[0:DC1, 0:P],
                                  func=AF.Identity, bias=bias_ap))

    for t in range(NKT):
        lat_tile(io["xb"](t),
                 [(wdkv_s, DC1, bdq_s[0:DC1, 1:2], kvlat[0:DC1, t * P:(t + 1) * P], ps_s2[0]),
                  (wkr_s, DR1, bkr_s[0:DR1, 0:1], kvlat[64:64 + DR1, t * P:(t + 1) * P], ps_s2[1])])
    for t in range(TILES_PC):
        lat_tile(io["xown"](t),
                 [(wdq_s, DC1, bdq_s[0:DC1, 0:1], cqT[0:DC1, t * P:(t + 1) * P], ps_s2[0])])

    te.w(ve.mark())
    m_prev_ev = sc.mark()
    for m in range(DC):
        for n in range(4):
            te.w(m_prev_ev)
            mm = te.op(te.r.matmul(ps_s2[n % 2][:, 0:512],
                                   wuq_s[0:DC1 + 1, m * P:(m + 1) * P],
                                   cqT[0:DC1 + 1, n * 512:(n + 1) * 512],
                                   start=True, stop=True))
            sc.w(mm)
            m_prev_ev = sc.op(sc.r.activation(
                out=qcT[:, m * TPC + n * 512: m * TPC + (n + 1) * 512],
                in_=ps_s2[n % 2][:, 0:512], func=AF.Copy, scale=s160))
    for m in range(2):
        for n in range(4):
            te.w(m_prev_ev)
            mm = te.op(te.r.matmul(ps_s2[n % 2][:, 0:512],
                                   wqr_s[0:DC1 + 1, m * P:(m + 1) * P],
                                   cqT[0:DC1 + 1, n * 512:(n + 1) * 512],
                                   start=True, stop=True))
            sc.w(mm)
            m_prev_ev = sc.op(sc.r.activation(
                out=qrT[:, m * TPC + n * 512: m * TPC + (n + 1) * 512],
                in_=ps_s2[n % 2][:, 0:512], func=AF.Copy, scale=s160))

    m_prev_lv = sc.mark()
    for n in range(16):
        sc.w(te.mark())
        m_sqk = sc.op(sc.r.activation(out=sqk[0:DR1, 0:512],
                                      in_=kvlat[64:64 + DR1, n * 512:(n + 1) * 512],
                                      func=AF.Square))
        te.w(m_sqk, m_prev_lv)
        mm = te.op(te.r.matmul(ps_l[0:1, 0:512], ones_col[0:DR1, :],
                               sqk[0:DR1, 0:512], start=True, stop=True))
        sc.w(mm)
        m_prev_lv = sc.op(sc.r.activation(out=kr2[0:1, n * 512:(n + 1) * 512],
                                          in_=ps_l[0:1, 0:512], func=AF.Copy))

    for h in range(H1):
        qr_pb = (h % 4) * DR1
        qr_ch = (h // 4) * TPC
        gp.r.wait_ge(k.te.sem, k.te.cnt)
        m_qrh = gp.dma(gp.r.dma_start(out=qr_h[64:64 + DR1, :],
                                      in_=qrT[qr_pb:qr_pb + DR1, qr_ch:qr_ch + TPC]))
        m_kc = None
        for n in range(16):
            te.w(sc.mark())
            mm = te.op(te.r.matmul(ps_s2[n % 2][:, 0:512],
                                   wuk_s[0:DC1 + 1, h * P:(h + 1) * P],
                                   kvlat[0:DC1 + 1, n * 512:(n + 1) * 512],
                                   start=True, stop=True))
            sc.w(mm)
            m_kc = sc.op(sc.r.activation(out=kcT_h[:, n * 512:(n + 1) * 512],
                                         in_=ps_s2[n % 2][:, 0:512], func=AF.Copy))
        m_vev = [sc.mark(), sc.mark()]
        for g0 in range(0, NKT, 4):
            bk = (g0 // 4) % 2
            te.w(m_vev[bk])
            for j in range(4):
                t = g0 + j
                mm = te.op(te.r.matmul(ps_t2[bk][:, j * P:(j + 1) * P],
                                       kvlat[0:DC1 + 1, t * P:(t + 1) * P],
                                       wuv_s[0:DC1 + 1, h * P:(h + 1) * P],
                                       start=True, stop=True))
            sc.w(mm)
            m_vev[bk] = sc.op(sc.r.activation(
                out=v_h[:].rearrange("p (c r) -> p c r", r=P)[:, g0:g0 + 4, :],
                in_=ps_t2[bk][:, 0:512].rearrange("p (g r) -> p g r", r=P), func=AF.Copy))
        m_vh = m_vev

        m_pl = sc.mark()
        m_mx = ve.mark()
        for n in range(16):
            sc.w(m_kc)
            sc.w(te.mark())
            m_sq2 = sc.op(sc.r.activation(out=sqk[:, 0:512],
                                          in_=kcT_h[:, n * 512:(n + 1) * 512], func=AF.Square))
            te.w(m_sq2, m_pl, m_mx)
            mm = te.op(te.r.matmul(ps_l[0:1, 0:512], ones_col[:, :],
                                   sqk[:, 0:512], start=True, stop=True))
            sc.w(mm)
            m_pl = sc.op(sc.r.activation(out=lrow, in_=ps_l[0:1, 0:512], func=AF.Copy))
            ve.w(m_pl)
            m_ad = ve.op(ve.r.tensor_tensor(lrow, lrow,
                                            kr2[0:1, n * 512:(n + 1) * 512], AL.add))
            ve.w(m_ad)
            m_mx = ve.op(ve.r.reduce_max(kmx[0:1, n:n + 1], lrow, axis=AX.X))
        ve.w(m_mx)
        m_km = ve.op(ve.r.reduce_max(kmax2[0:1, 0:1], kmx[0:1, 0:16], axis=AX.X))
        m_pq = sc.mark()
        for n in range(4):
            sc.w(te.mark())
            m_sq3 = sc.op(sc.r.activation(out=sqk[:, 0:512],
                                          in_=qcT[:, h * TPC + n * 512: h * TPC + (n + 1) * 512],
                                          func=AF.Square))
            sc.w(m_qrh)
            m_sq4 = sc.op(sc.r.activation(out=sqk[0:DR1, 512:1024],
                                          in_=qr_h[64:64 + DR1, n * 512:(n + 1) * 512],
                                          func=AF.Square))
            te.w(m_sq3, m_sq4, m_pq)
            te.op(te.r.matmul(ps_l[0:1, 0:512], ones_col[:, :],
                              sqk[:, 0:512], start=True, stop=False))
            mm = te.op(te.r.matmul(ps_l[0:1, 0:512], ones_col[0:DR1, :],
                                   sqk[0:DR1, 512:1024],
                                   start=False, stop=True))
            sc.w(mm)
            m_pq = sc.op(sc.r.activation(out=qn2[0:1, n * 512:(n + 1) * 512],
                                         in_=ps_l[0:1, 0:512], func=AF.Copy))
        ve.w(m_pq, m_km)
        m_pr = ve.op(ve.r.tensor_scalar_mul(mrow[0:1, :], qn2[0:1, :], kmax2[0:1, 0:1]))
        sc.w(m_pr)
        m_sqr = sc.op(sc.r.activation(out=mrow[0:1, :], in_=mrow[0:1, :], func=AF.Sqrt,
                                      scale=1.0404))
        ve.w(m_sqr)
        m_red = ve.op(ve.r.reduce_max(mred[0:1, 0:4], mrow[0:1, :].rearrange("o (g q) -> o g q", q=512),
                                      axis=AX.X))
        ve.w(m_red)
        m_neg = ve.op(ve.r.tensor_scalar_mul(mred[0:1, 4:8], mred[0:1, 0:4], -1.0))
        te.w(m_neg)
        te.w(sc.mark())
        mmb = te.op(te.r.matmul(ps_l[0:P, 0:4], ones1b[:, :], mred[0:1, 4:8],
                                start=True, stop=True))
        sc.w(mmb)
        m_negt = sc.op(sc.r.activation(out=negm_t[:, h * 4:(h + 1) * 4], in_=ps_l[0:P, 0:4],
                                       func=AF.Copy))

        for qg in range(4):
            te.w(sc.mark())
            m_at = [te.mark()] * 4
            m_exp = [sc.mark()] * 4
            def av_pair(kb):
                slx = kb % 4
                te.w(m_exp[slx])
                te.op(te.r.matmul(ps_o[:, 0:512], v_h[:, kb * P:(kb + 1) * P], at[slx][:],
                                  start=(kb == 0), stop=(kb == NKT - 1)))
                m_at[slx] = te.op(te.r.matmul(ps_l[0:1, 0:512], ones_col[:, :], at[slx][:],
                                              start=(kb == 0), stop=(kb == NKT - 1)))
            for kb in range(NKT):
                sl = kb % 4
                te.w(m_exp[sl])
                mm1 = te.op(te.r.matmul(ps_s4[sl][:, 0:512],
                                        kcT_h[:, kb * P:(kb + 1) * P],
                                        qcT[:, h * TPC + qg * 512: h * TPC + (qg + 1) * 512],
                                        start=True, stop=False))
                te.w(m_qrh)
                mm2 = te.op(te.r.matmul(ps_s4[sl][:, 0:512],
                                        kvlat[64:64 + DR1, kb * P:(kb + 1) * P],
                                        qr_h[64:64 + DR1, qg * 512:(qg + 1) * 512],
                                        start=False, stop=True))
                sc.w(mm2)
                sc.w(m_at[sl])
                sc.w(m_negt)
                m_exp[sl] = sc.op(sc.r.activation(out=at[sl][:], in_=ps_s4[sl][:, 0:512],
                                                  func=AF.Exp,
                                                  bias=negm_t[:, h * 4 + qg: h * 4 + qg + 1]))
                if kb >= 2:
                    av_pair(kb - 2)
            av_pair(NKT - 2)
            av_pair(NKT - 1)
            m_avl = m_at[(NKT - 1) % 4]
            sc.w(m_avl)
            m_oev = sc.op(sc.r.activation(out=sq_s[:, 0:512], in_=ps_o[:, 0:512], func=AF.Copy))
            m_lev = sc.op(sc.r.activation(out=lrow, in_=ps_l[0:1, 0:512], func=AF.Copy))
            ve.w(m_lev)
            m_rl = ve.op(ve.r.reciprocal(rl_s, lrow))
            te.w(m_rl)
            te.w(m_oev, m_lev)
            mmb2 = te.op(te.r.matmul(ps_l[0:P, 0:512], ones1f[:, :], rl_s,
                                     start=True, stop=True))
            sc.w(mmb2)
            m_rlb = sc.op(sc.r.activation(out=rlb_s, in_=ps_l[0:P, 0:512], func=AF.Copy))
            ve.w(m_rlb, m_oev)
            m_on = ve.op(ve.r.tensor_tensor(
                o_all[:, h * TPC + qg * 512: h * TPC + (qg + 1) * 512],
                sq_s[:, 0:512], rlb_s, AL.mult))

    m_prev = sc.mark()
    for tix in range(TILES_PC):
        sy.w(ve.mark())
        m_x = sy.dma(sy.r.dma_start(out=x_s[:], in_=io["xown"](tix)))
        for n2 in range(2):
            te.w(ve.mark())
            te.w(m_prev)
            te.op(te.r.matmul(ps_s2[n2][:, 0:512], ones1b[:, :],
                              bo_row[0:1, n2 * 512:(n2 + 1) * 512], start=True, stop=False))
            for hh in range(DC):
                mm = te.op(te.r.matmul(ps_s2[n2][:, 0:512],
                                       o_all[:, hh * TPC + tix * P: hh * TPC + (tix + 1) * P],
                                       wo_s[:, hh * DM + n2 * 512: hh * DM + (n2 + 1) * 512],
                                       start=False, stop=(hh == DC - 1)))
            sc.w(mm)
            sc.w(ve.mark())
            m_prev = sc.op(sc.r.activation(out=sq_s[:, DM + n2 * 512: DM + (n2 + 1) * 512],
                                           in_=ps_s2[n2][:, 0:512], func=AF.Copy))
        ve.w(m_prev, m_x)
        ve.w(sy.dmark())
        m_res = ve.op(ve.r.tensor_tensor(sq_s[:, 0:DM], sq_s[:, DM:2 * DM], x_s[:], AL.add))
        sy.w(m_res)
        sy.dma(sy.r.dma_start(out=io["xo"](tix), in_=sq_s[:, 0:DM]))



def nc_attn1():
    k = K("attn1")
    nc = k.nc
    xo_d = nc.dram_tensor("xo", [TPC, DM], F32, kind="ExternalOutput").ap()
    xown_d = nc.dram_tensor("xown", [TPC, DM], F32, kind="ExternalInput").ap()
    xb_d = nc.dram_tensor("xb", [SEQ, DM], F32, kind="ExternalInput").ap()
    ident_d = nc.dram_tensor("ident", [P, P], F32, kind="ExternalInput").ap()
    wnames = [("w_dq", [DM, DC1]), ("b_dq", [DC1]), ("w_uq", [DC1, DM]), ("b_uq", [DM]),
              ("w_dkv", [DM, DC1]), ("b_dkv", [DC1]), ("w_uk", [DC1, DM]), ("b_uk", [DM]),
              ("w_uv", [DC1, DM]), ("b_uv", [DM]), ("w_qr", [DC1, 256]), ("b_qr", [256]),
              ("w_kr", [DM, DR1]), ("b_kr", [DR1]), ("w_o", [DM, DM]), ("b_o", [DM])]
    io = {nm: nc.dram_tensor(nm, sh, F32, kind="ExternalInput").ap() for nm, sh in wnames}
    io["xb"] = lambda t: xb_d[t * P:(t + 1) * P, :]
    io["xown"] = lambda t: xown_d[t * P:(t + 1) * P, :]
    io["xo"] = lambda t: xo_d[t * P:(t + 1) * P, :]
    with k.ctx:
        k.start()
        k.ident = k.sbuf("g_ident", [P, P], BF16)
        k.ones1f = k.sbuf("g_ones1f", [1, P], F32)
        k.ones1b = k.sbuf("g_ones1b", [1, P], BF16)
        k.ones_col = k.sbuf("g_ones_col", [P, 1], BF16)
        with nc.Block() as block:
            @block.sync
            def _(_h):
                m_id = k.gp.dma(k.gp.r.dma_start(out=k.ident[:], in_=ident_d[:, :]))
                k.ve.op(k.ve.r.memset(k.ones1f[:], 1.0))
                k.ve.op(k.ve.r.memset(k.ones1b[:], 1.0))
                m_on = k.ve.op(k.ve.r.memset(k.ones_col[:], 1.0))
                io["after"] = (m_id, m_on)
                emit_attn1(k, "a1_", k.ctx, io)
                k.sy.w(k.sy.dmark())
    return k.nc



def emit_attn0(k, pfx, sctx, io):
    nc = k.nc
    NKT = SEQ // P
    s0 = 1.0 / math.sqrt(E0)
    gp, te, sc, ve, sy = k.gp, k.te, k.sc, k.ve, k.sy
    for _e in (gp, te, sc, ve, sy):
        _e.w(*io.get('after', ()))
    ident = k.ident
    ones1f = k.ones1f
    ones1b = k.ones1b
    ones_col = k.ones_col
    pool = k.sbuf(pfx + "pool", [P, 3 * DC * DM], BF16, ctx=sctx)
    wq_s = pool[:, 0:DC * DM]
    wk_s = pool[:, DC * DM:2 * DC * DM]
    wv_s = pool[:, 2 * DC * DM:3 * DC * DM]
    VW = 65 * NKT
    ves = pool[:, SEQ:SEQ + VW]
    q_h = pool[:, SEQ + VW: SEQ + VW + TPC]
    m16_s = pool[:, SEQ + VW + TPC: SEQ + VW + TPC + 16 * 512]
    wo_s = k.sbuf(pfx + "wo_s", [P, DC * DM], BF16, ctx=sctx)
    grow = k.sbuf(pfx + "grow", [1, 2 * DM], F32, ctx=sctx)
    gbt = k.sbuf(pfx + "gbt", [P, 2 * DM], F32, ctx=sctx)
    qT_s = k.sbuf(pfx + "qT_s", [P, DC * TPC], BF16, ctx=sctx)
    o_all = k.sbuf(pfx + "o_all", [P, DC * TPC], BF16, ctx=sctx)
    at = [k.sbuf(pfx + "at%d" % i, [P, 512], BF16, ctx=sctx) for i in range(4)]
    at_raw = [k.sbuf(pfx + "at_raw%d" % i, [P, 512], BF16, ctx=sctx) for i in range(2)]
    x_s = k.sbuf(pfx + "x_s", [P, DM], F32, ctx=sctx)
    xin_s = k.sbuf(pfx + "xin_s", [P, DM], F32, ctx=sctx)
    sq_s = k.sbuf(pfx + "sq_s", [P, 2 * DM], F32, ctx=sctx)
    xln = k.sbuf(pfx + "xln", [P, DM], BF16, ctx=sctx)
    xln2 = k.sbuf(pfx + "xln2", [P, DM], F32, ctx=sctx)
    xlnT = k.sbuf(pfx + "xlnT", [P, DM], BF16, ctx=sctx)
    ktile = k.sbuf(pfx + "ktile", [P, DM], BF16, ctx=sctx)
    vtile = k.sbuf(pfx + "vtile", [P, DM], BF16, ctx=sctx)
    stat = k.sbuf(pfx + "stat", [P, 8], F32, ctx=sctx)
    qn2 = k.sbuf(pfx + "qn2", [1, TPC], BF16, ctx=sctx)
    mrow = k.sbuf(pfx + "mrow", [1, TPC], BF16, ctx=sctx)
    mred = k.sbuf(pfx + "mred", [1, 8], BF16, ctx=sctx)
    kmx = k.sbuf(pfx + "kmx", [1, NKT + 2], F32, ctx=sctx)
    kmax2 = k.sbuf(pfx + "kmax2", [1, 1], F32, ctx=sctx)
    negm_t = k.sbuf(pfx + "negm_t", [P, 64], F32, ctx=sctx)
    sqk = k.sbuf(pfx + "sqk", [P, 1024], BF16, ctx=sctx)
    lrow = sq_s[0:1, 1024:1536]
    rl_s = sq_s[0:1, 1536:2048]
    rlb_s = sq_s[:, 512:1024]
    ps_t2 = [k.psum(pfx + "ps_ta", [P, 512], ctx=sctx), k.psum(pfx + "ps_tb", [P, 512], ctx=sctx)]
    ps_s2 = [k.psum(pfx + "ps_sa", [P, 512], ctx=sctx), k.psum(pfx + "ps_sb", [P, 512], ctx=sctx)]
    ps_o = k.psum(pfx + "ps_o", [P, 512], ctx=sctx)
    ps_l = k.psum(pfx + "ps_l", [P, 512], ctx=sctx)
    ps_s4 = ps_s2 + [k.psum(pfx + "ps_sc", [P, 512], ctx=sctx),
                     k.psum(pfx + "ps_sd", [P, 512], ctx=sctx)]

    for (wd, ws) in ((io["wq"], wq_s), (io["wk"], wk_s), (io["wv"], wv_s)):
        wsv = ws.rearrange("p (d h e) -> p d h e", h=H0, e=E0)
        wdv = wd.rearrange("h (d p) e -> p d h e", p=P)
        for d in range(DC):
            gp.dma(gp.r.dma_start(out=wsv[:, d, :, :], in_=wdv[:, d, :, :]))
    for d in range(DC):
        gp.dma(gp.r.dma_start(out=wo_s[:, d * DM:(d + 1) * DM], in_=io["wo"][d * P:(d + 1) * P, :]))
    gp.dma(gp.r.dma_start(out=grow[0:1, 0:DM], in_=io["gam"][:].unsqueeze(0)))
    gp.dma(gp.r.dma_start(out=grow[0:1, DM:2 * DM], in_=io["bet"][:].unsqueeze(0)))
    m_wload = gp.dmark()
    m_ones = k.ve.mark()

    xlnT3 = xlnT[:].rearrange("p (c r) -> p c r", r=P)

    lat_ev = [sc.mark()] * 4

    def qkv_tile(src_ap, tix, q_j):
        sy.w(ve.mark())
        sy.w(sc.mark())
        m_x = sy.dma(sy.r.dma_start(out=x_s[:], in_=src_ap))
        m_prevst = sy.dmark()
        ve.w(te.mark())
        m_ln = e_ln(k, x_s[:], xln[:], GLUE_EPS, DM, sq=sq_s[:, 0:DM], st=stat,
                    after=(m_x,))
        m_tr = e_transp(k, xln[:], xlnT3, DC, ps_t2, ident, after=(m_ln, m_wload))
        if q_j is None:
            for mh in range(DC):
                sl = mh % 4
                te.w(m_tr, lat_ev[sl])
                for d in range(DC):
                    mm = te.op(te.r.matmul(ps_s4[sl][:, 0:P],
                                           wk_s[:, d * DM + mh * P: d * DM + mh * P + P],
                                           xlnT3[:, d, :], start=(d == 0), stop=(d == DC - 1)))
                sc.w(mm)
                if mh < 4:
                    sc.w(m_prevst)
                lat_ev[sl] = sc.op(sc.r.activation(out=ktile[:, mh * P:(mh + 1) * P],
                                                   in_=ps_s4[sl][:, 0:P], func=AF.Copy))
            sy.w(lat_ev[(DC - 1) % 4], lat_ev[(DC - 2) % 4], lat_ev[(DC - 3) % 4], lat_ev[(DC - 4) % 4])
            sy.dma(sy.r.dma_start(
                out=io["kT_d"].rearrange("(c p) s -> p c s", p=P)[:, :, tix * P:(tix + 1) * P],
                in_=ktile[:].rearrange("p (c r) -> p c r", r=P)))
            for n2 in range(2):
                te.w(m_tr, lat_ev[n2])
                for d in range(DC):
                    mm = te.op(te.r.matmul(ps_s4[n2][:, 0:512],
                                           xlnT3[:, d, :],
                                           wv_s[:, d * DM + n2 * 512: d * DM + (n2 + 1) * 512],
                                           start=(d == 0), stop=(d == DC - 1)))
                sc.w(mm)
                sc.w(m_prevst)
                lat_ev[n2] = sc.op(sc.r.activation(out=vtile[:, n2 * 512:(n2 + 1) * 512],
                                                   in_=ps_s4[n2][:, 0:512], func=AF.Copy))
            sy.w(lat_ev[0], lat_ev[1])
            sy.dma(sy.r.dma_start(out=io["v_d"][tix * P:(tix + 1) * P, :], in_=vtile[:]))
        else:
            for mh in range(DC):
                sl = mh % 4
                te.w(m_tr, lat_ev[sl])
                for d in range(DC):
                    mm = te.op(te.r.matmul(ps_s4[sl][:, 0:P],
                                           wq_s[:, d * DM + mh * P: d * DM + mh * P + P],
                                           xlnT3[:, d, :], start=(d == 0), stop=(d == DC - 1)))
                sc.w(mm)
                lat_ev[sl] = sc.op(sc.r.activation(
                    out=qT_s[:, mh * TPC + q_j * P: mh * TPC + (q_j + 1) * P],
                    in_=ps_s4[sl][:, 0:P], func=AF.Copy, scale=s0))

    for t in range(NKT):
        qkv_tile(io["xb"](t), t, None)
    for j in range(TILES_PC):
        qkv_tile(io["xown"](j), j, j)

    gp.r.wait_ge(k.te.sem, k.te.cnt)
    m16v = m16_s.rearrange("p (i q) -> p i q", q=512)
    for i in range(16):
        gp.dma(gp.r.dma_start(out=m16v[:, i, :], in_=io["m16"][i, :, :]))
    m_m16 = gp.dmark()

    for h in range(H0):
        hp, hsub = h // 2, h % 2
        gp.r.wait_ge(k.te.sem, k.te.cnt)
        gp.dma(gp.r.dma_start(out=pool[0:E0, 0:SEQ], in_=io["kT_d"][h * E0:(h + 1) * E0, :]))
        m_qh = gp.dma(gp.r.dma_start(
            out=q_h[0:E0, :],
            in_=qT_s[hsub * E0:(hsub + 1) * E0, hp * TPC:(hp + 1) * TPC]))
        ve.w(te.mark())
        m_vset = ve.op(ve.r.memset(ves, 1.0))
        sy.w(m_vset)
        sy.w(te.mark())
        m_ves = sy.dma(sy.r.dma_start(
            out=ves.rearrange("p (b e) -> p b e", e=65)[:, :, 0:E0],
            in_=io["v_d"].rearrange("(b p) d -> p b d", p=P)[:, :, h * E0:(h + 1) * E0]))
        m_ldh = gp.dmark()
        m_pl = sc.mark()
        m_mx = ve.mark()
        for n in range(16):
            sc.w(m_ldh)
            sc.w(te.mark())
            m_sq2 = sc.op(sc.r.activation(out=sqk[0:E0, 0:512],
                                          in_=pool[0:E0, n * 512:(n + 1) * 512],
                                          func=AF.Square))
            te.w(m_sq2, m_pl, m_mx, m_ones)
            mm = te.op(te.r.matmul(ps_l[0:1, 0:512], ones_col[0:E0, :],
                                   sqk[0:E0, 0:512], start=True, stop=True))
            sc.w(mm)
            m_pl = sc.op(sc.r.activation(out=lrow, in_=ps_l[0:1, 0:512], func=AF.Copy))
            ve.w(m_pl)
            m_mx = ve.op(ve.r.reduce_max(kmx[0:1, n:n + 1], lrow, axis=AX.X))
        ve.w(m_mx)
        m_km = ve.op(ve.r.reduce_max(kmax2[0:1, 0:1], kmx[0:1, 0:16], axis=AX.X))
        m_pq = sc.mark()
        for n in range(4):
            sc.w(te.mark())
            sc.w(m_qh, m_ldh)
            m_sq3 = sc.op(sc.r.activation(out=sqk[0:E0, 0:512],
                                          in_=q_h[0:E0, n * 512:(n + 1) * 512], func=AF.Square))
            te.w(m_sq3, m_pq)
            mm = te.op(te.r.matmul(ps_l[0:1, 0:512], ones_col[0:E0, :],
                                   sqk[0:E0, 0:512], start=True, stop=True))
            sc.w(mm)
            m_pq = sc.op(sc.r.activation(out=qn2[0:1, n * 512:(n + 1) * 512],
                                         in_=ps_l[0:1, 0:512], func=AF.Copy))
        ve.w(m_pq, m_km)
        m_pr = ve.op(ve.r.tensor_scalar_mul(mrow[0:1, :], qn2[0:1, :], kmax2[0:1, 0:1]))
        sc.w(m_pr)
        m_sqr = sc.op(sc.r.activation(out=mrow[0:1, :], in_=mrow[0:1, :], func=AF.Sqrt,
                                      scale=1.0404))
        ve.w(m_sqr)
        m_red = ve.op(ve.r.reduce_max(mred[0:1, 0:4],
                                      mrow[0:1, :].rearrange("o (g q) -> o g q", q=512), axis=AX.X))
        ve.w(m_red)
        m_neg = ve.op(ve.r.tensor_scalar_mul(mred[0:1, 4:8], mred[0:1, 0:4], -1.0))
        te.w(m_neg)
        te.w(sc.mark())
        mmb = te.op(te.r.matmul(ps_l[0:P, 0:4], ones1b[:, :], mred[0:1, 4:8],
                                start=True, stop=True))
        sc.w(mmb)
        m_negt = sc.op(sc.r.activation(out=negm_t[:, h * 4:(h + 1) * 4], in_=ps_l[0:P, 0:4],
                                       func=AF.Copy))

        for qg in range(4):
            te.w(sc.mark())
            nkb = 16 * qg + 16
            m_at = [te.mark()] * 4
            m_exp = [sc.mark()] * 4
            m_msk = [ve.mark()] * 4
            m_rawc = [ve.mark(), ve.mark()]
            m_rdy = [None] * 4
            def av0(kb):
                slx = kb % 4
                te.w(m_rdy[slx])
                if kb == 0:
                    te.w(m_ves)
                te.op(te.r.matmul(ps_o[0:65, 0:512], ves[:, kb * 65:(kb + 1) * 65],
                                  at[slx][:], start=(kb == 0), stop=(kb == nkb - 1)))
                m_at[slx] = te.mark()
            for kb in range(nkb):
                sl = kb % 4
                te.w(m_exp[sl])
                mm1 = te.op(te.r.matmul(ps_s4[sl][:, 0:512],
                                        pool[0:E0, kb * P:(kb + 1) * P],
                                        q_h[0:E0, qg * 512:(qg + 1) * 512],
                                        start=True, stop=True))
                sc.w(mm1)
                sc.w(m_at[sl], m_msk[sl])
                sc.w(m_negt)
                masked = kb >= 16 * qg
                if masked:
                    rp = kb % 2
                    sc.w(m_rawc[rp])
                    dst = at_raw[rp][:]
                else:
                    dst = at[sl][:]
                m_exp[sl] = sc.op(sc.r.activation(out=dst, in_=ps_s4[sl][:, 0:512],
                                                  func=AF.Exp,
                                                  bias=negm_t[:, h * 4 + qg: h * 4 + qg + 1]))
                if masked:
                    rel = kb - 16 * qg
                    ve.w(m_exp[sl], m_at[sl], m_m16)
                    m_msk[sl] = ve.op(ve.r.tensor_tensor(
                        at[sl][:], at_raw[rp][:], m16v[:, rel, :], AL.mult))
                    m_rawc[rp] = m_msk[sl]
                    m_rdy[sl] = m_msk[sl]
                else:
                    m_rdy[sl] = m_exp[sl]
                if kb >= 2:
                    av0(kb - 2)
            av0(nkb - 2)
            av0(nkb - 1)
            sc.w(te.mark())
            m_oev = sc.op(sc.r.activation(out=sq_s[0:65, 0:512], in_=ps_o[0:65, 0:512],
                                          func=AF.Copy))
            ve.w(m_oev)
            m_rl = ve.op(ve.r.reciprocal(rl_s, sq_s[64:65, 0:512]))
            te.w(m_rl)
            te.w(sc.mark())
            mmb2 = te.op(te.r.matmul(ps_l[0:E0, 0:512], ones1f[0:1, 0:E0], rl_s,
                                     start=True, stop=True))
            sc.w(mmb2)
            m_rlb = sc.op(sc.r.activation(out=rlb_s[0:E0, :], in_=ps_l[0:E0, 0:512], func=AF.Copy))
            ve.w(m_rlb, m_oev)
            ve.op(ve.r.tensor_tensor(
                o_all[hsub * E0:(hsub + 1) * E0, hp * TPC + qg * 512: hp * TPC + (qg + 1) * 512],
                sq_s[0:E0, 0:512], rlb_s[0:E0, :], AL.mult))

    te.w(sc.mark())
    te.w(ve.mark())
    m_gb = sc.mark()
    for n4 in range(4):
        te.w(m_gb)
        te.w(m_wload)
        mmg = te.op(te.r.matmul(ps_l[0:P, 0:512], ones1f[:, :],
                                grow[0:1, n4 * 512:(n4 + 1) * 512], start=True, stop=True))
        sc.w(mmg)
        m_gb = sc.op(sc.r.activation(out=gbt[:, n4 * 512:(n4 + 1) * 512],
                                     in_=ps_l[0:P, 0:512], func=AF.Copy))

    m_prev = sc.mark()
    for j in range(TILES_PC):
        sy.w(ve.mark())
        sy.w(sc.mark())
        m_x = sy.dma(sy.r.dma_start(out=x_s[:], in_=io["xown"](j)))
        ve.w(te.mark())
        m_ln1 = e_ln(k, x_s[:], xin_s[:], GLUE_EPS, DM, sq=sq_s[:, 0:DM], st=stat,
                     after=(m_x,))
        for n2 in range(2):
            te.w(ve.mark())
            te.w(m_prev)
            for hp in range(DC):
                mm = te.op(te.r.matmul(ps_s2[n2][:, 0:512],
                                       o_all[:, hp * TPC + j * P: hp * TPC + (j + 1) * P],
                                       wo_s[:, hp * DM + n2 * 512: hp * DM + (n2 + 1) * 512],
                                       start=(hp == 0), stop=(hp == DC - 1)))
            sc.w(mm)
            sc.w(m_ln1)
            m_prev = sc.op(sc.r.activation(out=sq_s[:, n2 * 512:(n2 + 1) * 512],
                                           in_=ps_s2[n2][:, 0:512], func=AF.Copy))
        ve.w(m_prev, m_ln1)
        m_y1 = ve.op(ve.r.tensor_tensor(sq_s[:, DM:2 * DM], xin_s[:], sq_s[:, 0:DM], AL.add))
        m_ln2 = e_ln(k, sq_s[:, DM:2 * DM], xln2[:], LN_EPS0, DM, sq=sq_s[:, 0:DM], st=stat,
                     after=(m_y1,))
        ve.w(m_ln2)
        m_t1 = ve.op(ve.r.tensor_tensor(xin_s[:], xln2[:], gbt[:, 0:DM], AL.mult))
        ve.w(m_t1)
        m_t2 = ve.op(ve.r.tensor_tensor(xln2[:], xin_s[:], gbt[:, DM:2 * DM], AL.add))
        ve.w(m_t2, m_x)
        ve.w(sy.dmark())
        m_out = ve.op(ve.r.tensor_tensor(xin_s[:], xln2[:], x_s[:], AL.add))
        sy.w(m_out)
        sy.dma(sy.r.dma_start(out=io["xo"](j), in_=xin_s[:]))


def nc_attn0():
    k = K("attn0")
    nc = k.nc
    xo_d = nc.dram_tensor("xo", [TPC, DM], F32, kind="ExternalOutput").ap()
    xown_d = nc.dram_tensor("xown", [TPC, DM], F32, kind="ExternalInput").ap()
    xb_d = nc.dram_tensor("xb", [SEQ, DM], F32, kind="ExternalInput").ap()
    ident_d = nc.dram_tensor("ident", [P, P], F32, kind="ExternalInput").ap()
    io = {
        "wq": nc.dram_tensor("wq", [H0, DM, E0], F32, kind="ExternalInput").ap(),
        "wk": nc.dram_tensor("wk", [H0, DM, E0], F32, kind="ExternalInput").ap(),
        "wv": nc.dram_tensor("wv", [H0, DM, E0], F32, kind="ExternalInput").ap(),
        "wo": nc.dram_tensor("wo", [DM, DM], F32, kind="ExternalInput").ap(),
        "gam": nc.dram_tensor("gam", [DM], F32, kind="ExternalInput").ap(),
        "bet": nc.dram_tensor("bet", [DM], F32, kind="ExternalInput").ap(),
        "m16": nc.dram_tensor("m16", [16, P, 512], F32, kind="ExternalInput").ap(),
        "kT_d": nc.dram_tensor("kT_scr", [DM, SEQ], BF16).ap(),
        "v_d": nc.dram_tensor("v_scr", [SEQ, DM], BF16).ap(),
    }
    io["xb"] = lambda t: xb_d[t * P:(t + 1) * P, :]
    io["xown"] = lambda t: xown_d[t * P:(t + 1) * P, :]
    io["xo"] = lambda t: xo_d[t * P:(t + 1) * P, :]
    with k.ctx:
        k.start()
        k.ident = k.sbuf("g_ident", [P, P], BF16)
        k.ones1f = k.sbuf("g_ones1f", [1, P], F32)
        k.ones1b = k.sbuf("g_ones1b", [1, P], BF16)
        k.ones_col = k.sbuf("g_ones_col", [P, 1], BF16)
        with nc.Block() as block:
            @block.sync
            def _(_h):
                m_id = k.gp.dma(k.gp.r.dma_start(out=k.ident[:], in_=ident_d[:, :]))
                k.ve.op(k.ve.r.memset(k.ones1f[:], 1.0))
                k.ve.op(k.ve.r.memset(k.ones1b[:], 1.0))
                m_on = k.ve.op(k.ve.r.memset(k.ones_col[:], 1.0))
                io["after"] = (m_id, m_on)
                emit_attn0(k, "a0_", k.ctx, io)
                k.sy.w(k.sy.dmark())
    return k.nc


def build_mask16(r):
    m = np.zeros((16, P, 512), np.float32)
    for i in range(16):
        for jj in range(4):
            qg = (r + 4 * jj) * P + np.arange(P)[None, :]
            kg = i * P + np.arange(P)[:, None]
            m[i, :, jj * P:(jj + 1) * P] = (qg >= kg).astype(np.float32)
    return m


def run_attn0(x, wq, wk, wv, wo, gam, bet, ident):
    nc = _get("attn0", nc_attn0)
    in_maps = []
    for c in range(NCORES):
        b, r = c // 4, c % 4
        in_maps.append({
            "xown": np.ascontiguousarray(x[own_rows(c)]),
            "xb": np.ascontiguousarray(x[b * SEQ:(b + 1) * SEQ]),
            "ident": ident, "wq": wq, "wk": wk, "wv": wv, "wo": wo,
            "gam": gam, "bet": bet, "m16": build_mask16(r),
        })
    res = run_bass_kernel_spmd(nc, in_maps, core_ids=list(range(NCORES)))
    out = np.empty((NTOK, DM), np.float32)
    for c in range(NCORES):
        out[own_rows(c)] = res.results[c]["xo"]
    return out



def emit_moeB(k, pfx, sctx, io):
    nc = k.nc
    NT_ALL = NTOK // P
    FC = DFF0 // P
    SC = 256
    NSC = CAP // SC
    gp, te, sc, ve, sy = k.gp, k.te, k.sc, k.ve, k.sy
    for _e in (gp, te, sc, ve, sy):
        _e.w(*io.get('after', ()))
    ident = k.ident
    ones1f = k.ones1f
    ones1b = k.ones1b
    triu = k.sbuf(pfx + "triu_s", [P, P], F32, ctx=sctx)
    ones_cf = k.sbuf(pfx + "ones_cf", [P, 1], F32, ctx=sctx)
    w1q = k.sbuf(pfx + "w1q", [P, DC * DFF0], BF16, ctx=sctx)
    w2q = k.sbuf(pfx + "w2q", [P, FC * DM], BF16, ctx=sctx)
    b1_s = k.sbuf(pfx + "b1_s", [P, FC], F32, ctx=sctx)
    b2row = k.sbuf(pfx + "b2row", [1, DM], BF16, ctx=sctx)
    rwT = k.sbuf(pfx + "rwT", [P, DC * NEXP], BF16, ctx=sctx)
    x4 = k.sbuf(pfx + "x4", [P, 8 * DM], F32, ctx=sctx)
    sq4 = k.sbuf(pfx + "sq4", [P, 2 * DM], F32, ctx=sctx)
    xlnT = k.sbuf(pfx + "xlnT", [P, DM], BF16, ctx=sctx)
    u16 = k.sbuf(pfx + "u16", [P, FC * SC], BF16, ctx=sctx)
    st4 = k.sbuf(pfx + "st4", [P, 16], F32, ctx=sctx)
    lg4 = k.sbuf(pfx + "lg4", [P, 8 * 32], F32, ctx=sctx)
    ohb4 = k.sbuf(pfx + "ohb4", [P, 32], F32, ctx=sctx)
    pg4 = k.sbuf(pfx + "pg4", [P, 32], F32, ctx=sctx)
    tok4 = k.sbuf(pfx + "tok4", [P, 32], F32, ctx=sctx)
    dst4i = k.sbuf(pfx + "dst4i", [P, 8], I32, ctx=sctx)
    gi4 = k.sbuf(pfx + "gi4", [P, 8], I32, ctx=sctx)
    gk4 = k.sbuf(pfx + "gk4", [P, 8], F32, ctx=sctx)
    base_row = k.sbuf(pfx + "base_row", [1, NEXP], F32, ctx=sctx)
    base4 = k.sbuf(pfx + "base4", [1, 32], F32, ctx=sctx)
    cs4 = k.sbuf(pfx + "cs4", [1, 32], F32, ctx=sctx)
    base4t = k.sbuf(pfx + "base4t", [P, 32], F32, ctx=sctx)
    rows2 = k.sbuf(pfx + "rows2", [1, 64], F32, ctx=sctx)
    esel_t = k.sbuf(pfx + "esel_t", [P, 32], F32, ctx=sctx)
    e8_t = k.sbuf(pfx + "e8_t", [P, 32], F32, ctx=sctx)
    zero_s = k.sbuf(pfx + "zero_s", [P, DM], BF16, ctx=sctx)
    btile = k.sbuf(pfx + "btile", [P, DM], BF16, ctx=sctx)
    btile2 = k.sbuf(pfx + "btile2", [P, DM], BF16, ctx=sctx)
    bufT = k.sbuf(pfx + "bufT", [P, DC * SC], BF16, ctx=sctx)
    hT = u16
    eot = k.sbuf(pfx + "eot", [P, DM], BF16, ctx=sctx)
    ps_t2 = [k.psum(pfx + "ps_ta", [P, 512], ctx=sctx), k.psum(pfx + "ps_tb", [P, 512], ctx=sctx)]
    ps_a = k.psum(pfx + "ps_a", [P, 512], ctx=sctx)
    ps_b = k.psum(pfx + "ps_b", [P, 512], ctx=sctx)
    ps_c = k.psum(pfx + "ps_c", [P, 512], ctx=sctx)

    gp.dma(gp.r.dma_start(out=triu[:], in_=io["triu"][:, :]))
    for d in range(DC):
        gp.dma(gp.r.dma_start(out=w1q[:, d * DFF0:(d + 1) * DFF0], in_=io["w1"][d * P:(d + 1) * P, :]))
    for f in range(FC):
        gp.dma(gp.r.dma_start(out=w2q[:, f * DM:(f + 1) * DM], in_=io["w2"][f * P:(f + 1) * P, :]))
        gp.dma(gp.r.dma_start(out=b1_s[:, f:f + 1], in_=io["b1"][f * P:(f + 1) * P].unsqueeze(1)))
    gp.dma(gp.r.dma_start(out=b2row[:], in_=io["b2"][:].unsqueeze(0)))
    gp.dma(gp.r.dma_start(out=xlnT[0:NEXP, 0:DM], in_=io["rw"][:, :]))
    gp.dma(gp.r.dma_start(out=rows2[0:1, 0:32], in_=io["esel"][:, :]))
    gp.dma(gp.r.dma_start(out=rows2[0:1, 32:64], in_=io["e8"][:, :]))
    m_wload = m_rwn = m_rows = gp.dmark()
    ve.op(ve.r.memset(ones_cf[:], 1.0))
    ve.op(ve.r.memset(zero_s[:], 0.0))
    m_base0 = ve.op(ve.r.memset(base_row[0:1, :], 0.0))
    te.w(m_wload, m_base0)
    te.w(m_rwn, m_wload)
    m_rt = sc.mark()
    for d in range(DC):
        te.w(m_rt)
        mmr = te.op(te.r.matmul(ps_b[0:P, 0:NEXP], xlnT[0:NEXP, d * P:(d + 1) * P],
                                ident[0:NEXP, 0:NEXP], start=True, stop=True))
        sc.w(mmr)
        m_rt = sc.op(sc.r.activation(out=rwT[:, d * NEXP:(d + 1) * NEXP],
                                     in_=ps_b[0:P, 0:NEXP], func=AF.Copy))
    te.w(m_rows)
    mm1 = te.op(te.r.matmul(ps_a[:, 0:32], ones1f[:, :], rows2[0:1, 0:32], start=True, stop=True))
    sc.w(mm1)
    m_es = sc.op(sc.r.activation(out=esel_t[:], in_=ps_a[:, 0:32], func=AF.Copy))
    te.w(m_es)
    mm2 = te.op(te.r.matmul(ps_a[:, 0:32], ones1f[:, :], rows2[0:1, 32:64], start=True, stop=True))
    sc.w(mm2)
    m_e8 = sc.op(sc.r.activation(out=e8_t[:], in_=ps_a[:, 0:32], func=AF.Copy))
    sy.w(ve.mark())
    for t in range(CAP // P):
        sy.dma(sy.r.dma_start(out=io["buf_d"][t * P:(t + 1) * P, :], in_=zero_s[:]))
    m_zero = sy.dmark()

    xlnT3 = xlnT[:].rearrange("p (c r) -> p c r", r=P)
    NBATCH = NT_ALL // 4
    sq4v = sq4[:].rearrange("p (g f) -> p g f", f=DM)

    def lgr(r):
        return lg4[:, r * 32:(r + 1) * 32]

    def lgrv(r):
        return lgr(r).rearrange("p (g e) -> p g e", e=NEXP)
    mxb = tok4[:, 0:4].unsqueeze(2).broadcast_to([P, 4, NEXP])
    scat_hist = []
    st_hist = []
    m_ap_prev = m_sq_prev = None
    m_ss2_prev = None
    m_rstd_prev = None
    m_el_prev = None
    m_sub_prev = None
    m_tri_prev = None
    m_bc_prev = None
    m_lg_prev = m_e8
    xs_sy = [sctx.enter_context(nc.semaphore(pfx + "xsy0")),
             sctx.enter_context(nc.semaphore(pfx + "xsy1"))]
    xs_gp = [sctx.enter_context(nc.semaphore(pfx + "xgp0")),
             sctx.enter_context(nc.semaphore(pfx + "xgp1"))]
    xcnt = [0, 0]

    def issue_loads(bb):
        pp = (bb % 2) * 4 * DM
        for i in range(2):
            sy.r.dma_start(out=x4[:, pp + i * DM: pp + (i + 1) * DM],
                           in_=io["xf"](4 * bb + i)).then_inc(xs_sy[bb % 2], 16)
        for i in range(2, 4):
            gp.r.dma_start(out=x4[:, pp + i * DM: pp + (i + 1) * DM],
                           in_=io["xf"](4 * bb + i)).then_inc(xs_gp[bb % 2], 16)
        xcnt[bb % 2] += 32
        return ((xs_sy[bb % 2], xcnt[bb % 2]), (xs_gp[bb % 2], xcnt[bb % 2]))
    sy.w(m_zero)
    gp.w(m_zero)
    m_x_next = issue_loads(0)
    for b in range(NBATCH):
        par = b % 2
        xlnp = u16[:, par * (4 * DM):(par + 1) * (4 * DM)]
        xp = x4[:, par * (4 * DM):(par + 1) * (4 * DM)]
        xpv = xp.rearrange("p (g f) -> p g f", f=DM)
        m_x = m_x_next
        if b + 1 < NBATCH:
            if m_ap_prev is not None:
                sy.w(m_ap_prev, m_sq_prev)
                gp.w(m_ap_prev, m_sq_prev)
            m_x_next = issue_loads(b + 1)
        ve.w(*m_x)
        m_sum = ve.op(ve.r.tensor_reduce(st4[:, 0:4], xpv, axis=AX.X, op=AL.add))
        m_ss2 = None
        for h in range(2):
            sc.w(*m_x)
            if h == 0 and m_ss2_prev is not None:
                sc.w(m_ss2_prev)
            if h == 1:
                sc.w(m_ss2)
            m_sq = sc.op(sc.r.activation(out=sq4[:], in_=xp[:, h * 2 * DM:(h + 1) * 2 * DM],
                                         func=AF.Square))
            ve.w(m_sq)
            m_ss2 = ve.op(ve.r.tensor_reduce(st4[:, 4 + 2 * h:6 + 2 * h], sq4v, axis=AX.X, op=AL.add))
        m_sq_prev = m_sq
        m_ss2_prev = m_ss2
        ve.w(m_sum)
        m_mean = ve.op(ve.r.tensor_scalar_mul(st4[:, 0:4], st4[:, 0:4], 1.0 / DM))
        ve.w(m_ss2)
        m_ex2 = ve.op(ve.r.tensor_scalar_mul(st4[:, 4:8], st4[:, 4:8], 1.0 / DM))
        ve.w(m_mean)
        if m_rstd_prev is not None:
            ve.w(m_rstd_prev)
        m_m2 = ve.op(ve.r.tensor_tensor(st4[:, 8:12], st4[:, 0:4], st4[:, 0:4], AL.mult))
        ve.w(m_ex2, m_m2)
        m_var = ve.op(ve.r.tensor_tensor(st4[:, 8:12], st4[:, 4:8], st4[:, 8:12], AL.subtract))
        ve.w(m_var)
        m_vpe = ve.op(ve.r.tensor_single_scalar(st4[:, 8:12], st4[:, 8:12], float(GLUE_EPS), AL.add))
        sc.w(m_vpe)
        m_ln = sc.op(sc.r.activation(out=st4[:, 12:16], in_=st4[:, 8:12], func=AF.Ln))
        sc.w(m_ln)
        m_rstd = sc.op(sc.r.activation(out=st4[:, 8:12], in_=st4[:, 12:16], func=AF.Exp, scale=-0.5))
        m_rstd_prev = m_rstd
        ve.w(m_rstd)
        m_b2 = ve.op(ve.r.scalar_tensor_tensor(st4[:, 12:16], st4[:, 0:4], -1.0, st4[:, 8:12], AL.mult, AL.mult))
        ve.w(m_b2)
        if b >= 2:
            ve.w(scat_hist[b - 2])
        ap_ms = []
        for i in range(4):
            m_ap = ve.op(ve.r.tensor_scalar(xlnp[:, i * DM:(i + 1) * DM], xp[:, i * DM:(i + 1) * DM],
                                            st4[:, 8 + i:9 + i], st4[:, 12 + i:13 + i],
                                            AL.mult, AL.add))
            ap_ms.append(m_ap)
        m_ap_prev = ap_ms[-1]
        te.w(m_lg_prev)
        for i in range(4):
            m_tr = e_transp(k, xlnp[:, i * DM:(i + 1) * DM], xlnT3, DC, ps_t2, ident,
                            after=(ap_ms[i], m_wload))
            te.w(m_tr)
            for d in range(DC):
                mm = te.op(te.r.matmul(ps_a[:, i * NEXP:(i + 1) * NEXP], xlnT3[:, d, :],
                                       rwT[:, d * NEXP:(d + 1) * NEXP],
                                       start=(d == 0), stop=(d == DC - 1)))
        sc.w(mm)
        if m_sub_prev is not None:
            sc.w(m_sub_prev)
        m_lg = sc.op(sc.r.activation(out=lgr(0), in_=ps_a[:, 0:32], func=AF.Copy))
        m_lg_prev = m_lg
        ve.w(m_lg)
        m_mx = ve.op(ve.r.tensor_reduce(tok4[:, 0:4], lgrv(0), axis=AX.X, op=AL.max))
        ve.w(m_mx)
        if m_el_prev is not None:
            ve.w(m_el_prev)
        m_sub = ve.op(ve.r.tensor_tensor(lgrv(7), lgrv(0), mxb, AL.subtract))
        m_sub_prev = m_sub
        sc.w(m_sub)
        m_el = sc.op(sc.r.activation(out=lgr(1), in_=lgr(7), func=AF.Exp))
        m_el_prev = m_el
        ve.w(m_el)
        m_sm = ve.op(ve.r.tensor_reduce(tok4[:, 4:8], lgrv(1), axis=AX.X, op=AL.add))
        ve.w(m_sm)
        m_p = ve.op(ve.r.reciprocal(tok4[:, 8:12], tok4[:, 4:8]))
        ve.w(m_sub)
        m_oh = ve.op(ve.r.tensor_single_scalar(lgr(2), lgr(7), 0.0, AL.is_ge))
        ve.w(m_oh)
        m_s0 = ve.op(ve.r.memset(lgr(3), 0.0))
        ve.w(m_s0)
        m_s1 = ve.op(ve.r.tensor_copy(lgrv(3)[:, :, 1:8], lgrv(2)[:, :, 0:7]))
        ve.w(m_s1)
        m_s2 = ve.op(ve.r.tensor_copy(lgr(4), lgr(3)))
        ve.w(m_s2)
        m_s3 = ve.op(ve.r.scalar_tensor_tensor(lgrv(4)[:, :, 2:8], lgrv(3)[:, :, 0:6], 1.0, lgrv(4)[:, :, 2:8], AL.mult, AL.add))
        ve.w(m_s3)
        m_s4 = ve.op(ve.r.tensor_copy(lgr(5), lgr(4)))
        ve.w(m_s4)
        m_s5 = ve.op(ve.r.scalar_tensor_tensor(lgrv(5)[:, :, 4:8], lgrv(4)[:, :, 0:4], 1.0, lgrv(5)[:, :, 4:8], AL.mult, AL.add))
        ve.w(m_s5)
        m_pre = ve.op(ve.r.tensor_single_scalar(lgr(6), lgr(5), 0.5, AL.is_lt))
        ve.w(m_pre)
        if m_tri_prev is not None:
            ve.w(m_tri_prev)
        m_ohs = ve.op(ve.r.tensor_tensor(ohb4[:], lgr(2), lgr(6), AL.mult))
        te.w(m_ohs, m_lg)
        for i in range(4):
            mm_tri = te.op(te.r.matmul(ps_b[:, i * NEXP:(i + 1) * NEXP], triu[:],
                                       ohb4[:, i * NEXP:(i + 1) * NEXP], start=True, stop=True))
        mm_cs = te.op(te.r.matmul(ps_c[0:1, 0:32], ones_cf[:, :], ohb4[:, 0:32], start=True, stop=True))
        m_tri_prev = mm_cs
        sc.w(mm_tri)
        m_pg = sc.op(sc.r.activation(out=pg4[:], in_=ps_b[:, 0:32], func=AF.Copy))
        sc.w(mm_cs)
        m_cse = sc.op(sc.r.activation(out=cs4[0:1, :], in_=ps_c[0:1, 0:32], func=AF.Copy))
        ve.w(m_cse)
        if m_bc_prev is not None:
            ve.w(m_bc_prev)
        m_c0 = ve.op(ve.r.tensor_copy(base4[0:1, 0:8], base_row[0:1, 0:8]))
        ve.w(m_c0)
        m_c1 = ve.op(ve.r.tensor_tensor(base4[0:1, 8:16], base4[0:1, 0:8], cs4[0:1, 0:8], AL.add))
        ve.w(m_c1)
        m_c2 = ve.op(ve.r.tensor_tensor(base4[0:1, 16:24], base4[0:1, 8:16], cs4[0:1, 8:16], AL.add))
        ve.w(m_c2)
        m_c3 = ve.op(ve.r.tensor_tensor(base4[0:1, 24:32], base4[0:1, 16:24], cs4[0:1, 16:24], AL.add))
        ve.w(m_c3)
        m_bru = ve.op(ve.r.tensor_tensor(base_row[0:1, 0:8], base4[0:1, 24:32], cs4[0:1, 24:32], AL.add))
        te.w(m_c3, m_cse)
        mm_bc = te.op(te.r.matmul(ps_c[:, 0:32], ones1f[:, :], base4[0:1, 0:32], start=True, stop=True))
        m_bc_prev = mm_bc
        sc.w(mm_bc)
        m_b4t = sc.op(sc.r.activation(out=base4t[:], in_=ps_c[:, 0:32], func=AF.Copy))
        ve.w(m_pg, m_b4t)
        m_pos = ve.op(ve.r.tensor_tensor(pg4[:], pg4[:], base4t[:], AL.add))
        ve.w(m_pos)
        m_t1 = ve.op(ve.r.tensor_tensor(lgr(7), pg4[:], ohb4[:], AL.mult))
        ve.w(m_t1)
        m_pt = ve.op(ve.r.tensor_reduce(tok4[:, 12:16], lgrv(7), axis=AX.X, op=AL.add))
        ve.w(m_pt)
        m_t2 = ve.op(ve.r.tensor_tensor(lgr(7), e8_t[:], ohb4[:], AL.mult))
        ve.w(m_t2)
        m_et = ve.op(ve.r.tensor_reduce(tok4[:, 16:20], lgrv(7), axis=AX.X, op=AL.add))
        ve.w(m_et)
        m_t3 = ve.op(ve.r.tensor_tensor(lgr(7), esel_t[:], ohb4[:], AL.mult))
        ve.w(m_t3)
        m_oc = ve.op(ve.r.tensor_reduce(tok4[:, 20:24], lgrv(7), axis=AX.X, op=AL.add))
        if b >= 2:
            ve.w(st_hist[b - 2])
        ve.w(m_pt, m_p)
        m_kp = ve.op(ve.r.tensor_single_scalar(tok4[:, 24:28], tok4[:, 12:16], float(CAP), AL.is_lt))
        ve.w(m_kp)
        m_k2 = ve.op(ve.r.scalar_tensor_tensor(gk4[:, 4 * par:4 * par + 4], tok4[:, 24:28], 0.5, tok4[:, 8:12], AL.mult, AL.mult))
        ve.w(m_k2)
        m_pc = ve.op(ve.r.tensor_single_scalar(tok4[:, 24:28], tok4[:, 12:16], float(CAP - 1), AL.min))
        ve.w(m_et, m_pc)
        m_gi = ve.op(ve.r.scalar_tensor_tensor(tok4[:, 28:32], tok4[:, 16:20], float(CAP), tok4[:, 24:28], AL.mult, AL.add))
        ve.w(m_gi)
        m_gii = ve.op(ve.r.tensor_copy(gi4[:, 4 * par:4 * par + 4], tok4[:, 28:32]))
        ve.w(m_oc)
        m_d1 = ve.op(ve.r.tensor_scalar(tok4[:, 20:24], tok4[:, 20:24], -1.0, 1.0, AL.mult, AL.add))
        ve.w(m_d1)
        m_d2 = ve.op(ve.r.scalar_tensor_tensor(tok4[:, 24:28], tok4[:, 20:24], BIGF, tok4[:, 12:16], AL.mult, AL.add))
        ve.w(m_d2)
        m_d3 = ve.op(ve.r.tensor_single_scalar(tok4[:, 24:28], tok4[:, 24:28], float(CAP), AL.min))
        ve.w(m_d3)
        m_di = ve.op(ve.r.tensor_copy(dst4i[:, 4 * par:4 * par + 4], tok4[:, 24:28]))
        sy.w(m_k2, m_gii)
        for i in range(4):
            sy.dma(sy.r.dma_start(out=io["gidx_d"][(4 * b + i) * P:(4 * b + i + 1) * P, :],
                                  in_=gi4[:, 4 * par + i:4 * par + i + 1]))
            sy.dma(sy.r.dma_start(out=io["keep2_d"][(4 * b + i) * P:(4 * b + i + 1) * P, :],
                                  in_=gk4[:, 4 * par + i:4 * par + i + 1]))
        st_hist.append(sy.dmark())
        gp.r.wait_ge(k.ve.sem, m_di[1])
        for i in range(4):
            gp.dma(gp.r.indirect_dma_start(
                out=io["buf_d"][:, :], out_offset=IndirectOffsetOnAxis(dst4i[:, 4 * par + i:4 * par + i + 1], 0),
                in_=xlnp[:, i * DM:(i + 1) * DM], in_offset=None))
        scat_hist.append(gp.dmark())
    m_scat = gp.dmark()

    for s in range(NSC):
        sy.w(te.mark())
        sy.w(sy.dmark())
        gp.r.wait_ge(k.sy.dsem, k.sy.dcnt)
        sy.w(m_scat)
        for ti in range(2):
            sy.dma(sy.r.dma_start(out=btile[:] if ti == 0 else btile2[:],
                                  in_=io["buf_d"][s * SC + ti * P: s * SC + (ti + 1) * P, :]))
        m_bl = sy.dmark()
        bufT3 = bufT[:].rearrange("p (c r) -> p c r", r=SC)
        m_tr1 = e_transp(k, btile[:], bufT3[:, :, 0:P], DC, ps_t2, ident, after=(m_bl,))
        m_tr2 = e_transp(k, btile2[:], bufT3[:, :, P:SC], DC, ps_t2, ident, after=(m_bl,))
        m_hvb = [sc.mark(), sc.mark()]
        for f in range(FC):
            te.w(m_tr1, m_tr2)
            te.w(m_hvb[f % 2])
            for d in range(DC):
                mm = te.op(te.r.matmul(ps_a[:, 0:SC] if f % 2 == 0 else ps_b[:, 0:SC],
                                       w1q[:, d * DFF0 + f * P: d * DFF0 + (f + 1) * P],
                                       bufT[:, d * SC:(d + 1) * SC],
                                       start=(d == 0), stop=(d == DC - 1)))
            sc.w(mm)
            sc.w(m_wload)
            m_hvb[f % 2] = sc.op(sc.r.activation(out=hT[:, f * SC:(f + 1) * SC],
                                                 in_=ps_a[:, 0:SC] if f % 2 == 0 else ps_b[:, 0:SC],
                                                 func=AF.Gelu, bias=b1_s[:, f:f + 1]))
        m_hv = m_hvb[1]
        for sm in range(2):
            m_eev = sc.mark()
            for n2 in range(2):
                te.w(m_hv, m_eev)
                te.op(te.r.matmul(ps_a[:, 0:512] if n2 == 0 else ps_b[:, 0:512],
                                  ones1b[:, :], b2row[0:1, n2 * 512:(n2 + 1) * 512],
                                  start=True, stop=False))
                for f in range(FC):
                    mm = te.op(te.r.matmul(ps_a[:, 0:512] if n2 == 0 else ps_b[:, 0:512],
                                           hT[:, f * SC + sm * P: f * SC + sm * P + P],
                                           w2q[:, f * DM + n2 * 512: f * DM + (n2 + 1) * 512],
                                           start=False, stop=(f == FC - 1)))
                sc.w(mm)
                sc.w((k.sy.dsem, k.sy.dcnt))
                m_eev = sc.op(sc.r.activation(out=eot[:, n2 * 512:(n2 + 1) * 512],
                                              in_=ps_a[:, 0:512] if n2 == 0 else ps_b[:, 0:512],
                                              func=AF.Copy))
            sy.w(m_eev)
            sy.dma(sy.r.dma_start(out=io["eo_d"][s * SC + sm * P: s * SC + (sm + 1) * P, :], in_=eot[:]))


def emit_moeC(k, pfx, sctx, io):
    nc = k.nc
    gp, te, sc, ve, sy = k.gp, k.te, k.sc, k.ve, k.sy
    for _e in (gp, te, sc, ve, sy):
        _e.w(*io.get('after', ()))
    oix = k.sbuf(pfx + "oix", [P, 1], I32, ctx=sctx) if "ownidx" in io else None
    x_s = k.sbuf(pfx + "x_s", [P, DM], F32, ctx=sctx)
    g_s = k.sbuf(pfx + "g_s", [P, DM], BF16, ctx=sctx)
    o_s = k.sbuf(pfx + "o_s", [P, DM], F32, ctx=sctx)
    gi = k.sbuf(pfx + "gi", [P, 1], I32, ctx=sctx)
    kp = k.sbuf(pfx + "kp", [P, 1], F32, ctx=sctx)
    for t in range(TILES_PC):
        sy.w(ve.mark())
        m_x = sy.dma(sy.r.dma_start(out=x_s[:], in_=io["xown"](t)))
        if oix is not None:
            gp.r.wait_ge(k.ve.sem, k.ve.cnt)
            m_oix = gp.dma(gp.r.dma_start(out=oix[:], in_=io["ownidx"][t * P:(t + 1) * P, :]))
            gp.r.wait_ge(k.gp.dsem, m_oix[1])
            gp.dma(gp.r.indirect_dma_start(
                out=gi[:, :], out_offset=None,
                in_=io["gidx_full"][:, :], in_offset=IndirectOffsetOnAxis(oix[:, :], 0)))
            m_meta = gp.dma(gp.r.indirect_dma_start(
                out=kp[:, :], out_offset=None,
                in_=io["keep2_full"][:, :], in_offset=IndirectOffsetOnAxis(oix[:, :], 0)))
            gp.r.wait_ge(k.gp.dsem, m_meta[1])
        else:
            sy.dma(sy.r.dma_start(out=gi[:], in_=io["gidx"](t)))
            m_meta = sy.dma(sy.r.dma_start(out=kp[:], in_=io["keep2"](t)))
            gp.r.wait_ge(k.sy.dsem, m_meta[1])
            gp.r.wait_ge(k.ve.sem, k.ve.cnt)
        m_g = gp.dma(gp.r.indirect_dma_start(
            out=g_s[:, :], out_offset=None,
            in_=io["eoall"][:, :], in_offset=IndirectOffsetOnAxis(gi[:, :], 0)))
        ve.w(m_g, m_x)
        ve.w(m_meta if oix is None else (k.gp.dsem, k.gp.dcnt))
        ve.w(sy.dmark())
        m_s = ve.op(ve.r.tensor_scalar_mul(o_s[:], g_s[:], kp[:, 0:1]))
        ve.w(m_s)
        m_o = ve.op(ve.r.tensor_tensor(o_s[:], o_s[:, 0:DM], x_s[:], AL.add)) if False else None
        m_o = ve.op(ve.r.tensor_tensor(x_s[:], o_s[:], x_s[:], AL.add))
        sy.w(m_o)
        sy.dma(sy.r.dma_start(out=io["xo"](t), in_=x_s[:]))


def barrier(k):
    marks = [k.te.mark(), k.sc.mark(), k.ve.mark(), k.gp.mark(), k.gp.dmark(),
             k.sy.dmark(), k.sy.mark()]
    for e in (k.te, k.sc, k.ve, k.gp, k.sy):
        e.w(*marks)


def emit_ag(k, in_ap, out_ap, groups):
    s = k.ccsem()
    k.gp.r.collective_compute("AllGather", AL.bypass, ins=[in_ap.opt()],
                              outs=[out_ap.opt()], replica_groups=groups).then_inc(s)
    return (s, 1)


def emit_prep(k, pfx, sctx, io):
    gp, te, sc, ve, sy = k.gp, k.te, k.sc, k.ve, k.sy
    for _e in (gp, te, sc, ve, sy):
        _e.w(*io.get('after', ()))
    xs = k.sbuf(pfx + "xs", [P, DI1], F32, ctx=sctx)
    qo = k.sbuf(pfx + "qo", [P, DI1], BF16, ctx=sctx)
    t = {
        "xn": k.sbuf(pfx + "t_xn", [P, DI1], F32, ctx=sctx),
        "a": k.sbuf(pfx + "t_a", [P, DI1], F32, ctx=sctx),
        "sgn": k.sbuf(pfx + "t_sgn", [P, DI1], BF16, ctx=sctx),
        "acc": k.sbuf(pfx + "t_acc", [P, DI1], BF16, ctx=sctx),
        "acc2": k.sbuf(pfx + "t_acc2", [P, DI1], BF16, ctx=sctx),
        "am": k.sbuf(pfx + "t_am", [P, NB_DI], F32, ctx=sctx),
        "scs": k.sbuf(pfx + "t_scs", [P, NB_DI], F32, ctx=sctx),
        "rcs": k.sbuf(pfx + "t_rcs", [P, NB_DI], F32, ctx=sctx),
    }
    jobs = []
    for s_ in range(3):
        jobs.append((io["p_wg"][s_, :, :], DM, NB_DM, s_ * DM))
    for s_ in range(3):
        jobs.append((io["p_wu"][s_, :, :], DM, NB_DM, (3 + s_) * DM))
    jobs.append((io["p_wd"][0, :, :], DI1, NB_DI, 6 * DM))
    for (srcap, width, nb, col0) in jobs:
        sy.w(ve.mark())
        m_ld = sy.dma(sy.r.dma_start(out=xs[:, 0:width], in_=srcap))
        ve.w(sy.dmark())
        sc.w(sy.dmark())
        tt = {kk2: (v[:, 0:nb] if kk2 in ("am", "scs", "rcs") else v[:, 0:width])
              for kk2, v in t.items()}
        m_q = e_quant(k, xs[:, 0:width], qo[:, 0:width], nb, tt, after=(m_ld,))
        sy.w(m_q)
        sy.dma(sy.r.dma_start(out=io["pack"][:, col0:col0 + width], in_=qo[:, 0:width]))


def nc_moeB():
    k = K("moeB")
    nc = k.nc
    io = {
        "eo_d": nc.dram_tensor("eo", [CAP, DM], BF16, kind="ExternalOutput").ap(),
        "gidx_d": nc.dram_tensor("gidx", [NTOK, 1], I32, kind="ExternalOutput").ap(),
        "keep2_d": nc.dram_tensor("keep2", [NTOK, 1], F32, kind="ExternalOutput").ap(),
        "rw": nc.dram_tensor("rw", [NEXP, DM], F32, kind="ExternalInput").ap(),
        "w1": nc.dram_tensor("w1", [DM, DFF0], F32, kind="ExternalInput").ap(),
        "b1": nc.dram_tensor("b1", [DFF0], F32, kind="ExternalInput").ap(),
        "w2": nc.dram_tensor("w2", [DFF0, DM], F32, kind="ExternalInput").ap(),
        "b2": nc.dram_tensor("b2", [DM], F32, kind="ExternalInput").ap(),
        "triu": nc.dram_tensor("triu", [P, P], F32, kind="ExternalInput").ap(),
        "esel": nc.dram_tensor("esel", [1, 4 * NEXP], F32, kind="ExternalInput").ap(),
        "e8": nc.dram_tensor("e8", [1, 4 * NEXP], F32, kind="ExternalInput").ap(),
        "buf_d": nc.dram_tensor("buf_scr", [CAP + 1, DM], BF16).ap(),
    }
    x_d = nc.dram_tensor("xf", [NTOK, DM], F32, kind="ExternalInput").ap()
    ident_d = nc.dram_tensor("ident", [P, P], F32, kind="ExternalInput").ap()
    io["xf"] = lambda t: x_d[t * P:(t + 1) * P, :]
    with k.ctx:
        k.start()
        k.ident = k.sbuf("g_ident", [P, P], BF16)
        k.ones1f = k.sbuf("g_ones1f", [1, P], F32)
        k.ones1b = k.sbuf("g_ones1b", [1, P], BF16)
        with nc.Block() as block:
            @block.sync
            def _(_h):
                m_id = k.gp.dma(k.gp.r.dma_start(out=k.ident[:], in_=ident_d[:, :]))
                k.ve.op(k.ve.r.memset(k.ones1f[:], 1.0))
                m_on = k.ve.op(k.ve.r.memset(k.ones1b[:], 1.0))
                io["after"] = (m_id, m_on)
                emit_moeB(k, "mb_", k.ctx, io)
                k.sy.w(k.sy.dmark())
    return k.nc


def nc_moeC():
    k = K("moeC")
    nc = k.nc
    xo_d = nc.dram_tensor("xo", [TPC, DM], F32, kind="ExternalOutput").ap()
    xown_d = nc.dram_tensor("xown", [TPC, DM], F32, kind="ExternalInput").ap()
    eo_d = nc.dram_tensor("eoall", [NEXP * CAP, DM], BF16, kind="ExternalInput").ap()
    gidx_d = nc.dram_tensor("gidx", [TPC, 1], I32, kind="ExternalInput").ap()
    keep2_d = nc.dram_tensor("keep2", [TPC, 1], F32, kind="ExternalInput").ap()
    io = {
        "xown": lambda t: xown_d[t * P:(t + 1) * P, :],
        "gidx": lambda t: gidx_d[t * P:(t + 1) * P, :],
        "keep2": lambda t: keep2_d[t * P:(t + 1) * P, :],
        "eoall": eo_d,
        "xo": lambda t: xo_d[t * P:(t + 1) * P, :],
    }
    with k.ctx:
        k.start()
        with nc.Block() as block:
            @block.sync
            def _(_h):
                emit_moeC(k, "mc_", k.ctx, io)
                k.sy.w(k.sy.dmark())
    return k.nc


def run_moe(x, rw, w1, b1, w2, b2, ident, triu):
    ncB = _get("moeB", nc_moeB)
    ncC = _get("moeC", nc_moeC)
    in_maps = []
    for c in range(NCORES):
        in_maps.append({
            "xf": x, "rw": rw,
            "w1": np.ascontiguousarray(w1[c]), "b1": np.ascontiguousarray(b1[c]),
            "w2": np.ascontiguousarray(w2[c]), "b2": np.ascontiguousarray(b2[c]),
            "ident": ident, "triu": triu,
            "esel": np.tile(np.eye(NEXP, dtype=np.float32)[c:c + 1], (1, 4)),
            "e8": np.tile(np.arange(NEXP, dtype=np.float32).reshape(1, NEXP), (1, 4)),
        })
    res = run_bass_kernel_spmd(ncB, in_maps, core_ids=list(range(NCORES)))
    eo_all = np.concatenate([res.results[c]["eo"] for c in range(NCORES)], axis=0)
    gidx_all = res.results[0]["gidx"]
    keep2_all = res.results[0]["keep2"]
    in_maps2 = []
    for c in range(NCORES):
        own = own_rows(c)
        in_maps2.append({
            "xown": np.ascontiguousarray(x[own]),
            "eoall": eo_all,
            "gidx": np.ascontiguousarray(gidx_all[own]),
            "keep2": np.ascontiguousarray(keep2_all[own]),
        })
    res2 = run_bass_kernel_spmd(ncC, in_maps2, core_ids=list(range(NCORES)))
    out = np.empty((NTOK, DM), np.float32)
    for c in range(NCORES):
        out[own_rows(c)] = res2.results[c]["xo"]
    return out


def run_attn1(x, W, ident):
    nc = _get("attn1", nc_attn1)
    in_maps = []
    for c in range(NCORES):
        b, r = c // 4, c % 4
        own = own_rows(c)
        in_maps.append({
            "xown": np.ascontiguousarray(x[own]),
            "xb": np.ascontiguousarray(x[b * SEQ:(b + 1) * SEQ]),
            "ident": ident,
            "w_dq": W["W_DQ"], "b_dq": W["b_DQ"], "w_uq": W["W_UQ"], "b_uq": W["b_UQ"],
            "w_dkv": W["W_DKV"], "b_dkv": W["b_DKV"], "w_uk": W["W_UK"], "b_uk": W["b_UK"],
            "w_uv": W["W_UV"], "b_uv": W["b_UV"], "w_qr": W["W_QR"], "b_qr": W["b_QR"],
            "w_kr": W["W_KR"], "b_kr": W["b_KR"], "w_o": W["W_O"], "b_o": W["b_O"],
        })
    res = run_bass_kernel_spmd(nc, in_maps, core_ids=list(range(NCORES)))
    out = np.empty((NTOK, DM), np.float32)
    for c in range(NCORES):
        out[own_rows(c)] = res.results[c]["xo"]
    return out


def own_rows(c):
    b, r = c // 4, c % 4
    rows = []
    for j in range(TILES_PC):
        t = r + 4 * j
        start = b * SEQ + t * P
        rows.append(np.arange(start, start + P))
    return np.concatenate(rows)


def run_ffn1(x, wgq, wuq, wdq, ident):
    nc = _get("ffn1", nc_ffn1)
    in_maps = []
    for c in range(NCORES):
        in_maps.append({
            "xs": np.ascontiguousarray(x[c * TPC:(c + 1) * TPC]),
            "wgq": wgq, "wuq": wuq, "wdq": wdq, "ident": ident,
        })
    res = run_bass_kernel_spmd(nc, in_maps, core_ids=list(range(NCORES)))
    return np.concatenate([res.results[c]["xo"] for c in range(NCORES)], axis=0)


def run_ffn1_prep(wg, wu, wd):
    nc = _get("ffn1prep", nc_ffn1_prep)
    RG, RD = DI1 // NCORES, DM // NCORES
    in_maps = [{"wg": np.ascontiguousarray(wg[c * RG:(c + 1) * RG]),
                "wu": np.ascontiguousarray(wu[c * RG:(c + 1) * RG]),
                "wd": np.ascontiguousarray(wd[c * RD:(c + 1) * RD])} for c in range(NCORES)]
    res = run_bass_kernel_spmd(nc, in_maps, core_ids=list(range(NCORES)))
    wgq = np.concatenate([res.results[c]["wgq"] for c in range(NCORES)], axis=0)
    wuq = np.concatenate([res.results[c]["wuq"] for c in range(NCORES)], axis=0)
    wdq = np.concatenate([res.results[c]["wdq"] for c in range(NCORES)], axis=0)
    return wgq, wuq, wdq



GROUPS8 = [[0, 1, 2, 3, 4, 5, 6, 7]]
GROUPS24 = [[0, 1, 2, 3], [4, 5, 6, 7]]
PACKW = 6 * DM + DI1


def _rm_global(g):
    b, t = g // 64, g % 64
    rank = b * 4 + (t % 4)
    slot = t // 4
    return rank * TPC + slot * P


def _rm_batch(t):
    return (t % 4) * TPC + (t // 4) * P


def nc_mega(stages=6):
    k = K("mega")
    nc = k.nc
    xout_d = nc.dram_tensor("xout", [TPC, DM], F32, kind="ExternalOutput").ap()
    xmy0_d = nc.dram_tensor("xmy0", [TPC, DM], F32, kind="ExternalInput").ap()
    xfull_d = nc.dram_tensor("xfull", [NTOK, DM], F32, kind="ExternalInput").ap()
    ident_d = nc.dram_tensor("ident", [P, P], F32, kind="ExternalInput").ap()
    ownidx_d = nc.dram_tensor("ownidx", [TPC, 1], I32, kind="ExternalInput").ap()
    moe_io = {
        "rw": nc.dram_tensor("rw", [NEXP, DM], F32, kind="ExternalInput").ap(),
        "w1": nc.dram_tensor("w1", [DM, DFF0], F32, kind="ExternalInput").ap(),
        "b1": nc.dram_tensor("b1", [DFF0], F32, kind="ExternalInput").ap(),
        "w2": nc.dram_tensor("w2", [DFF0, DM], F32, kind="ExternalInput").ap(),
        "b2": nc.dram_tensor("b2", [DM], F32, kind="ExternalInput").ap(),
        "triu": nc.dram_tensor("triu", [P, P], F32, kind="ExternalInput").ap(),
        "esel": nc.dram_tensor("esel", [1, 4 * NEXP], F32, kind="ExternalInput").ap(),
        "e8": nc.dram_tensor("e8", [1, 4 * NEXP], F32, kind="ExternalInput").ap(),
    }
    a0_io = {
        "wq": nc.dram_tensor("wq", [H0, DM, E0], F32, kind="ExternalInput").ap(),
        "wk": nc.dram_tensor("wk", [H0, DM, E0], F32, kind="ExternalInput").ap(),
        "wv": nc.dram_tensor("wv", [H0, DM, E0], F32, kind="ExternalInput").ap(),
        "wo": nc.dram_tensor("wo", [DM, DM], F32, kind="ExternalInput").ap(),
        "gam": nc.dram_tensor("gam", [DM], F32, kind="ExternalInput").ap(),
        "bet": nc.dram_tensor("bet", [DM], F32, kind="ExternalInput").ap(),
        "m16": nc.dram_tensor("m16", [16, P, 512], F32, kind="ExternalInput").ap(),
    }
    a1w = [("w_dq", [DM, DC1]), ("b_dq", [DC1]), ("w_uq", [DC1, DM]), ("b_uq", [DM]),
           ("w_dkv", [DM, DC1]), ("b_dkv", [DC1]), ("w_uk", [DC1, DM]), ("b_uk", [DM]),
           ("w_uv", [DC1, DM]), ("b_uv", [DM]), ("w_qr", [DC1, 256]), ("b_qr", [256]),
           ("w_kr", [DM, DR1]), ("b_kr", [DR1]), ("w_o", [DM, DM]), ("b_o", [DM])]
    a1_io = {nm: nc.dram_tensor(nm, sh, F32, kind="ExternalInput").ap() for nm, sh in a1w}
    prep_io = {
        "p_wg": nc.dram_tensor("p_wg", [3, P, DM], F32, kind="ExternalInput").ap(),
        "p_wu": nc.dram_tensor("p_wu", [3, P, DM], F32, kind="ExternalInput").ap(),
        "p_wd": nc.dram_tensor("p_wd", [1, P, DI1], F32, kind="ExternalInput").ap(),
    }
    xmy_d = nc.dram_tensor("xmy", [TPC, DM], F32).ap()
    xagB_d = nc.dram_tensor("xagB", [4 * TPC, DM], F32).ap()
    xagF_d = nc.dram_tensor("xagF", [NTOK, DM], F32).ap()
    eomy_d = nc.dram_tensor("eomy", [CAP, DM], BF16).ap()
    eoag_d = nc.dram_tensor("eoag", [NEXP * CAP, DM], BF16).ap()
    buf_d = nc.dram_tensor("buf_scr", [CAP + 1, DM], BF16).ap()
    gidx_d = nc.dram_tensor("gidx_scr", [NTOK, 1], I32).ap()
    keep2_d = nc.dram_tensor("keep2_scr", [NTOK, 1], F32).ap()
    hq_d = nc.dram_tensor("hq_scr", [TPC, DI1], BF16).ap()
    kT_d = nc.dram_tensor("kT_scr", [DM, SEQ], BF16).ap()
    v_d = nc.dram_tensor("v_scr", [SEQ, DM], BF16).ap()
    pack_d = nc.dram_tensor("pack_scr", [P, PACKW], BF16).ap()
    packag_d = nc.dram_tensor("packag_scr", [NCORES * P, PACKW], BF16).ap()

    moe_io.update({
        "eo_d": eomy_d, "gidx_d": gidx_d, "keep2_d": keep2_d, "buf_d": buf_d,
    })
    prep_io["pack"] = pack_d

    with k.ctx:
        k.start()
        k.ident = k.sbuf("g_ident", [P, P], BF16)
        k.ones1f = k.sbuf("g_ones1f", [1, P], F32)
        k.ones1b = k.sbuf("g_ones1b", [1, P], BF16)
        k.ones_col = k.sbuf("g_ones_col", [P, 1], BF16)
        with nc.Block() as block:
            @block.sync
            def _(_h):
                m_id = k.gp.dma(k.gp.r.dma_start(out=k.ident[:], in_=ident_d[:, :]))
                k.ve.op(k.ve.r.memset(k.ones1f[:], 1.0))
                k.ve.op(k.ve.r.memset(k.ones1b[:], 1.0))
                m_on = k.ve.op(k.ve.r.memset(k.ones_col[:], 1.0))
                base_after = (m_id, m_on)

                with contextlib.ExitStack() as sctx:
                    prep_io["after"] = base_after
                    emit_prep(k, "pp_", sctx, prep_io)
                    barrier(k)
                m_wq = emit_ag(k, pack_d[:, :], packag_d[:, :], GROUPS8)

                def moe_pass(xf_fn, xown_fn, tag, after=()):
                    with contextlib.ExitStack() as sctx:
                        mio = dict(moe_io)
                        mio["xf"] = xf_fn
                        mio["after"] = base_after + tuple(after)
                        emit_moeB(k, tag + "b_", sctx, mio)
                        barrier(k)
                    m_eo = emit_ag(k, eomy_d[:, :], eoag_d[:, :], GROUPS8)
                    with contextlib.ExitStack() as sctx:
                        cio = {
                            "xown": xown_fn, "eoall": eoag_d,
                            "ownidx": ownidx_d, "gidx_full": gidx_d, "keep2_full": keep2_d,
                            "gidx": None, "keep2": None,
                            "xo": lambda t: xmy_d[t * P:(t + 1) * P, :],
                            "after": base_after + (m_eo,),
                        }
                        emit_moeC(k, tag + "c_", sctx, cio)
                        barrier(k)

                moe_pass(lambda g: xfull_d[g * P:(g + 1) * P, :],
                         lambda t: xmy0_d[t * P:(t + 1) * P, :], "m1")
                if stages >= 2:
                    m_b1 = emit_ag(k, xmy_d[:, :], xagB_d[:, :], GROUPS24)
                if stages >= 2:
                  with contextlib.ExitStack() as sctx:
                    aio = dict(a0_io)
                    aio["kT_d"] = kT_d
                    aio["v_d"] = v_d
                    aio["xb"] = lambda t: xagB_d[_rm_batch(t):_rm_batch(t) + P, :]
                    aio["xown"] = lambda t: xmy_d[t * P:(t + 1) * P, :]
                    aio["xo"] = lambda t: xmy_d[t * P:(t + 1) * P, :]
                    aio["after"] = base_after + (m_b1,)
                    emit_attn0(k, "a0_", sctx, aio)
                    barrier(k)
                if stages >= 3:
                    m_f1 = emit_ag(k, xmy_d[:, :], xagF_d[:, :], GROUPS8)
                    moe_pass(lambda g: xagF_d[_rm_global(g):_rm_global(g) + P, :],
                             lambda t: xmy_d[t * P:(t + 1) * P, :], "m2", after=(m_f1,))

                if stages >= 4:
                  with contextlib.ExitStack() as sctx:
                    fio = {
                        "x": lambda t: xmy_d[t * P:(t + 1) * P, :],
                        "xo": lambda t: xmy_d[t * P:(t + 1) * P, :],
                        "wgq_t": lambda t: packag_d[(t % 8) * P:(t % 8 + 1) * P, (t // 8) * DM:(t // 8 + 1) * DM],
                        "wuq_t": lambda t: packag_d[(t % 8) * P:(t % 8 + 1) * P, (3 + t // 8) * DM:(4 + t // 8) * DM],
                        "wdq_t": lambda t: packag_d[(t % 8) * P:(t % 8 + 1) * P, 6 * DM:6 * DM + DI1],
                        "hq_d": hq_d,
                        "after": base_after + (m_wq,),
                    }
                    emit_ffn1(k, "f1_", sctx, fio)
                    barrier(k)
                if stages >= 5:
                  m_b2 = emit_ag(k, xmy_d[:, :], xagB_d[:, :], GROUPS24)
                  with contextlib.ExitStack() as sctx:
                    aio = dict(a1_io)
                    aio["xb"] = lambda t: xagB_d[_rm_batch(t):_rm_batch(t) + P, :]
                    aio["xown"] = lambda t: xmy_d[t * P:(t + 1) * P, :]
                    aio["xo"] = lambda t: xmy_d[t * P:(t + 1) * P, :]
                    aio["after"] = base_after + (m_b2,)
                    emit_attn1(k, "a1_", sctx, aio)
                    barrier(k)
                if stages >= 6:
                  with contextlib.ExitStack() as sctx:
                    fio = {
                        "x": lambda t: xmy_d[t * P:(t + 1) * P, :],
                        "xo": lambda t: xout_d[t * P:(t + 1) * P, :],
                        "wgq_t": lambda t: packag_d[(t % 8) * P:(t % 8 + 1) * P, (t // 8) * DM:(t // 8 + 1) * DM],
                        "wuq_t": lambda t: packag_d[(t % 8) * P:(t % 8 + 1) * P, (3 + t // 8) * DM:(4 + t // 8) * DM],
                        "wdq_t": lambda t: packag_d[(t % 8) * P:(t % 8 + 1) * P, 6 * DM:6 * DM + DI1],
                        "hq_d": hq_d,
                        "after": base_after,
                    }
                    emit_ffn1(k, "f2_", sctx, fio)
                if stages < 6:
                    barrier(k)
                    for t in range(TILES_PC):
                        k.sy.dma(k.sy.r.dma_start(out=xout_d[t * P:(t + 1) * P, :],
                                                  in_=xmy_d[t * P:(t + 1) * P, :]))
                k.sy.w(k.sy.dmark())
                k.sy.w(k.gp.dmark())
    return k.nc


def run_mega(inputs, stages=6):
    nc = _get("mega%d" % stages, lambda: nc_mega(stages))
    ins = inputs
    x = ins["L0_attn_x"].reshape(NTOK, DM)
    ident = np.eye(P, dtype=np.float32)
    triu = np.triu(np.ones((P, P), np.float32), 1)
    wg, wu, wd = ins["L1_ffn_w_gate"], ins["L1_ffn_w_up"], ins["L1_ffn_w_down"]
    in_maps = []
    for c in range(NCORES):
        b, r = c // 4, c % 4
        own = own_rows(c)
        gt = [min(c, 21), min(8 + c, 21), min(16 + c, 21)]
        im = {
            "xmy0": np.ascontiguousarray(x[own]),
            "xfull": x,
            "ident": ident,
            "ownidx": own.reshape(TPC, 1).astype(np.int32),
            "rw": ins["L0_ffn_router_w"],
            "w1": np.ascontiguousarray(ins["L0_ffn_w1"][c]),
            "b1": np.ascontiguousarray(ins["L0_ffn_b1"][c]),
            "w2": np.ascontiguousarray(ins["L0_ffn_w2"][c]),
            "b2": np.ascontiguousarray(ins["L0_ffn_b2"][c]),
            "triu": triu,
            "esel": np.tile(np.eye(NEXP, dtype=np.float32)[c:c + 1], (1, 4)),
            "e8": np.tile(np.arange(NEXP, dtype=np.float32).reshape(1, NEXP), (1, 4)),
            "wq": ins["L0_attn_Wq"], "wk": ins["L0_attn_Wk"], "wv": ins["L0_attn_Wv"],
            "wo": ins["L0_attn_Wo"], "gam": ins["L0_attn_gamma"], "bet": ins["L0_attn_beta"],
            "m16": build_mask16(r),
            "p_wg": np.ascontiguousarray(np.stack([wg[t * P:(t + 1) * P] for t in gt])),
            "p_wu": np.ascontiguousarray(np.stack([wu[t * P:(t + 1) * P] for t in gt])),
            "p_wd": np.ascontiguousarray(wd[c * P:(c + 1) * P][None]),
        }
        for nm in ("W_DQ", "b_DQ", "W_UQ", "b_UQ", "W_DKV", "b_DKV", "W_UK", "b_UK",
                   "W_UV", "b_UV", "W_QR", "b_QR", "W_KR", "b_KR", "W_O", "b_O"):
            im[nm.lower() if nm.startswith("b") else nm.lower()] = ins["L1_attn_" + nm]
        in_maps.append(im)
    res = run_bass_kernel_spmd(nc, in_maps, core_ids=list(range(NCORES)))
    out = np.empty((NTOK, DM), np.float32)
    for c in range(NCORES):
        out[own_rows(c)] = res.results[c]["xout"]
    return out.reshape(BATCH, SEQ, DM)


def kernel(**inputs):
    ins = {kk: np.ascontiguousarray(np.asarray(v)) for kk, v in inputs.items()}
    return run_mega(ins)


def kernel_unfused(**inputs):
    ins = {kk: np.ascontiguousarray(np.asarray(v)) for kk, v in inputs.items()}
    x = ins["L0_attn_x"].reshape(NTOK, DM).astype(np.float32, copy=True)
    ident = np.eye(P, dtype=np.float32)
    triu = np.triu(np.ones((P, P), np.float32), 1)
    W1 = {kk.replace("L1_attn_", ""): ins[kk] for kk in ins if kk.startswith("L1_attn_")}
    wgq, wuq, wdq = run_ffn1_prep(ins["L1_ffn_w_gate"], ins["L1_ffn_w_up"], ins["L1_ffn_w_down"])
    moe_args = (ins["L0_ffn_router_w"], ins["L0_ffn_w1"], ins["L0_ffn_b1"],
                ins["L0_ffn_w2"], ins["L0_ffn_b2"], ident, triu)
    x = run_moe(x, *moe_args)
    x = run_attn0(x, ins["L0_attn_Wq"], ins["L0_attn_Wk"], ins["L0_attn_Wv"],
                  ins["L0_attn_Wo"], ins["L0_attn_gamma"], ins["L0_attn_beta"], ident)
    x = run_moe(x, *moe_args)
    x = run_ffn1(x, wgq, wuq, wdq, ident)
    x = run_attn1(x, W1, ident)
    x = run_ffn1(x, wgq, wuq, wdq, ident)
    return x.reshape(BATCH, SEQ, DM)


# version 09
# speedup vs baseline: 1.3641x; 1.0052x over previous
import contextlib
import math
import numpy as np
import ml_dtypes
import concourse.bass as bass
import concourse.mybir as mybir
from concourse.bass import IndirectOffsetOnAxis
from concourse.bass_utils import run_bass_kernel_spmd

F32 = mybir.dt.float32
BF16 = mybir.dt.bfloat16
I32 = mybir.dt.int32
AF = mybir.ActivationFunctionType
AL = mybir.AluOpType
AX = mybir.AxisListType

P = 128
NCORES = 8
BATCH = 2
SEQ = 8192
DM = 1024
NTOK = BATCH * SEQ
TPC = NTOK // NCORES
DC = DM // P
TILES_PC = TPC // P

H0, E0 = 16, 64
LN_EPS0 = 1e-3
NEXP, CAP, DFF0 = 8, 3072, 4096
H1, DK1, DC1, DR1 = 8, 128, 32, 32
DI1, QB = 2816, 16
GLUE_EPS = 1e-5
BIGF = float(1 << 27)

NB_DM = DM // QB
NB_DI = DI1 // QB


class Eng:

    def __init__(self, raw, sem, dsem):
        self.r = raw
        self.sem = sem
        self.dsem = dsem
        self.cnt = 0
        self.dcnt = 0

    def op(self, inst):
        self.cnt += 1
        inst.then_inc(self.sem, 1)
        return (self.sem, self.cnt)

    def dma(self, inst):
        self.dcnt += 16
        inst.then_inc(self.dsem, 16)
        return (self.dsem, self.dcnt)

    def w(self, *ms):
        for m in ms:
            if m is not None:
                self.r.wait_ge(m[0], m[1])

    def mark(self):
        return (self.sem, self.cnt)

    def dmark(self):
        return (self.dsem, self.dcnt)


class K:
    def __init__(self, name):
        self.nc = bass.Bass("TRN2", target_bir_lowering=False, name=name)
        self.ctx = contextlib.ExitStack()

    def start(self):
        nc = self.nc
        self.gp = Eng(nc.gpsimd, self.ctx.enter_context(nc.semaphore("s_gp")),
                      self.ctx.enter_context(nc.semaphore("d_gp")))
        self.te = Eng(nc.tensor, self.ctx.enter_context(nc.semaphore("s_te")),
                      self.ctx.enter_context(nc.semaphore("d_te")))
        self.sc = Eng(nc.scalar, self.ctx.enter_context(nc.semaphore("s_sc")),
                      self.ctx.enter_context(nc.semaphore("d_sc")))
        self.ve = Eng(nc.vector, self.ctx.enter_context(nc.semaphore("s_ve")),
                      self.ctx.enter_context(nc.semaphore("d_ve")))
        self.sy = Eng(nc.sync, self.ctx.enter_context(nc.semaphore("s_sy")),
                      self.ctx.enter_context(nc.semaphore("d_sy")))
        self.ccsems = []

    def ccsem(self):
        s = self.ctx.enter_context(self.nc.semaphore(f"s_cc{len(self.ccsems)}"))
        self.ccsems.append(s)
        return s

    def sbuf(self, name, shape, dtype, ctx=None):
        return (ctx or self.ctx).enter_context(self.nc.sbuf_tensor(name, shape, dtype))

    def psum(self, name, shape, dtype=F32, ctx=None):
        return (ctx or self.ctx).enter_context(self.nc.psum_tensor(name, shape, dtype))



def e_ln(k, x_f32, xln_out, eps, n_feat, sq, st, after=()):
    ve, sc = k.ve, k.sc
    ve.w(*after)
    m_sum = ve.op(ve.r.reduce_sum(st[:, 0:1], x_f32, axis=AX.X))
    sc.w(*after)
    m_sq = sc.op(sc.r.activation(out=sq, in_=x_f32, func=AF.Square))
    ve.w(m_sq)
    m_ss = ve.op(ve.r.reduce_sum(st[:, 1:2], sq, axis=AX.X))
    ve.w(m_sum)
    m_mean = ve.op(ve.r.tensor_scalar_mul(st[:, 2:3], st[:, 0:1], 1.0 / n_feat))
    ve.w(m_ss)
    m_ex2 = ve.op(ve.r.tensor_scalar_mul(st[:, 3:4], st[:, 1:2], 1.0 / n_feat))
    ve.w(m_mean)
    m_m2 = ve.op(ve.r.tensor_tensor(st[:, 4:5], st[:, 2:3], st[:, 2:3], AL.mult))
    ve.w(m_ex2, m_m2)
    m_var = ve.op(ve.r.tensor_tensor(st[:, 5:6], st[:, 3:4], st[:, 4:5], AL.subtract))
    ve.w(m_var)
    m_vpe = ve.op(ve.r.tensor_single_scalar(st[:, 5:6], st[:, 5:6], float(eps), AL.add))
    sc.w(m_vpe)
    m_ln = sc.op(sc.r.activation(out=st[:, 6:7], in_=st[:, 5:6], func=AF.Ln))
    sc.w(m_ln)
    m_rstd = sc.op(sc.r.activation(out=st[:, 7:8], in_=st[:, 6:7], func=AF.Exp, scale=-0.5))
    ve.w(m_rstd)
    m_b2 = ve.op(ve.r.scalar_tensor_tensor(st[:, 4:5], st[:, 2:3], -1.0, st[:, 7:8], AL.mult, AL.mult))
    ve.w(m_b2)
    m_out = ve.op(ve.r.tensor_scalar(xln_out, x_f32, st[:, 7:8], st[:, 4:5], AL.mult, AL.add))
    return m_out


def e_quant(k, x_in, q_out, nb, t, after=()):
    ve, sc = k.ve, k.sc
    t = {kk: (v if isinstance(v, bass.AP) else v[:]) for kk, v in t.items()}
    xv = x_in.rearrange("p (nb b) -> p nb b", b=QB)
    ve.w(*after)
    m_am = ve.op(ve.r.tensor_reduce(t["am"], xv, axis=AX.X, op=AL.max, apply_absolute_value=True))
    ve.w(m_am)
    m_sc = ve.op(ve.r.tensor_scalar(t["scs"], t["am"], 1.0 / 6.0, 1e-12, AL.mult, AL.max))
    ve.w(m_sc)
    m_rc = ve.op(ve.r.reciprocal(t["rcs"], t["scs"]))
    nbv = t["am"].shape[1]
    rb = t["rcs"].unsqueeze(2).broadcast_to([t["rcs"].shape[0], nbv, QB])
    sb = t["scs"].unsqueeze(2).broadcast_to([t["scs"].shape[0], nbv, QB])
    ve.w(m_rc)
    m_xn = ve.op(ve.r.tensor_tensor(t["xn"].rearrange("p (nb b) -> p nb b", b=QB), xv, rb, AL.mult))
    sc.w(m_xn)
    m_sg = sc.op(sc.r.sign(out=t["sgn"], in_=t["xn"]))
    ve.w(m_xn)
    m_a = ve.op(ve.r.scalar_tensor_tensor(t["a"], t["xn"], -1.0, t["xn"], AL.mult, AL.max))
    ve.w(m_a)
    m = ve.op(ve.r.tensor_single_scalar(t["acc"], t["a"], 0.25, AL.is_gt))
    for th in (0.75, 1.25, 1.75):
        ve.w(m)
        m = ve.op(ve.r.scalar_tensor_tensor(t["acc"], t["a"], float(th), t["acc"], AL.is_gt, AL.add))
    m2 = ve.op(ve.r.tensor_single_scalar(t["acc2"], t["a"], 2.5, AL.is_gt))
    for th in (3.5, 5.0, 5.0):
        ve.w(m2)
        m2 = ve.op(ve.r.scalar_tensor_tensor(t["acc2"], t["a"], float(th), t["acc2"], AL.is_gt, AL.add))
    ve.w(m, m2)
    mq = ve.op(ve.r.scalar_tensor_tensor(t["a"], t["acc"], 0.5, t["acc2"], AL.mult, AL.add))
    ve.w(mq, m_sg)
    mqs = ve.op(ve.r.tensor_tensor(t["a"], t["a"], t["sgn"], AL.mult))
    ve.w(mqs)
    m_out = ve.op(ve.r.tensor_tensor(q_out.rearrange("p (nb b) -> p nb b", b=QB),
                                     t["a"].rearrange("p (nb b) -> p nb b", b=QB), sb, AL.mult))
    return m_out


def e_transp(k, src_bf16, dst3, n_chunks, ps2, ident, after=()):
    te, sc = k.te, k.sc
    m_ev = [sc.mark(), sc.mark()]
    m_last = None
    for g0 in range(0, n_chunks, 4):
        ng = min(4, n_chunks - g0)
        bk = (g0 // 4) % 2
        ps = ps2[bk]
        te.w(*after)
        te.w(m_ev[bk])
        for j in range(ng):
            m_mm = te.op(te.r.matmul(ps[:, j * P:(j + 1) * P], src_bf16[:, (g0 + j) * P:(g0 + j + 1) * P],
                                     ident[:], start=True, stop=True))
        sc.w(m_mm)
        m_ev[bk] = sc.op(sc.r.activation(
            out=dst3[:, g0:g0 + ng, :],
            in_=ps[:, 0:ng * P].rearrange("p (g r) -> p g r", r=P), func=AF.Copy))
        m_last = m_ev[bk]
    return m_last


def host_consts():
    ident = np.eye(P, dtype=np.float32)
    triu = np.triu(np.ones((P, P), np.float32), 1)
    masks = np.zeros((4, P, 512), np.float32)
    kk = np.arange(P)[:, None]
    qq = np.arange(512)[None, :]
    for rel in range(4):
        masks[rel] = (qq >= rel * P + kk).astype(np.float32)
    e8 = np.tile(np.arange(8, dtype=np.float32)[None, :], (P, 1))
    return ident, triu, masks, e8



def nc_ffn1_prep():
    RG = DI1 // NCORES
    RD = DM // NCORES
    k = K("ffn1prep")
    nc = k.nc
    wg = nc.dram_tensor("wg", [RG, DM], F32, kind="ExternalInput").ap()
    wu = nc.dram_tensor("wu", [RG, DM], F32, kind="ExternalInput").ap()
    wd = nc.dram_tensor("wd", [RD, DI1], F32, kind="ExternalInput").ap()
    wgq = nc.dram_tensor("wgq", [RG, DM], BF16, kind="ExternalOutput").ap()
    wuq = nc.dram_tensor("wuq", [RG, DM], BF16, kind="ExternalOutput").ap()
    wdq = nc.dram_tensor("wdq", [RD, DI1], BF16, kind="ExternalOutput").ap()
    with k.ctx:
        k.start()
        xs = k.sbuf("xs", [P, DI1], F32)
        qo = k.sbuf("qo", [P, DI1], BF16)
        t = {
            "xn": k.sbuf("t_xn", [P, DI1], F32),
            "a": k.sbuf("t_a", [P, DI1], F32),
            "sgn": k.sbuf("t_sgn", [P, DI1], BF16),
            "acc": k.sbuf("t_acc", [P, DI1], BF16),
            "acc2": k.sbuf("t_acc2", [P, DI1], BF16),
            "am": k.sbuf("t_am", [P, NB_DI], F32),
            "scs": k.sbuf("t_scs", [P, NB_DI], F32),
            "rcs": k.sbuf("t_rcs", [P, NB_DI], F32),
        }
        with nc.Block() as block:
            @block.sync
            def _(_h):
                sy, ve, sc = k.sy, k.ve, k.sc
                jobs = []
                for (src, dst, rows, width, nb) in ((wg, wgq, RG, DM, NB_DM),
                                                    (wu, wuq, RG, DM, NB_DM),
                                                    (wd, wdq, RD, DI1, NB_DI)):
                    r = 0
                    while r < rows:
                        jobs.append((src, dst, r, min(P, rows - r), width, nb))
                        r += P
                for (src, dst, r0, nr, width, nb) in jobs:
                    sy.w(ve.mark())
                    m_ld = sy.dma(sy.r.dma_start(out=xs[0:nr, 0:width], in_=src[r0:r0 + nr, :]))
                    ve.w(sy.dmark())
                    sc.w(sy.dmark())
                    tt = {kk2: (v[0:nr, 0:nb] if kk2 in ("am", "scs", "rcs") else v[0:nr, 0:width])
                          for kk2, v in t.items()}
                    m_q = e_quant(k, xs[0:nr, 0:width], qo[0:nr, 0:width], nb, tt, after=(m_ld,))
                    sy.w(m_q)
                    m_store = sy.dma(sy.r.dma_start(out=dst[r0:r0 + nr, :], in_=qo[0:nr, 0:width]))
                sy.w(sy.dmark())
    return k.nc



def emit_ffn1(k, pfx, sctx, io):
    nc = k.nc
    IC = DI1 // P
    ident = k.ident
    wgT = k.sbuf(pfx + "wgT", [P, DC * DI1], BF16, ctx=sctx)
    wuT = k.sbuf(pfx + "wuT", [P, DC * DI1], BF16, ctx=sctx)
    wdT = k.sbuf(pfx + "wdT", [P, IC * DM], BF16, ctx=sctx)
    x_s = k.sbuf(pfx + "x_s", [P, DM], F32, ctx=sctx)
    xln = k.sbuf(pfx + "xln", [P, DM], F32, ctx=sctx)
    xq = k.sbuf(pfx + "xq", [P, DM], BF16, ctx=sctx)
    xqT = k.sbuf(pfx + "xqT", [P, DM], BF16, ctx=sctx)
    g_s = k.sbuf(pfx + "g_s", [P, DI1], F32, ctx=sctx)
    u_s = k.sbuf(pfx + "u_s", [P, DI1], F32, ctx=sctx)
    h_s = k.sbuf(pfx + "h_s", [P, DI1], F32, ctx=sctx)
    hq = k.sbuf(pfx + "hq", [P, DI1], BF16, ctx=sctx)
    hqT = k.sbuf(pfx + "hqT", [P, DI1], BF16, ctx=sctx)
    wtile = hq
    stat = k.sbuf(pfx + "stat", [P, 8], F32, ctx=sctx)
    tq = {
        "xn": g_s, "a": u_s,
        "sgn": hq,
        "acc": k.sbuf(pfx + "t_acc", [P, DI1], BF16, ctx=sctx),
        "acc2": hqT,
        "am": k.sbuf(pfx + "t_am", [P, NB_DI], F32, ctx=sctx),
        "scs": k.sbuf(pfx + "t_scs", [P, NB_DI], F32, ctx=sctx),
        "rcs": k.sbuf(pfx + "t_rcs", [P, NB_DI], F32, ctx=sctx),
    }
    ps_t2 = [k.psum(pfx + "ps_ta", [P, 512], ctx=sctx), k.psum(pfx + "ps_tb", [P, 512], ctx=sctx)]
    ps_g = k.psum(pfx + "ps_g", [P, 512], ctx=sctx)
    ps_u = k.psum(pfx + "ps_u", [P, 512], ctx=sctx)
    hq_d = io["hq_d"]
    after = tuple(io.get("after", ()))

    gp, te, sc, ve, sy = k.gp, k.te, k.sc, k.ve, k.sy
    for e in (gp, te, sc, ve, sy):
        e.w(*after)

    def load_T(tile_fn, dstT, rows, width):
        nblk = rows // P
        m_fin = None
        dst3_full = dstT[:].rearrange("p (c r) -> p c r", r=rows)
        for tix in range(nblk):
            sy.w(k.te.mark())
            m_ld = sy.dma(sy.r.dma_start(out=wtile[:, 0:width], in_=tile_fn(tix)))
            m_fin = e_transp(k, wtile[:, 0:width], dst3_full[:, :, tix * P:(tix + 1) * P],
                             width // P, ps_t2, ident, after=(m_ld,))
        return m_fin

    m_wg = load_T(io["wgq_t"], wgT, DI1, DM)
    m_wu = load_T(io["wuq_t"], wuT, DI1, DM)
    m_wd = load_T(io["wdq_t"], wdT, DM, DI1)

    NS = [512, 512, 512, 512, 512, 256]
    m_gev = m_uev = None
    for tix in range(TILES_PC):
        sy.w(ve.mark())
        sy.w(sc.mark())
        m_prev_ve = ve.mark()
        m_x = sy.dma(sy.r.dma_start(out=x_s[:], in_=io["x"](tix)))
        ve.w(te.mark())
        ve.w(sy.dmark())
        sc.w(sy.dmark())
        m_ln = e_ln(k, x_s[:], xln[:], GLUE_EPS, DM, sq=h_s[:, 0:DM], st=stat,
                    after=(m_x, m_prev_ve))
        tqx = {kk2: (v[:, 0:NB_DM] if kk2 in ("am", "scs", "rcs") else v[:, 0:DM])
               for kk2, v in tq.items()}
        tqx["xn"] = g_s[:, 0:DM]
        tqx["a"] = u_s[:, 0:DM]
        m_q = e_quant(k, xln[:], xq[:], NB_DM, tqx, after=(m_ln,))
        m_xT = e_transp(k, xq[:], xqT[:].rearrange("p (c r) -> p c r", r=P),
                        DC, ps_t2, ident, after=(m_q, m_wg, m_wu, m_wd))
        n0 = 0
        for ni, nsz in enumerate(NS):
            te.w(m_xT)
            if m_gev is not None:
                te.w(m_gev)
            for d in range(DC):
                mm_g = te.op(te.r.matmul(ps_g[:, 0:nsz], xqT[:, d * P:(d + 1) * P],
                                         wgT[:, d * DI1 + n0: d * DI1 + n0 + nsz],
                                         start=(d == 0), stop=(d == DC - 1)))
            sc.w(mm_g)
            sc.w(ve.mark() if ni == 0 else None)
            m_gev = sc.op(sc.r.activation(out=g_s[:, n0:n0 + nsz], in_=ps_g[:, 0:nsz], func=AF.Silu))
            if m_uev is not None:
                te.w(m_uev)
            for d in range(DC):
                mm_u = te.op(te.r.matmul(ps_u[:, 0:nsz], xqT[:, d * P:(d + 1) * P],
                                         wuT[:, d * DI1 + n0: d * DI1 + n0 + nsz],
                                         start=(d == 0), stop=(d == DC - 1)))
            sc.w(mm_u)
            m_uev = sc.op(sc.r.activation(out=u_s[:, n0:n0 + nsz], in_=ps_u[:, 0:nsz], func=AF.Copy))
            n0 += nsz
        ve.w(m_gev, m_uev)
        m_h = ve.op(ve.r.tensor_tensor(h_s[:], g_s[:], u_s[:], AL.mult))
        m_hq = e_quant(k, h_s[:], hq[:], NB_DI, tq, after=(m_h,))
        sy.w(m_hq)
        sy.dma(sy.r.dma_start(out=hq_d[tix * P:(tix + 1) * P, :], in_=hq[:]))

    m_y = None
    for tix in range(TILES_PC):
        te.w(sc.mark())
        sy.w(ve.mark())
        sy.w(te.mark())
        sy.w(sy.dmark())
        sy.dma(sy.r.dma_start(out=hq[:], in_=hq_d[tix * P:(tix + 1) * P, :]))
        sy.dma(sy.r.dma_start(out=x_s[:], in_=io["x"](tix)))
        m_x = m_hl = sy.dmark()
        m_hT = e_transp(k, hq[:], hqT[:].rearrange("p (c r) -> p c r", r=P),
                        IC, ps_t2, ident, after=(m_hl,))
        for n2 in range(2):
            te.w(m_hT)
            if m_y is not None:
                te.w(m_y)
            for i in range(IC):
                mm_d = te.op(te.r.matmul(ps_g[:, 0:512], hqT[:, i * P:(i + 1) * P],
                                         wdT[:, i * DM + n2 * 512: i * DM + (n2 + 1) * 512],
                                         start=(i == 0), stop=(i == IC - 1)))
            sc.w(mm_d)
            sc.w(ve.mark() if tix > 0 and n2 == 0 else None)
            m_y = sc.op(sc.r.activation(out=g_s[:, n2 * 512:(n2 + 1) * 512], in_=ps_g[:, 0:512],
                                        func=AF.Copy, scale=0.5))
        ve.w(m_y, m_x)
        ve.w(sy.dmark())
        m_o = ve.op(ve.r.tensor_tensor(u_s[:, 0:DM], g_s[:, 0:DM], x_s[:], AL.add))
        sy.w(m_o)
        sy.dma(sy.r.dma_start(out=io["xo"](tix), in_=u_s[:, 0:DM]))


def nc_ffn1():
    k = K("ffn1")
    nc = k.nc
    xs_d = nc.dram_tensor("xs", [TPC, DM], F32, kind="ExternalInput").ap()
    wgq_d = nc.dram_tensor("wgq", [DI1, DM], BF16, kind="ExternalInput").ap()
    wuq_d = nc.dram_tensor("wuq", [DI1, DM], BF16, kind="ExternalInput").ap()
    wdq_d = nc.dram_tensor("wdq", [DM, DI1], BF16, kind="ExternalInput").ap()
    ident_d = nc.dram_tensor("ident", [P, P], F32, kind="ExternalInput").ap()
    xo_d = nc.dram_tensor("xo", [TPC, DM], F32, kind="ExternalOutput").ap()
    hq_d = nc.dram_tensor("hq_scr", [TPC, DI1], BF16).ap()
    with k.ctx:
        k.start()
        k.ident = k.sbuf("ident_s", [P, P], BF16)
        with nc.Block() as block:
            @block.sync
            def _(_h):
                m_id = k.gp.dma(k.gp.r.dma_start(out=k.ident[:], in_=ident_d[:, :]))
                io = {
                    "x": lambda t: xs_d[t * P:(t + 1) * P, :],
                    "xo": lambda t: xo_d[t * P:(t + 1) * P, :],
                    "wgq_t": lambda t: wgq_d[t * P:(t + 1) * P, :],
                    "wuq_t": lambda t: wuq_d[t * P:(t + 1) * P, :],
                    "wdq_t": lambda t: wdq_d[t * P:(t + 1) * P, :],
                    "hq_d": hq_d,
                    "after": (m_id,),
                }
                emit_ffn1(k, "f_", k.ctx, io)
                k.sy.w(k.sy.dmark())
    return k.nc


_cache = {}


def _get(name, builder):
    if name not in _cache:
        _cache[name] = builder()
    return _cache[name]


def emit_attn1(k, pfx, sctx, io):
    nc = k.nc
    NKT = SEQ // P
    s160 = 1.0 / math.sqrt(DK1 + DR1)
    gp, te, sc, ve, sy = k.gp, k.te, k.sc, k.ve, k.sy
    for _e in (gp, te, sc, ve, sy):
        _e.w(*io.get('after', ()))
    ident = k.ident
    ones1f = k.ones1f
    ones1b = k.ones1b
    ones_col = k.ones_col
    kvlat = k.sbuf(pfx + "kvlat", [P, SEQ], BF16, ctx=sctx)
    cqT = k.sbuf(pfx + "cqT", [P, TPC], BF16, ctx=sctx)
    kcT_h = k.sbuf(pfx + "kcT_h", [P, SEQ], BF16, ctx=sctx)
    v_h = k.sbuf(pfx + "v_h", [P, SEQ], BF16, ctx=sctx)
    qcT = k.sbuf(pfx + "qcT", [P, DC * TPC], BF16, ctx=sctx)
    qrT = k.sbuf(pfx + "qrT", [P, 2 * TPC], BF16, ctx=sctx)
    o_all = k.sbuf(pfx + "o_all", [P, DC * TPC], BF16, ctx=sctx)
    at = [k.sbuf(pfx + "at%d" % i, [P, 512], BF16, ctx=sctx) for i in range(4)]
    x_s = k.sbuf(pfx + "x_s", [P, DM], F32, ctx=sctx)
    sq_s = k.sbuf(pfx + "sq_s", [P, 2 * DM], F32, ctx=sctx)
    xln = k.sbuf(pfx + "xln", [P, DM], BF16, ctx=sctx)
    xlnT = k.sbuf(pfx + "xlnT", [P, DM], BF16, ctx=sctx)
    stat = k.sbuf(pfx + "stat", [P, 8], F32, ctx=sctx)
    wdq_s = k.sbuf(pfx + "wdq_s", [P, DC * DC1], BF16, ctx=sctx)
    wdkv_s = k.sbuf(pfx + "wdkv_s", [P, DC * DC1], BF16, ctx=sctx)
    wkr_s = k.sbuf(pfx + "wkr_s", [P, DC * DR1], BF16, ctx=sctx)
    wuq_s = k.sbuf(pfx + "wuq_s", [P, DM], BF16, ctx=sctx)
    wuk_s = k.sbuf(pfx + "wuk_s", [P, DM], BF16, ctx=sctx)
    wuv_s = k.sbuf(pfx + "wuv_s", [P, DM], BF16, ctx=sctx)
    wqr_s = k.sbuf(pfx + "wqr_s", [P, 256], BF16, ctx=sctx)
    wo_s = k.sbuf(pfx + "wo_s", [P, DC * DM], BF16, ctx=sctx)
    bo_row = k.sbuf(pfx + "bo_row", [1, DM], BF16, ctx=sctx)
    bdq_s = k.sbuf(pfx + "bdq_s", [P, 2], F32, ctx=sctx)
    bkr_s = k.sbuf(pfx + "bkr_s", [P, 1], F32, ctx=sctx)
    kr2 = k.sbuf(pfx + "kr2", [1, SEQ], BF16, ctx=sctx)
    qn2 = k.sbuf(pfx + "qn2", [1, TPC], BF16, ctx=sctx)
    mrow = k.sbuf(pfx + "mrow", [1, TPC], BF16, ctx=sctx)
    mred = k.sbuf(pfx + "mred", [1, 8], BF16, ctx=sctx)
    kmx = k.sbuf(pfx + "kmx", [1, 18], F32, ctx=sctx)
    kmax2 = k.sbuf(pfx + "kmax2", [1, 1], F32, ctx=sctx)
    negm_t = k.sbuf(pfx + "negm_t", [P, 32], F32, ctx=sctx)
    sqk = k.sbuf(pfx + "sqk", [P, 1024], BF16, ctx=sctx)
    qr_h = k.sbuf(pfx + "qr_h", [P, TPC], BF16, ctx=sctx)
    lrow = sq_s[0:1, 1024:1536]
    rl_s = sq_s[0:1, 1536:2048]
    rlb_s = sq_s[:, 512:1024]
    ps_t2 = [k.psum(pfx + "ps_ta", [P, 512], ctx=sctx), k.psum(pfx + "ps_tb", [P, 512], ctx=sctx)]
    ps_s2 = [k.psum(pfx + "ps_sa", [P, 512], ctx=sctx), k.psum(pfx + "ps_sb", [P, 512], ctx=sctx)]
    ps_o = k.psum(pfx + "ps_o", [P, 512], ctx=sctx)
    ps_l = k.psum(pfx + "ps_l", [P, 512], ctx=sctx)
    ps_s4 = ps_s2 + [k.psum(pfx + "ps_sc", [P, 512], ctx=sctx),
                     k.psum(pfx + "ps_sd", [P, 512], ctx=sctx)]

    gp.dma(gp.r.dma_start(out=wuq_s[0:DC1, :], in_=io["w_uq"][:, :]))
    gp.dma(gp.r.dma_start(out=wuq_s[DC1:DC1 + 1, :], in_=io["b_uq"][:].unsqueeze(0)))
    gp.dma(gp.r.dma_start(out=wuk_s[0:DC1, :], in_=io["w_uk"][:, :]))
    gp.dma(gp.r.dma_start(out=wuk_s[DC1:DC1 + 1, :], in_=io["b_uk"][:].unsqueeze(0)))
    gp.dma(gp.r.dma_start(out=wuv_s[0:DC1, :], in_=io["w_uv"][:, :]))
    gp.dma(gp.r.dma_start(out=wuv_s[DC1:DC1 + 1, :], in_=io["b_uv"][:].unsqueeze(0)))
    gp.dma(gp.r.dma_start(out=wqr_s[0:DC1, :], in_=io["w_qr"][:, :]))
    gp.dma(gp.r.dma_start(out=wqr_s[DC1:DC1 + 1, :], in_=io["b_qr"][:].unsqueeze(0)))
    gp.dma(gp.r.dma_start(out=bo_row[:], in_=io["b_o"][:].unsqueeze(0)))
    for d in range(DC):
        gp.dma(gp.r.dma_start(out=wdq_s[:, d * DC1:(d + 1) * DC1], in_=io["w_dq"][d * P:(d + 1) * P, :]))
        gp.dma(gp.r.dma_start(out=wdkv_s[:, d * DC1:(d + 1) * DC1], in_=io["w_dkv"][d * P:(d + 1) * P, :]))
        gp.dma(gp.r.dma_start(out=wkr_s[:, d * DR1:(d + 1) * DR1], in_=io["w_kr"][d * P:(d + 1) * P, :]))
        gp.dma(gp.r.dma_start(out=wo_s[:, d * DM:(d + 1) * DM], in_=io["w_o"][d * P:(d + 1) * P, :]))
    gp.dma(gp.r.dma_start(out=bdq_s[0:DC1, 0:1], in_=io["b_dq"][:].unsqueeze(1)))
    gp.dma(gp.r.dma_start(out=bdq_s[0:DC1, 1:2], in_=io["b_dkv"][:].unsqueeze(1)))
    gp.dma(gp.r.dma_start(out=bkr_s[0:DR1, 0:1], in_=io["b_kr"][:].unsqueeze(1)))
    m_wload = gp.dmark()
    ve.op(ve.r.memset(kvlat[DC1:DC1 + 1, :], 1.0))
    m_ones = ve.op(ve.r.memset(cqT[DC1:DC1 + 1, :], 1.0))

    xlnT3 = xlnT[:].rearrange("p (c r) -> p c r", r=P)

    def lat_tile(src_ap, out_desc):
        sy.w(ve.mark())
        sy.w(sc.mark())
        m_x = sy.dma(sy.r.dma_start(out=x_s[:], in_=src_ap))
        ve.w(te.mark())
        m_ln = e_ln(k, x_s[:], xln[:], GLUE_EPS, DM, sq=sq_s[:, 0:DM], st=stat,
                    after=(m_x,))
        m_tr = e_transp(k, xln[:], xlnT3, DC, ps_t2, ident, after=(m_ln, m_wload))
        for (w_sb, wstep, bias_ap, dst_ap, bank) in out_desc:
            te.w(m_tr)
            te.w(sc.mark())
            for d in range(DC):
                mm = te.op(te.r.matmul(bank[0:DC1, 0:P],
                                       w_sb[:, d * wstep:(d + 1) * wstep],
                                       xlnT3[:, d, :],
                                       start=(d == 0), stop=(d == DC - 1)))
            sc.w(mm)
            sc.op(sc.r.activation(out=dst_ap, in_=bank[0:DC1, 0:P],
                                  func=AF.Identity, bias=bias_ap))

    for t in range(NKT):
        lat_tile(io["xb"](t),
                 [(wdkv_s, DC1, bdq_s[0:DC1, 1:2], kvlat[0:DC1, t * P:(t + 1) * P], ps_s2[0]),
                  (wkr_s, DR1, bkr_s[0:DR1, 0:1], kvlat[64:64 + DR1, t * P:(t + 1) * P], ps_s2[1])])
    for t in range(TILES_PC):
        lat_tile(io["xown"](t),
                 [(wdq_s, DC1, bdq_s[0:DC1, 0:1], cqT[0:DC1, t * P:(t + 1) * P], ps_s2[0])])

    te.w(ve.mark())
    m_ev4 = [sc.mark()] * 4
    jq = 0
    for m in range(DC):
        for n in range(4):
            sl = jq % 4
            jq += 1
            te.w(m_ev4[sl])
            mm = te.op(te.r.matmul(ps_s4[sl][:, 0:512],
                                   wuq_s[0:DC1 + 1, m * P:(m + 1) * P],
                                   cqT[0:DC1 + 1, n * 512:(n + 1) * 512],
                                   start=True, stop=True))
            sc.w(mm)
            m_ev4[sl] = sc.op(sc.r.activation(
                out=qcT[:, m * TPC + n * 512: m * TPC + (n + 1) * 512],
                in_=ps_s4[sl][:, 0:512], func=AF.Copy, scale=s160))
    for m in range(2):
        for n in range(4):
            sl = jq % 4
            jq += 1
            te.w(m_ev4[sl])
            mm = te.op(te.r.matmul(ps_s4[sl][:, 0:512],
                                   wqr_s[0:DC1 + 1, m * P:(m + 1) * P],
                                   cqT[0:DC1 + 1, n * 512:(n + 1) * 512],
                                   start=True, stop=True))
            sc.w(mm)
            m_ev4[sl] = sc.op(sc.r.activation(
                out=qrT[:, m * TPC + n * 512: m * TPC + (n + 1) * 512],
                in_=ps_s4[sl][:, 0:512], func=AF.Copy, scale=s160))
    m_prev_ev = m_ev4[(jq - 1) % 4]

    m_prev_lv = sc.mark()
    for n in range(16):
        sc.w(te.mark())
        m_sqk = sc.op(sc.r.activation(out=sqk[0:DR1, 0:512],
                                      in_=kvlat[64:64 + DR1, n * 512:(n + 1) * 512],
                                      func=AF.Square))
        te.w(m_sqk, m_prev_lv)
        mm = te.op(te.r.matmul(ps_l[0:1, 0:512], ones_col[0:DR1, :],
                               sqk[0:DR1, 0:512], start=True, stop=True))
        sc.w(mm)
        m_prev_lv = sc.op(sc.r.activation(out=kr2[0:1, n * 512:(n + 1) * 512],
                                          in_=ps_l[0:1, 0:512], func=AF.Copy))

    for h in range(H1):
        qr_pb = (h % 4) * DR1
        qr_ch = (h // 4) * TPC
        gp.r.wait_ge(k.te.sem, k.te.cnt)
        m_qrh = gp.dma(gp.r.dma_start(out=qr_h[64:64 + DR1, :],
                                      in_=qrT[qr_pb:qr_pb + DR1, qr_ch:qr_ch + TPC]))
        m_kc = None
        for n in range(16):
            te.w(sc.mark())
            mm = te.op(te.r.matmul(ps_s2[n % 2][:, 0:512],
                                   wuk_s[0:DC1 + 1, h * P:(h + 1) * P],
                                   kvlat[0:DC1 + 1, n * 512:(n + 1) * 512],
                                   start=True, stop=True))
            sc.w(mm)
            m_kc = sc.op(sc.r.activation(out=kcT_h[:, n * 512:(n + 1) * 512],
                                         in_=ps_s2[n % 2][:, 0:512], func=AF.Copy))
        m_vev = [sc.mark(), sc.mark()]
        for g0 in range(0, NKT, 4):
            bk = (g0 // 4) % 2
            te.w(m_vev[bk])
            for j in range(4):
                t = g0 + j
                mm = te.op(te.r.matmul(ps_t2[bk][:, j * P:(j + 1) * P],
                                       kvlat[0:DC1 + 1, t * P:(t + 1) * P],
                                       wuv_s[0:DC1 + 1, h * P:(h + 1) * P],
                                       start=True, stop=True))
            sc.w(mm)
            m_vev[bk] = sc.op(sc.r.activation(
                out=v_h[:].rearrange("p (c r) -> p c r", r=P)[:, g0:g0 + 4, :],
                in_=ps_t2[bk][:, 0:512].rearrange("p (g r) -> p g r", r=P), func=AF.Copy))
        m_vh = m_vev

        m_pl = sc.mark()
        m_mx = ve.mark()
        for n in range(16):
            sc.w(m_kc)
            sc.w(te.mark())
            m_sq2 = sc.op(sc.r.activation(out=sqk[:, 0:512],
                                          in_=kcT_h[:, n * 512:(n + 1) * 512], func=AF.Square))
            te.w(m_sq2, m_pl, m_mx)
            mm = te.op(te.r.matmul(ps_l[0:1, 0:512], ones_col[:, :],
                                   sqk[:, 0:512], start=True, stop=True))
            sc.w(mm)
            m_pl = sc.op(sc.r.activation(out=lrow, in_=ps_l[0:1, 0:512], func=AF.Copy))
            ve.w(m_pl)
            m_ad = ve.op(ve.r.tensor_tensor(lrow, lrow,
                                            kr2[0:1, n * 512:(n + 1) * 512], AL.add))
            ve.w(m_ad)
            m_mx = ve.op(ve.r.reduce_max(kmx[0:1, n:n + 1], lrow, axis=AX.X))
        ve.w(m_mx)
        m_km = ve.op(ve.r.reduce_max(kmax2[0:1, 0:1], kmx[0:1, 0:16], axis=AX.X))
        m_pq = sc.mark()
        for n in range(4):
            sc.w(te.mark())
            m_sq3 = sc.op(sc.r.activation(out=sqk[:, 0:512],
                                          in_=qcT[:, h * TPC + n * 512: h * TPC + (n + 1) * 512],
                                          func=AF.Square))
            sc.w(m_qrh)
            m_sq4 = sc.op(sc.r.activation(out=sqk[0:DR1, 512:1024],
                                          in_=qr_h[64:64 + DR1, n * 512:(n + 1) * 512],
                                          func=AF.Square))
            te.w(m_sq3, m_sq4, m_pq)
            te.op(te.r.matmul(ps_l[0:1, 0:512], ones_col[:, :],
                              sqk[:, 0:512], start=True, stop=False))
            mm = te.op(te.r.matmul(ps_l[0:1, 0:512], ones_col[0:DR1, :],
                                   sqk[0:DR1, 512:1024],
                                   start=False, stop=True))
            sc.w(mm)
            m_pq = sc.op(sc.r.activation(out=qn2[0:1, n * 512:(n + 1) * 512],
                                         in_=ps_l[0:1, 0:512], func=AF.Copy))
        ve.w(m_pq, m_km)
        m_pr = ve.op(ve.r.tensor_scalar_mul(mrow[0:1, :], qn2[0:1, :], kmax2[0:1, 0:1]))
        sc.w(m_pr)
        m_sqr = sc.op(sc.r.activation(out=mrow[0:1, :], in_=mrow[0:1, :], func=AF.Sqrt,
                                      scale=1.0404))
        ve.w(m_sqr)
        m_red = ve.op(ve.r.reduce_max(mred[0:1, 0:4], mrow[0:1, :].rearrange("o (g q) -> o g q", q=512),
                                      axis=AX.X))
        ve.w(m_red)
        m_neg = ve.op(ve.r.tensor_scalar_mul(mred[0:1, 4:8], mred[0:1, 0:4], -1.0))
        te.w(m_neg)
        te.w(sc.mark())
        mmb = te.op(te.r.matmul(ps_l[0:P, 0:4], ones1b[:, :], mred[0:1, 4:8],
                                start=True, stop=True))
        sc.w(mmb)
        m_negt = sc.op(sc.r.activation(out=negm_t[:, h * 4:(h + 1) * 4], in_=ps_l[0:P, 0:4],
                                       func=AF.Copy))

        for qg in range(4):
            te.w(sc.mark())
            m_at = [te.mark()] * 4
            m_exp = [sc.mark()] * 4
            def av_pair(kb):
                slx = kb % 4
                te.w(m_exp[slx])
                te.op(te.r.matmul(ps_o[:, 0:512], v_h[:, kb * P:(kb + 1) * P], at[slx][:],
                                  start=(kb == 0), stop=(kb == NKT - 1)))
                m_at[slx] = te.op(te.r.matmul(ps_l[0:1, 0:512], ones_col[:, :], at[slx][:],
                                              start=(kb == 0), stop=(kb == NKT - 1)))
            for kb in range(NKT):
                sl = kb % 4
                te.w(m_exp[sl])
                mm1 = te.op(te.r.matmul(ps_s4[sl][:, 0:512],
                                        kcT_h[:, kb * P:(kb + 1) * P],
                                        qcT[:, h * TPC + qg * 512: h * TPC + (qg + 1) * 512],
                                        start=True, stop=False))
                te.w(m_qrh)
                mm2 = te.op(te.r.matmul(ps_s4[sl][:, 0:512],
                                        kvlat[64:64 + DR1, kb * P:(kb + 1) * P],
                                        qr_h[64:64 + DR1, qg * 512:(qg + 1) * 512],
                                        start=False, stop=True))
                sc.w(mm2)
                sc.w(m_at[sl])
                sc.w(m_negt)
                m_exp[sl] = sc.op(sc.r.activation(out=at[sl][:], in_=ps_s4[sl][:, 0:512],
                                                  func=AF.Exp,
                                                  bias=negm_t[:, h * 4 + qg: h * 4 + qg + 1]))
                if kb >= 2:
                    av_pair(kb - 2)
            av_pair(NKT - 2)
            av_pair(NKT - 1)
            m_avl = m_at[(NKT - 1) % 4]
            sc.w(m_avl)
            m_oev = sc.op(sc.r.activation(out=sq_s[:, 0:512], in_=ps_o[:, 0:512], func=AF.Copy))
            m_lev = sc.op(sc.r.activation(out=lrow, in_=ps_l[0:1, 0:512], func=AF.Copy))
            ve.w(m_lev)
            m_rl = ve.op(ve.r.reciprocal(rl_s, lrow))
            te.w(m_rl)
            te.w(m_oev, m_lev)
            mmb2 = te.op(te.r.matmul(ps_l[0:P, 0:512], ones1f[:, :], rl_s,
                                     start=True, stop=True))
            sc.w(mmb2)
            m_rlb = sc.op(sc.r.activation(out=rlb_s, in_=ps_l[0:P, 0:512], func=AF.Copy))
            ve.w(m_rlb, m_oev)
            m_on = ve.op(ve.r.tensor_tensor(
                o_all[:, h * TPC + qg * 512: h * TPC + (qg + 1) * 512],
                sq_s[:, 0:512], rlb_s, AL.mult))

    m_prev = sc.mark()
    for tix in range(TILES_PC):
        sy.w(ve.mark())
        m_x = sy.dma(sy.r.dma_start(out=x_s[:], in_=io["xown"](tix)))
        for n2 in range(2):
            te.w(ve.mark())
            te.w(m_prev)
            te.op(te.r.matmul(ps_s2[n2][:, 0:512], ones1b[:, :],
                              bo_row[0:1, n2 * 512:(n2 + 1) * 512], start=True, stop=False))
            for hh in range(DC):
                mm = te.op(te.r.matmul(ps_s2[n2][:, 0:512],
                                       o_all[:, hh * TPC + tix * P: hh * TPC + (tix + 1) * P],
                                       wo_s[:, hh * DM + n2 * 512: hh * DM + (n2 + 1) * 512],
                                       start=False, stop=(hh == DC - 1)))
            sc.w(mm)
            sc.w(ve.mark())
            m_prev = sc.op(sc.r.activation(out=sq_s[:, DM + n2 * 512: DM + (n2 + 1) * 512],
                                           in_=ps_s2[n2][:, 0:512], func=AF.Copy))
        ve.w(m_prev, m_x)
        ve.w(sy.dmark())
        m_res = ve.op(ve.r.tensor_tensor(sq_s[:, 0:DM], sq_s[:, DM:2 * DM], x_s[:], AL.add))
        sy.w(m_res)
        sy.dma(sy.r.dma_start(out=io["xo"](tix), in_=sq_s[:, 0:DM]))



def nc_attn1():
    k = K("attn1")
    nc = k.nc
    xo_d = nc.dram_tensor("xo", [TPC, DM], F32, kind="ExternalOutput").ap()
    xown_d = nc.dram_tensor("xown", [TPC, DM], F32, kind="ExternalInput").ap()
    xb_d = nc.dram_tensor("xb", [SEQ, DM], F32, kind="ExternalInput").ap()
    ident_d = nc.dram_tensor("ident", [P, P], F32, kind="ExternalInput").ap()
    wnames = [("w_dq", [DM, DC1]), ("b_dq", [DC1]), ("w_uq", [DC1, DM]), ("b_uq", [DM]),
              ("w_dkv", [DM, DC1]), ("b_dkv", [DC1]), ("w_uk", [DC1, DM]), ("b_uk", [DM]),
              ("w_uv", [DC1, DM]), ("b_uv", [DM]), ("w_qr", [DC1, 256]), ("b_qr", [256]),
              ("w_kr", [DM, DR1]), ("b_kr", [DR1]), ("w_o", [DM, DM]), ("b_o", [DM])]
    io = {nm: nc.dram_tensor(nm, sh, F32, kind="ExternalInput").ap() for nm, sh in wnames}
    io["xb"] = lambda t: xb_d[t * P:(t + 1) * P, :]
    io["xown"] = lambda t: xown_d[t * P:(t + 1) * P, :]
    io["xo"] = lambda t: xo_d[t * P:(t + 1) * P, :]
    with k.ctx:
        k.start()
        k.ident = k.sbuf("g_ident", [P, P], BF16)
        k.ones1f = k.sbuf("g_ones1f", [1, P], F32)
        k.ones1b = k.sbuf("g_ones1b", [1, P], BF16)
        k.ones_col = k.sbuf("g_ones_col", [P, 1], BF16)
        with nc.Block() as block:
            @block.sync
            def _(_h):
                m_id = k.gp.dma(k.gp.r.dma_start(out=k.ident[:], in_=ident_d[:, :]))
                k.ve.op(k.ve.r.memset(k.ones1f[:], 1.0))
                k.ve.op(k.ve.r.memset(k.ones1b[:], 1.0))
                m_on = k.ve.op(k.ve.r.memset(k.ones_col[:], 1.0))
                io["after"] = (m_id, m_on)
                emit_attn1(k, "a1_", k.ctx, io)
                k.sy.w(k.sy.dmark())
    return k.nc



def emit_attn0(k, pfx, sctx, io):
    nc = k.nc
    NKT = SEQ // P
    s0 = 1.0 / math.sqrt(E0)
    gp, te, sc, ve, sy = k.gp, k.te, k.sc, k.ve, k.sy
    for _e in (gp, te, sc, ve, sy):
        _e.w(*io.get('after', ()))
    ident = k.ident
    ones1f = k.ones1f
    ones1b = k.ones1b
    ones_col = k.ones_col
    pool = k.sbuf(pfx + "pool", [P, 3 * DC * DM], BF16, ctx=sctx)
    wq_s = pool[:, 0:DC * DM]
    wk_s = pool[:, DC * DM:2 * DC * DM]
    wv_s = pool[:, 2 * DC * DM:3 * DC * DM]
    VW = 65 * NKT
    ves = pool[:, SEQ:SEQ + VW]
    q_h = pool[:, SEQ + VW: SEQ + VW + TPC]
    m16_s = pool[:, SEQ + VW + TPC: SEQ + VW + TPC + 16 * 512]
    wo_s = k.sbuf(pfx + "wo_s", [P, DC * DM], BF16, ctx=sctx)
    grow = k.sbuf(pfx + "grow", [1, 2 * DM], F32, ctx=sctx)
    gbt = k.sbuf(pfx + "gbt", [P, 2 * DM], F32, ctx=sctx)
    qT_s = k.sbuf(pfx + "qT_s", [P, DC * TPC], BF16, ctx=sctx)
    o_all = k.sbuf(pfx + "o_all", [P, DC * TPC], BF16, ctx=sctx)
    at = [k.sbuf(pfx + "at%d" % i, [P, 512], BF16, ctx=sctx) for i in range(4)]
    at_raw = [k.sbuf(pfx + "at_raw%d" % i, [P, 512], BF16, ctx=sctx) for i in range(2)]
    x_s = k.sbuf(pfx + "x_s", [P, DM], F32, ctx=sctx)
    xin_s = k.sbuf(pfx + "xin_s", [P, DM], F32, ctx=sctx)
    sq_s = k.sbuf(pfx + "sq_s", [P, 2 * DM], F32, ctx=sctx)
    xln = k.sbuf(pfx + "xln", [P, DM], BF16, ctx=sctx)
    xln2 = k.sbuf(pfx + "xln2", [P, DM], F32, ctx=sctx)
    xlnT = k.sbuf(pfx + "xlnT", [P, DM], BF16, ctx=sctx)
    ktile = k.sbuf(pfx + "ktile", [P, DM], BF16, ctx=sctx)
    vtile = k.sbuf(pfx + "vtile", [P, DM], BF16, ctx=sctx)
    stat = k.sbuf(pfx + "stat", [P, 8], F32, ctx=sctx)
    qn2 = k.sbuf(pfx + "qn2", [1, TPC], BF16, ctx=sctx)
    mrow = k.sbuf(pfx + "mrow", [1, TPC], BF16, ctx=sctx)
    mred = k.sbuf(pfx + "mred", [1, 8], BF16, ctx=sctx)
    kmx = k.sbuf(pfx + "kmx", [1, NKT + 2], F32, ctx=sctx)
    kmax2 = k.sbuf(pfx + "kmax2", [1, 1], F32, ctx=sctx)
    negm_t = k.sbuf(pfx + "negm_t", [P, 64], F32, ctx=sctx)
    sqk = k.sbuf(pfx + "sqk", [P, 1024], BF16, ctx=sctx)
    lrow = sq_s[0:1, 1024:1536]
    rl_s = sq_s[0:1, 1536:2048]
    rlb_s = sq_s[:, 512:1024]
    ps_t2 = [k.psum(pfx + "ps_ta", [P, 512], ctx=sctx), k.psum(pfx + "ps_tb", [P, 512], ctx=sctx)]
    ps_s2 = [k.psum(pfx + "ps_sa", [P, 512], ctx=sctx), k.psum(pfx + "ps_sb", [P, 512], ctx=sctx)]
    ps_o = k.psum(pfx + "ps_o", [P, 512], ctx=sctx)
    ps_l = k.psum(pfx + "ps_l", [P, 512], ctx=sctx)
    ps_s4 = ps_s2 + [k.psum(pfx + "ps_sc", [P, 512], ctx=sctx),
                     k.psum(pfx + "ps_sd", [P, 512], ctx=sctx)]

    for (wd, ws) in ((io["wq"], wq_s), (io["wk"], wk_s), (io["wv"], wv_s)):
        wsv = ws.rearrange("p (d h e) -> p d h e", h=H0, e=E0)
        wdv = wd.rearrange("h (d p) e -> p d h e", p=P)
        for d in range(DC):
            gp.dma(gp.r.dma_start(out=wsv[:, d, :, :], in_=wdv[:, d, :, :]))
    for d in range(DC):
        gp.dma(gp.r.dma_start(out=wo_s[:, d * DM:(d + 1) * DM], in_=io["wo"][d * P:(d + 1) * P, :]))
    gp.dma(gp.r.dma_start(out=grow[0:1, 0:DM], in_=io["gam"][:].unsqueeze(0)))
    gp.dma(gp.r.dma_start(out=grow[0:1, DM:2 * DM], in_=io["bet"][:].unsqueeze(0)))
    m_wload = gp.dmark()
    m_ones = k.ve.mark()

    xlnT3 = xlnT[:].rearrange("p (c r) -> p c r", r=P)

    lat_ev = [sc.mark()] * 4

    def qkv_tile(src_ap, tix, q_j):
        sy.w(ve.mark())
        sy.w(sc.mark())
        m_x = sy.dma(sy.r.dma_start(out=x_s[:], in_=src_ap))
        m_prevst = sy.dmark()
        ve.w(te.mark())
        m_ln = e_ln(k, x_s[:], xln[:], GLUE_EPS, DM, sq=sq_s[:, 0:DM], st=stat,
                    after=(m_x,))
        m_tr = e_transp(k, xln[:], xlnT3, DC, ps_t2, ident, after=(m_ln, m_wload))
        if q_j is None:
            for mh in range(DC):
                sl = mh % 4
                te.w(m_tr, lat_ev[sl])
                for d in range(DC):
                    mm = te.op(te.r.matmul(ps_s4[sl][:, 0:P],
                                           wk_s[:, d * DM + mh * P: d * DM + mh * P + P],
                                           xlnT3[:, d, :], start=(d == 0), stop=(d == DC - 1)))
                sc.w(mm)
                if mh < 4:
                    sc.w(m_prevst)
                lat_ev[sl] = sc.op(sc.r.activation(out=ktile[:, mh * P:(mh + 1) * P],
                                                   in_=ps_s4[sl][:, 0:P], func=AF.Copy))
            sy.w(lat_ev[(DC - 1) % 4], lat_ev[(DC - 2) % 4], lat_ev[(DC - 3) % 4], lat_ev[(DC - 4) % 4])
            sy.dma(sy.r.dma_start(
                out=io["kT_d"].rearrange("(c p) s -> p c s", p=P)[:, :, tix * P:(tix + 1) * P],
                in_=ktile[:].rearrange("p (c r) -> p c r", r=P)))
            for n2 in range(2):
                te.w(m_tr, lat_ev[n2])
                for d in range(DC):
                    mm = te.op(te.r.matmul(ps_s4[n2][:, 0:512],
                                           xlnT3[:, d, :],
                                           wv_s[:, d * DM + n2 * 512: d * DM + (n2 + 1) * 512],
                                           start=(d == 0), stop=(d == DC - 1)))
                sc.w(mm)
                sc.w(m_prevst)
                lat_ev[n2] = sc.op(sc.r.activation(out=vtile[:, n2 * 512:(n2 + 1) * 512],
                                                   in_=ps_s4[n2][:, 0:512], func=AF.Copy))
            sy.w(lat_ev[0], lat_ev[1])
            sy.dma(sy.r.dma_start(out=io["v_d"][tix * P:(tix + 1) * P, :], in_=vtile[:]))
        else:
            for mh in range(DC):
                sl = mh % 4
                te.w(m_tr, lat_ev[sl])
                for d in range(DC):
                    mm = te.op(te.r.matmul(ps_s4[sl][:, 0:P],
                                           wq_s[:, d * DM + mh * P: d * DM + mh * P + P],
                                           xlnT3[:, d, :], start=(d == 0), stop=(d == DC - 1)))
                sc.w(mm)
                lat_ev[sl] = sc.op(sc.r.activation(
                    out=qT_s[:, mh * TPC + q_j * P: mh * TPC + (q_j + 1) * P],
                    in_=ps_s4[sl][:, 0:P], func=AF.Copy, scale=s0))

    for t in range(NKT):
        qkv_tile(io["xb"](t), t, None)
    for j in range(TILES_PC):
        qkv_tile(io["xown"](j), j, j)

    gp.r.wait_ge(k.te.sem, k.te.cnt)
    m16v = m16_s.rearrange("p (i q) -> p i q", q=512)
    for i in range(16):
        gp.dma(gp.r.dma_start(out=m16v[:, i, :], in_=io["m16"][i, :, :]))
    m_m16 = gp.dmark()

    for h in range(H0):
        hp, hsub = h // 2, h % 2
        gp.r.wait_ge(k.te.sem, k.te.cnt)
        gp.dma(gp.r.dma_start(out=pool[0:E0, 0:SEQ], in_=io["kT_d"][h * E0:(h + 1) * E0, :]))
        m_qh = gp.dma(gp.r.dma_start(
            out=q_h[0:E0, :],
            in_=qT_s[hsub * E0:(hsub + 1) * E0, hp * TPC:(hp + 1) * TPC]))
        ve.w(te.mark())
        m_vset = ve.op(ve.r.memset(ves, 1.0))
        sy.w(m_vset)
        sy.w(te.mark())
        m_ves = sy.dma(sy.r.dma_start(
            out=ves.rearrange("p (b e) -> p b e", e=65)[:, :, 0:E0],
            in_=io["v_d"].rearrange("(b p) d -> p b d", p=P)[:, :, h * E0:(h + 1) * E0]))
        m_ldh = gp.dmark()
        m_pl = sc.mark()
        m_mx = ve.mark()
        for n in range(16):
            sc.w(m_ldh)
            sc.w(te.mark())
            m_sq2 = sc.op(sc.r.activation(out=sqk[0:E0, 0:512],
                                          in_=pool[0:E0, n * 512:(n + 1) * 512],
                                          func=AF.Square))
            te.w(m_sq2, m_pl, m_mx, m_ones)
            mm = te.op(te.r.matmul(ps_l[0:1, 0:512], ones_col[0:E0, :],
                                   sqk[0:E0, 0:512], start=True, stop=True))
            sc.w(mm)
            m_pl = sc.op(sc.r.activation(out=lrow, in_=ps_l[0:1, 0:512], func=AF.Copy))
            ve.w(m_pl)
            m_mx = ve.op(ve.r.reduce_max(kmx[0:1, n:n + 1], lrow, axis=AX.X))
        ve.w(m_mx)
        m_km = ve.op(ve.r.reduce_max(kmax2[0:1, 0:1], kmx[0:1, 0:16], axis=AX.X))
        m_pq = sc.mark()
        for n in range(4):
            sc.w(te.mark())
            sc.w(m_qh, m_ldh)
            m_sq3 = sc.op(sc.r.activation(out=sqk[0:E0, 0:512],
                                          in_=q_h[0:E0, n * 512:(n + 1) * 512], func=AF.Square))
            te.w(m_sq3, m_pq)
            mm = te.op(te.r.matmul(ps_l[0:1, 0:512], ones_col[0:E0, :],
                                   sqk[0:E0, 0:512], start=True, stop=True))
            sc.w(mm)
            m_pq = sc.op(sc.r.activation(out=qn2[0:1, n * 512:(n + 1) * 512],
                                         in_=ps_l[0:1, 0:512], func=AF.Copy))
        ve.w(m_pq, m_km)
        m_pr = ve.op(ve.r.tensor_scalar_mul(mrow[0:1, :], qn2[0:1, :], kmax2[0:1, 0:1]))
        sc.w(m_pr)
        m_sqr = sc.op(sc.r.activation(out=mrow[0:1, :], in_=mrow[0:1, :], func=AF.Sqrt,
                                      scale=1.0404))
        ve.w(m_sqr)
        m_red = ve.op(ve.r.reduce_max(mred[0:1, 0:4],
                                      mrow[0:1, :].rearrange("o (g q) -> o g q", q=512), axis=AX.X))
        ve.w(m_red)
        m_neg = ve.op(ve.r.tensor_scalar_mul(mred[0:1, 4:8], mred[0:1, 0:4], -1.0))
        te.w(m_neg)
        te.w(sc.mark())
        mmb = te.op(te.r.matmul(ps_l[0:P, 0:4], ones1b[:, :], mred[0:1, 4:8],
                                start=True, stop=True))
        sc.w(mmb)
        m_negt = sc.op(sc.r.activation(out=negm_t[:, h * 4:(h + 1) * 4], in_=ps_l[0:P, 0:4],
                                       func=AF.Copy))

        for qg in range(4):
            te.w(sc.mark())
            nkb = 16 * qg + 16
            m_at = [te.mark()] * 4
            m_exp = [sc.mark()] * 4
            m_msk = [ve.mark()] * 4
            m_rawc = [ve.mark(), ve.mark()]
            m_rdy = [None] * 4
            def av0(kb):
                slx = kb % 4
                te.w(m_rdy[slx])
                if kb == 0:
                    te.w(m_ves)
                te.op(te.r.matmul(ps_o[0:65, 0:512], ves[:, kb * 65:(kb + 1) * 65],
                                  at[slx][:], start=(kb == 0), stop=(kb == nkb - 1)))
                m_at[slx] = te.mark()
            for kb in range(nkb):
                sl = kb % 4
                te.w(m_exp[sl])
                mm1 = te.op(te.r.matmul(ps_s4[sl][:, 0:512],
                                        pool[0:E0, kb * P:(kb + 1) * P],
                                        q_h[0:E0, qg * 512:(qg + 1) * 512],
                                        start=True, stop=True))
                sc.w(mm1)
                sc.w(m_at[sl], m_msk[sl])
                sc.w(m_negt)
                masked = kb >= 16 * qg
                if masked:
                    rp = kb % 2
                    sc.w(m_rawc[rp])
                    dst = at_raw[rp][:]
                else:
                    dst = at[sl][:]
                m_exp[sl] = sc.op(sc.r.activation(out=dst, in_=ps_s4[sl][:, 0:512],
                                                  func=AF.Exp,
                                                  bias=negm_t[:, h * 4 + qg: h * 4 + qg + 1]))
                if masked:
                    rel = kb - 16 * qg
                    ve.w(m_exp[sl], m_at[sl], m_m16)
                    m_msk[sl] = ve.op(ve.r.tensor_tensor(
                        at[sl][:], at_raw[rp][:], m16v[:, rel, :], AL.mult))
                    m_rawc[rp] = m_msk[sl]
                    m_rdy[sl] = m_msk[sl]
                else:
                    m_rdy[sl] = m_exp[sl]
                if kb >= 2:
                    av0(kb - 2)
            av0(nkb - 2)
            av0(nkb - 1)
            sc.w(te.mark())
            m_oev = sc.op(sc.r.activation(out=sq_s[0:65, 0:512], in_=ps_o[0:65, 0:512],
                                          func=AF.Copy))
            ve.w(m_oev)
            m_rl = ve.op(ve.r.reciprocal(rl_s, sq_s[64:65, 0:512]))
            te.w(m_rl)
            te.w(sc.mark())
            mmb2 = te.op(te.r.matmul(ps_l[0:E0, 0:512], ones1f[0:1, 0:E0], rl_s,
                                     start=True, stop=True))
            sc.w(mmb2)
            m_rlb = sc.op(sc.r.activation(out=rlb_s[0:E0, :], in_=ps_l[0:E0, 0:512], func=AF.Copy))
            ve.w(m_rlb, m_oev)
            ve.op(ve.r.tensor_tensor(
                o_all[hsub * E0:(hsub + 1) * E0, hp * TPC + qg * 512: hp * TPC + (qg + 1) * 512],
                sq_s[0:E0, 0:512], rlb_s[0:E0, :], AL.mult))

    te.w(sc.mark())
    te.w(ve.mark())
    m_gb = sc.mark()
    for n4 in range(4):
        te.w(m_gb)
        te.w(m_wload)
        mmg = te.op(te.r.matmul(ps_l[0:P, 0:512], ones1f[:, :],
                                grow[0:1, n4 * 512:(n4 + 1) * 512], start=True, stop=True))
        sc.w(mmg)
        m_gb = sc.op(sc.r.activation(out=gbt[:, n4 * 512:(n4 + 1) * 512],
                                     in_=ps_l[0:P, 0:512], func=AF.Copy))

    m_prev = sc.mark()
    for j in range(TILES_PC):
        sy.w(ve.mark())
        sy.w(sc.mark())
        m_x = sy.dma(sy.r.dma_start(out=x_s[:], in_=io["xown"](j)))
        ve.w(te.mark())
        m_ln1 = e_ln(k, x_s[:], xin_s[:], GLUE_EPS, DM, sq=sq_s[:, 0:DM], st=stat,
                     after=(m_x,))
        for n2 in range(2):
            te.w(ve.mark())
            te.w(m_prev)
            for hp in range(DC):
                mm = te.op(te.r.matmul(ps_s2[n2][:, 0:512],
                                       o_all[:, hp * TPC + j * P: hp * TPC + (j + 1) * P],
                                       wo_s[:, hp * DM + n2 * 512: hp * DM + (n2 + 1) * 512],
                                       start=(hp == 0), stop=(hp == DC - 1)))
            sc.w(mm)
            sc.w(m_ln1)
            m_prev = sc.op(sc.r.activation(out=sq_s[:, n2 * 512:(n2 + 1) * 512],
                                           in_=ps_s2[n2][:, 0:512], func=AF.Copy))
        ve.w(m_prev, m_ln1)
        m_y1 = ve.op(ve.r.tensor_tensor(sq_s[:, DM:2 * DM], xin_s[:], sq_s[:, 0:DM], AL.add))
        m_ln2 = e_ln(k, sq_s[:, DM:2 * DM], xln2[:], LN_EPS0, DM, sq=sq_s[:, 0:DM], st=stat,
                     after=(m_y1,))
        ve.w(m_ln2)
        m_t1 = ve.op(ve.r.tensor_tensor(xin_s[:], xln2[:], gbt[:, 0:DM], AL.mult))
        ve.w(m_t1)
        m_t2 = ve.op(ve.r.tensor_tensor(xln2[:], xin_s[:], gbt[:, DM:2 * DM], AL.add))
        ve.w(m_t2, m_x)
        ve.w(sy.dmark())
        m_out = ve.op(ve.r.tensor_tensor(xin_s[:], xln2[:], x_s[:], AL.add))
        sy.w(m_out)
        sy.dma(sy.r.dma_start(out=io["xo"](j), in_=xin_s[:]))


def nc_attn0():
    k = K("attn0")
    nc = k.nc
    xo_d = nc.dram_tensor("xo", [TPC, DM], F32, kind="ExternalOutput").ap()
    xown_d = nc.dram_tensor("xown", [TPC, DM], F32, kind="ExternalInput").ap()
    xb_d = nc.dram_tensor("xb", [SEQ, DM], F32, kind="ExternalInput").ap()
    ident_d = nc.dram_tensor("ident", [P, P], F32, kind="ExternalInput").ap()
    io = {
        "wq": nc.dram_tensor("wq", [H0, DM, E0], F32, kind="ExternalInput").ap(),
        "wk": nc.dram_tensor("wk", [H0, DM, E0], F32, kind="ExternalInput").ap(),
        "wv": nc.dram_tensor("wv", [H0, DM, E0], F32, kind="ExternalInput").ap(),
        "wo": nc.dram_tensor("wo", [DM, DM], F32, kind="ExternalInput").ap(),
        "gam": nc.dram_tensor("gam", [DM], F32, kind="ExternalInput").ap(),
        "bet": nc.dram_tensor("bet", [DM], F32, kind="ExternalInput").ap(),
        "m16": nc.dram_tensor("m16", [16, P, 512], F32, kind="ExternalInput").ap(),
        "kT_d": nc.dram_tensor("kT_scr", [DM, SEQ], BF16).ap(),
        "v_d": nc.dram_tensor("v_scr", [SEQ, DM], BF16).ap(),
    }
    io["xb"] = lambda t: xb_d[t * P:(t + 1) * P, :]
    io["xown"] = lambda t: xown_d[t * P:(t + 1) * P, :]
    io["xo"] = lambda t: xo_d[t * P:(t + 1) * P, :]
    with k.ctx:
        k.start()
        k.ident = k.sbuf("g_ident", [P, P], BF16)
        k.ones1f = k.sbuf("g_ones1f", [1, P], F32)
        k.ones1b = k.sbuf("g_ones1b", [1, P], BF16)
        k.ones_col = k.sbuf("g_ones_col", [P, 1], BF16)
        with nc.Block() as block:
            @block.sync
            def _(_h):
                m_id = k.gp.dma(k.gp.r.dma_start(out=k.ident[:], in_=ident_d[:, :]))
                k.ve.op(k.ve.r.memset(k.ones1f[:], 1.0))
                k.ve.op(k.ve.r.memset(k.ones1b[:], 1.0))
                m_on = k.ve.op(k.ve.r.memset(k.ones_col[:], 1.0))
                io["after"] = (m_id, m_on)
                emit_attn0(k, "a0_", k.ctx, io)
                k.sy.w(k.sy.dmark())
    return k.nc


def build_mask16(r):
    m = np.zeros((16, P, 512), np.float32)
    for i in range(16):
        for jj in range(4):
            qg = (r + 4 * jj) * P + np.arange(P)[None, :]
            kg = i * P + np.arange(P)[:, None]
            m[i, :, jj * P:(jj + 1) * P] = (qg >= kg).astype(np.float32)
    return m


def run_attn0(x, wq, wk, wv, wo, gam, bet, ident):
    nc = _get("attn0", nc_attn0)
    in_maps = []
    for c in range(NCORES):
        b, r = c // 4, c % 4
        in_maps.append({
            "xown": np.ascontiguousarray(x[own_rows(c)]),
            "xb": np.ascontiguousarray(x[b * SEQ:(b + 1) * SEQ]),
            "ident": ident, "wq": wq, "wk": wk, "wv": wv, "wo": wo,
            "gam": gam, "bet": bet, "m16": build_mask16(r),
        })
    res = run_bass_kernel_spmd(nc, in_maps, core_ids=list(range(NCORES)))
    out = np.empty((NTOK, DM), np.float32)
    for c in range(NCORES):
        out[own_rows(c)] = res.results[c]["xo"]
    return out



def emit_moeB(k, pfx, sctx, io):
    nc = k.nc
    NT_ALL = NTOK // P
    FC = DFF0 // P
    SC = 256
    NSC = CAP // SC
    gp, te, sc, ve, sy = k.gp, k.te, k.sc, k.ve, k.sy
    for _e in (gp, te, sc, ve, sy):
        _e.w(*io.get('after', ()))
    ident = k.ident
    ones1f = k.ones1f
    ones1b = k.ones1b
    triu = k.sbuf(pfx + "triu_s", [P, P], F32, ctx=sctx)
    ones_cf = k.sbuf(pfx + "ones_cf", [P, 1], F32, ctx=sctx)
    w1q = k.sbuf(pfx + "w1q", [P, DC * DFF0], BF16, ctx=sctx)
    w2q = k.sbuf(pfx + "w2q", [P, FC * DM], BF16, ctx=sctx)
    b1_s = k.sbuf(pfx + "b1_s", [P, FC], F32, ctx=sctx)
    b2row = k.sbuf(pfx + "b2row", [1, DM], BF16, ctx=sctx)
    rwT = k.sbuf(pfx + "rwT", [P, DC * NEXP], BF16, ctx=sctx)
    x4 = k.sbuf(pfx + "x4", [P, 8 * DM], F32, ctx=sctx)
    sq4 = k.sbuf(pfx + "sq4", [P, DM], F32, ctx=sctx)
    xlnT = k.sbuf(pfx + "xlnT", [P, DM], BF16, ctx=sctx)
    u16 = k.sbuf(pfx + "u16", [P, FC * SC], BF16, ctx=sctx)
    st4 = k.sbuf(pfx + "st4", [P, 16], F32, ctx=sctx)
    lg4 = k.sbuf(pfx + "lg4", [P, 8 * 32], F32, ctx=sctx)
    ohb4 = k.sbuf(pfx + "ohb4", [P, 32], F32, ctx=sctx)
    pg4 = k.sbuf(pfx + "pg4", [P, 32], F32, ctx=sctx)
    tok4 = k.sbuf(pfx + "tok4", [P, 32], F32, ctx=sctx)
    dst4i = k.sbuf(pfx + "dst4i", [P, 8], I32, ctx=sctx)
    gi4 = k.sbuf(pfx + "gi4", [P, 8], I32, ctx=sctx)
    gk4 = k.sbuf(pfx + "gk4", [P, 8], F32, ctx=sctx)
    base_row = k.sbuf(pfx + "base_row", [1, NEXP], F32, ctx=sctx)
    base4 = k.sbuf(pfx + "base4", [1, 32], F32, ctx=sctx)
    cs4 = k.sbuf(pfx + "cs4", [1, 32], F32, ctx=sctx)
    base4t = k.sbuf(pfx + "base4t", [P, 32], F32, ctx=sctx)
    rows2 = k.sbuf(pfx + "rows2", [1, 64], F32, ctx=sctx)
    esel_t = k.sbuf(pfx + "esel_t", [P, 32], F32, ctx=sctx)
    e8_t = k.sbuf(pfx + "e8_t", [P, 32], F32, ctx=sctx)
    zero_s = k.sbuf(pfx + "zero_s", [P, DM], BF16, ctx=sctx)
    btile = k.sbuf(pfx + "btile", [P, DM], BF16, ctx=sctx)
    btile2 = k.sbuf(pfx + "btile2", [P, DM], BF16, ctx=sctx)
    btB0 = k.sbuf(pfx + "btB0", [P, DM], BF16, ctx=sctx)
    btB1 = k.sbuf(pfx + "btB1", [P, DM], BF16, ctx=sctx)
    bufT = k.sbuf(pfx + "bufT", [P, DC * SC], BF16, ctx=sctx)
    hT = u16
    eot = k.sbuf(pfx + "eot", [P, DM], BF16, ctx=sctx)
    ps_t2 = [k.psum(pfx + "ps_ta", [P, 512], ctx=sctx), k.psum(pfx + "ps_tb", [P, 512], ctx=sctx)]
    ps_a = k.psum(pfx + "ps_a", [P, 512], ctx=sctx)
    ps_b = k.psum(pfx + "ps_b", [P, 512], ctx=sctx)
    ps_c = k.psum(pfx + "ps_c", [P, 512], ctx=sctx)

    gp.dma(gp.r.dma_start(out=triu[:], in_=io["triu"][:, :]))
    for d in range(DC):
        gp.dma(gp.r.dma_start(out=w1q[:, d * DFF0:(d + 1) * DFF0], in_=io["w1"][d * P:(d + 1) * P, :]))
    for f in range(FC):
        gp.dma(gp.r.dma_start(out=w2q[:, f * DM:(f + 1) * DM], in_=io["w2"][f * P:(f + 1) * P, :]))
        gp.dma(gp.r.dma_start(out=b1_s[:, f:f + 1], in_=io["b1"][f * P:(f + 1) * P].unsqueeze(1)))
    gp.dma(gp.r.dma_start(out=b2row[:], in_=io["b2"][:].unsqueeze(0)))
    gp.dma(gp.r.dma_start(out=xlnT[0:NEXP, 0:DM], in_=io["rw"][:, :]))
    gp.dma(gp.r.dma_start(out=rows2[0:1, 0:32], in_=io["esel"][:, :]))
    gp.dma(gp.r.dma_start(out=rows2[0:1, 32:64], in_=io["e8"][:, :]))
    m_wload = m_rwn = m_rows = gp.dmark()
    ve.op(ve.r.memset(ones_cf[:], 1.0))
    ve.op(ve.r.memset(zero_s[:], 0.0))
    m_base0 = ve.op(ve.r.memset(base_row[0:1, :], 0.0))
    te.w(m_wload, m_base0)
    te.w(m_rwn, m_wload)
    m_rt = sc.mark()
    for d in range(DC):
        te.w(m_rt)
        mmr = te.op(te.r.matmul(ps_b[0:P, 0:NEXP], xlnT[0:NEXP, d * P:(d + 1) * P],
                                ident[0:NEXP, 0:NEXP], start=True, stop=True))
        sc.w(mmr)
        m_rt = sc.op(sc.r.activation(out=rwT[:, d * NEXP:(d + 1) * NEXP],
                                     in_=ps_b[0:P, 0:NEXP], func=AF.Copy))
    te.w(m_rows)
    mm1 = te.op(te.r.matmul(ps_a[:, 0:32], ones1f[:, :], rows2[0:1, 0:32], start=True, stop=True))
    sc.w(mm1)
    m_es = sc.op(sc.r.activation(out=esel_t[:], in_=ps_a[:, 0:32], func=AF.Copy))
    te.w(m_es)
    mm2 = te.op(te.r.matmul(ps_a[:, 0:32], ones1f[:, :], rows2[0:1, 32:64], start=True, stop=True))
    sc.w(mm2)
    m_e8 = sc.op(sc.r.activation(out=e8_t[:], in_=ps_a[:, 0:32], func=AF.Copy))
    sy.w(ve.mark())
    for t in range(CAP // P):
        sy.dma(sy.r.dma_start(out=io["buf_d"][t * P:(t + 1) * P, :], in_=zero_s[:]))
    m_zero = sy.dmark()

    xlnT3 = xlnT[:].rearrange("p (c r) -> p c r", r=P)
    NBATCH = NT_ALL // 4
    sq4v = sq4[:].rearrange("p (g f) -> p g f", f=DM)

    def lgr(r):
        return lg4[:, r * 32:(r + 1) * 32]

    def lgrv(r):
        return lgr(r).rearrange("p (g e) -> p g e", e=NEXP)
    mxb = tok4[:, 0:4].unsqueeze(2).broadcast_to([P, 4, NEXP])
    scat_hist = []
    st_hist = []
    m_ap_prev = m_sq_prev = None
    m_ss2_prev = None
    m_rstd_prev = None
    m_el_prev = None
    m_sub_prev = None
    m_tri_prev = None
    m_bc_prev = None
    m_lg_prev = m_e8
    xs_sy = [sctx.enter_context(nc.semaphore(pfx + "xsy0")),
             sctx.enter_context(nc.semaphore(pfx + "xsy1"))]
    xs_gp = [sctx.enter_context(nc.semaphore(pfx + "xgp0")),
             sctx.enter_context(nc.semaphore(pfx + "xgp1"))]
    xcnt = [0, 0]

    def issue_loads(bb):
        pp = (bb % 2) * 4 * DM
        for i in range(2):
            sy.r.dma_start(out=x4[:, pp + i * DM: pp + (i + 1) * DM],
                           in_=io["xf"](4 * bb + i)).then_inc(xs_sy[bb % 2], 16)
        for i in range(2, 4):
            gp.r.dma_start(out=x4[:, pp + i * DM: pp + (i + 1) * DM],
                           in_=io["xf"](4 * bb + i)).then_inc(xs_gp[bb % 2], 16)
        xcnt[bb % 2] += 32
        return ((xs_sy[bb % 2], xcnt[bb % 2]), (xs_gp[bb % 2], xcnt[bb % 2]))
    sy.w(m_zero)
    gp.w(m_zero)
    m_x_next = issue_loads(0)
    for b in range(NBATCH):
        par = b % 2
        xlnp = u16[:, par * (4 * DM):(par + 1) * (4 * DM)]
        xp = x4[:, par * (4 * DM):(par + 1) * (4 * DM)]
        xpv = xp.rearrange("p (g f) -> p g f", f=DM)
        m_x = m_x_next
        if b + 1 < NBATCH:
            if m_ap_prev is not None:
                sy.w(m_ap_prev, m_sq_prev)
                gp.w(m_ap_prev, m_sq_prev)
            m_x_next = issue_loads(b + 1)
        ve.w(*m_x)
        m_sum = ve.op(ve.r.tensor_reduce(st4[:, 0:4], xpv, axis=AX.X, op=AL.add))
        m_ss2 = None
        for h in range(4):
            sc.w(*m_x)
            if h == 0 and m_ss2_prev is not None:
                sc.w(m_ss2_prev)
            if h >= 1:
                sc.w(m_ss2)
            m_sq = sc.op(sc.r.activation(out=sq4[:], in_=xp[:, h * DM:(h + 1) * DM],
                                         func=AF.Square))
            ve.w(m_sq)
            m_ss2 = ve.op(ve.r.reduce_sum(st4[:, 4 + h:5 + h], sq4[:], axis=AX.X))
        m_sq_prev = m_sq
        m_ss2_prev = m_ss2
        ve.w(m_sum)
        m_mean = ve.op(ve.r.tensor_scalar_mul(st4[:, 0:4], st4[:, 0:4], 1.0 / DM))
        ve.w(m_ss2)
        m_ex2 = ve.op(ve.r.tensor_scalar_mul(st4[:, 4:8], st4[:, 4:8], 1.0 / DM))
        ve.w(m_mean)
        if m_rstd_prev is not None:
            ve.w(m_rstd_prev)
        m_m2 = ve.op(ve.r.tensor_tensor(st4[:, 8:12], st4[:, 0:4], st4[:, 0:4], AL.mult))
        ve.w(m_ex2, m_m2)
        m_var = ve.op(ve.r.tensor_tensor(st4[:, 8:12], st4[:, 4:8], st4[:, 8:12], AL.subtract))
        ve.w(m_var)
        m_vpe = ve.op(ve.r.tensor_single_scalar(st4[:, 8:12], st4[:, 8:12], float(GLUE_EPS), AL.add))
        sc.w(m_vpe)
        m_ln = sc.op(sc.r.activation(out=st4[:, 12:16], in_=st4[:, 8:12], func=AF.Ln))
        sc.w(m_ln)
        m_rstd = sc.op(sc.r.activation(out=st4[:, 8:12], in_=st4[:, 12:16], func=AF.Exp, scale=-0.5))
        m_rstd_prev = m_rstd
        ve.w(m_rstd)
        m_b2 = ve.op(ve.r.scalar_tensor_tensor(st4[:, 12:16], st4[:, 0:4], -1.0, st4[:, 8:12], AL.mult, AL.mult))
        ve.w(m_b2)
        if b >= 2:
            ve.w(scat_hist[b - 2])
        ap_ms = []
        for i in range(4):
            m_ap = ve.op(ve.r.tensor_scalar(xlnp[:, i * DM:(i + 1) * DM], xp[:, i * DM:(i + 1) * DM],
                                            st4[:, 8 + i:9 + i], st4[:, 12 + i:13 + i],
                                            AL.mult, AL.add))
            ap_ms.append(m_ap)
        m_ap_prev = ap_ms[-1]
        te.w(m_lg_prev)
        for i in range(4):
            m_tr = e_transp(k, xlnp[:, i * DM:(i + 1) * DM], xlnT3, DC, ps_t2, ident,
                            after=(ap_ms[i], m_wload))
            te.w(m_tr)
            for d in range(DC):
                mm = te.op(te.r.matmul(ps_a[:, i * NEXP:(i + 1) * NEXP], xlnT3[:, d, :],
                                       rwT[:, d * NEXP:(d + 1) * NEXP],
                                       start=(d == 0), stop=(d == DC - 1)))
        sc.w(mm)
        if m_sub_prev is not None:
            sc.w(m_sub_prev)
        m_lg = sc.op(sc.r.activation(out=lgr(0), in_=ps_a[:, 0:32], func=AF.Copy))
        m_lg_prev = m_lg
        ve.w(m_lg)
        m_mx = ve.op(ve.r.tensor_reduce(tok4[:, 0:4], lgrv(0), axis=AX.X, op=AL.max))
        ve.w(m_mx)
        if m_el_prev is not None:
            ve.w(m_el_prev)
        m_sub = ve.op(ve.r.tensor_tensor(lgrv(7), lgrv(0), mxb, AL.subtract))
        m_sub_prev = m_sub
        sc.w(m_sub)
        m_el = sc.op(sc.r.activation(out=lgr(1), in_=lgr(7), func=AF.Exp))
        m_el_prev = m_el
        ve.w(m_el)
        m_sm = ve.op(ve.r.tensor_reduce(tok4[:, 4:8], lgrv(1), axis=AX.X, op=AL.add))
        ve.w(m_sm)
        m_p = ve.op(ve.r.reciprocal(tok4[:, 8:12], tok4[:, 4:8]))
        ve.w(m_sub)
        m_oh = ve.op(ve.r.tensor_single_scalar(lgr(2), lgr(7), 0.0, AL.is_ge))
        ve.w(m_oh)
        m_s0 = ve.op(ve.r.memset(lgr(3), 0.0))
        ve.w(m_s0)
        m_s1 = ve.op(ve.r.tensor_copy(lgrv(3)[:, :, 1:8], lgrv(2)[:, :, 0:7]))
        ve.w(m_s1)
        m_s2 = ve.op(ve.r.tensor_copy(lgr(4), lgr(3)))
        ve.w(m_s2)
        m_s3 = ve.op(ve.r.scalar_tensor_tensor(lgrv(4)[:, :, 2:8], lgrv(3)[:, :, 0:6], 1.0, lgrv(4)[:, :, 2:8], AL.mult, AL.add))
        ve.w(m_s3)
        m_s4 = ve.op(ve.r.tensor_copy(lgr(5), lgr(4)))
        ve.w(m_s4)
        m_s5 = ve.op(ve.r.scalar_tensor_tensor(lgrv(5)[:, :, 4:8], lgrv(4)[:, :, 0:4], 1.0, lgrv(5)[:, :, 4:8], AL.mult, AL.add))
        ve.w(m_s5)
        m_pre = ve.op(ve.r.tensor_single_scalar(lgr(6), lgr(5), 0.5, AL.is_lt))
        ve.w(m_pre)
        if m_tri_prev is not None:
            ve.w(m_tri_prev)
        m_ohs = ve.op(ve.r.tensor_tensor(ohb4[:], lgr(2), lgr(6), AL.mult))
        te.w(m_ohs, m_lg)
        for i in range(4):
            mm_tri = te.op(te.r.matmul(ps_b[:, i * NEXP:(i + 1) * NEXP], triu[:],
                                       ohb4[:, i * NEXP:(i + 1) * NEXP], start=True, stop=True))
        mm_cs = te.op(te.r.matmul(ps_c[0:1, 0:32], ones_cf[:, :], ohb4[:, 0:32], start=True, stop=True))
        m_tri_prev = mm_cs
        sc.w(mm_tri)
        m_pg = sc.op(sc.r.activation(out=pg4[:], in_=ps_b[:, 0:32], func=AF.Copy))
        sc.w(mm_cs)
        m_cse = sc.op(sc.r.activation(out=cs4[0:1, :], in_=ps_c[0:1, 0:32], func=AF.Copy))
        ve.w(m_cse)
        if m_bc_prev is not None:
            ve.w(m_bc_prev)
        m_c0 = ve.op(ve.r.tensor_copy(base4[0:1, 0:8], base_row[0:1, 0:8]))
        ve.w(m_c0)
        m_c1 = ve.op(ve.r.tensor_tensor(base4[0:1, 8:16], base4[0:1, 0:8], cs4[0:1, 0:8], AL.add))
        ve.w(m_c1)
        m_c2 = ve.op(ve.r.tensor_tensor(base4[0:1, 16:24], base4[0:1, 8:16], cs4[0:1, 8:16], AL.add))
        ve.w(m_c2)
        m_c3 = ve.op(ve.r.tensor_tensor(base4[0:1, 24:32], base4[0:1, 16:24], cs4[0:1, 16:24], AL.add))
        ve.w(m_c3)
        m_bru = ve.op(ve.r.tensor_tensor(base_row[0:1, 0:8], base4[0:1, 24:32], cs4[0:1, 24:32], AL.add))
        te.w(m_c3, m_cse)
        mm_bc = te.op(te.r.matmul(ps_c[:, 0:32], ones1f[:, :], base4[0:1, 0:32], start=True, stop=True))
        m_bc_prev = mm_bc
        sc.w(mm_bc)
        m_b4t = sc.op(sc.r.activation(out=base4t[:], in_=ps_c[:, 0:32], func=AF.Copy))
        ve.w(m_pg, m_b4t)
        m_pos = ve.op(ve.r.tensor_tensor(pg4[:], pg4[:], base4t[:], AL.add))
        ve.w(m_pos)
        m_t1 = ve.op(ve.r.tensor_tensor(lgr(7), pg4[:], ohb4[:], AL.mult))
        ve.w(m_t1)
        m_pt = ve.op(ve.r.tensor_reduce(tok4[:, 12:16], lgrv(7), axis=AX.X, op=AL.add))
        ve.w(m_pt)
        m_t2 = ve.op(ve.r.tensor_tensor(lgr(7), e8_t[:], ohb4[:], AL.mult))
        ve.w(m_t2)
        m_et = ve.op(ve.r.tensor_reduce(tok4[:, 16:20], lgrv(7), axis=AX.X, op=AL.add))
        ve.w(m_et)
        m_t3 = ve.op(ve.r.tensor_tensor(lgr(7), esel_t[:], ohb4[:], AL.mult))
        ve.w(m_t3)
        m_oc = ve.op(ve.r.tensor_reduce(tok4[:, 20:24], lgrv(7), axis=AX.X, op=AL.add))
        if b >= 2:
            ve.w(st_hist[b - 2])
        ve.w(m_pt, m_p)
        m_kp = ve.op(ve.r.tensor_single_scalar(tok4[:, 24:28], tok4[:, 12:16], float(CAP), AL.is_lt))
        ve.w(m_kp)
        m_k2 = ve.op(ve.r.scalar_tensor_tensor(gk4[:, 4 * par:4 * par + 4], tok4[:, 24:28], 0.5, tok4[:, 8:12], AL.mult, AL.mult))
        ve.w(m_k2)
        m_pc = ve.op(ve.r.tensor_single_scalar(tok4[:, 24:28], tok4[:, 12:16], float(CAP - 1), AL.min))
        ve.w(m_et, m_pc)
        m_gi = ve.op(ve.r.scalar_tensor_tensor(tok4[:, 28:32], tok4[:, 16:20], float(CAP), tok4[:, 24:28], AL.mult, AL.add))
        ve.w(m_gi)
        m_gii = ve.op(ve.r.tensor_copy(gi4[:, 4 * par:4 * par + 4], tok4[:, 28:32]))
        ve.w(m_oc)
        m_d1 = ve.op(ve.r.tensor_scalar(tok4[:, 20:24], tok4[:, 20:24], -1.0, 1.0, AL.mult, AL.add))
        ve.w(m_d1)
        m_d2 = ve.op(ve.r.scalar_tensor_tensor(tok4[:, 24:28], tok4[:, 20:24], BIGF, tok4[:, 12:16], AL.mult, AL.add))
        ve.w(m_d2)
        m_d3 = ve.op(ve.r.tensor_single_scalar(tok4[:, 24:28], tok4[:, 24:28], float(CAP), AL.min))
        ve.w(m_d3)
        m_di = ve.op(ve.r.tensor_copy(dst4i[:, 4 * par:4 * par + 4], tok4[:, 24:28]))
        sy.w(m_k2, m_gii)
        for i in range(4):
            sy.dma(sy.r.dma_start(out=io["gidx_d"][(4 * b + i) * P:(4 * b + i + 1) * P, :],
                                  in_=gi4[:, 4 * par + i:4 * par + i + 1]))
            sy.dma(sy.r.dma_start(out=io["keep2_d"][(4 * b + i) * P:(4 * b + i + 1) * P, :],
                                  in_=gk4[:, 4 * par + i:4 * par + i + 1]))
        st_hist.append(sy.dmark())
        gp.r.wait_ge(k.ve.sem, m_di[1])
        for i in range(4):
            gp.dma(gp.r.indirect_dma_start(
                out=io["buf_d"][:, :], out_offset=IndirectOffsetOnAxis(dst4i[:, 4 * par + i:4 * par + i + 1], 0),
                in_=xlnp[:, i * DM:(i + 1) * DM], in_offset=None))
        scat_hist.append(gp.dmark())
    m_scat = gp.dmark()

    bts = [btile, btile2, btB0, btB1]
    tr_hist = {}
    est = sctx.enter_context(nc.semaphore(pfx + "est"))
    ecnt = [0]

    def buf_loads(ss):
        pr = (ss % 2) * 2
        if ss >= 2:
            sy.w(*tr_hist[ss - 2])
        for ti in range(2):
            sy.dma(sy.r.dma_start(out=bts[pr + ti][:],
                                  in_=io["buf_d"][ss * SC + ti * P: ss * SC + (ti + 1) * P, :]))
        return sy.dmark()
    sy.w(te.mark())
    sy.w(sy.dmark())
    gp.r.wait_ge(k.sy.dsem, k.sy.dcnt)
    sy.w(m_scat)
    m_bl_next = buf_loads(0)
    for s in range(NSC):
        pr = (s % 2) * 2
        m_bl = m_bl_next
        if s + 1 < NSC:
            m_bl_next = buf_loads(s + 1)
        bufT3 = bufT[:].rearrange("p (c r) -> p c r", r=SC)
        m_tr1 = e_transp(k, bts[pr][:], bufT3[:, :, 0:P], DC, ps_t2, ident, after=(m_bl,))
        m_tr2 = e_transp(k, bts[pr + 1][:], bufT3[:, :, P:SC], DC, ps_t2, ident, after=(m_bl,))
        tr_hist[s] = (m_tr1, m_tr2)
        m_hvb = [sc.mark(), sc.mark()]
        for f in range(FC):
            te.w(m_tr1, m_tr2)
            te.w(m_hvb[f % 2])
            for d in range(DC):
                mm = te.op(te.r.matmul(ps_a[:, 0:SC] if f % 2 == 0 else ps_b[:, 0:SC],
                                       w1q[:, d * DFF0 + f * P: d * DFF0 + (f + 1) * P],
                                       bufT[:, d * SC:(d + 1) * SC],
                                       start=(d == 0), stop=(d == DC - 1)))
            sc.w(mm)
            sc.w(m_wload)
            m_hvb[f % 2] = sc.op(sc.r.activation(out=hT[:, f * SC:(f + 1) * SC],
                                                 in_=ps_a[:, 0:SC] if f % 2 == 0 else ps_b[:, 0:SC],
                                                 func=AF.Gelu, bias=b1_s[:, f:f + 1]))
        m_hv = m_hvb[1]
        for sm in range(2):
            m_eev = sc.mark()
            for n2 in range(2):
                te.w(m_hv, m_eev)
                te.op(te.r.matmul(ps_a[:, 0:512] if n2 == 0 else ps_b[:, 0:512],
                                  ones1b[:, :], b2row[0:1, n2 * 512:(n2 + 1) * 512],
                                  start=True, stop=False))
                for f in range(FC):
                    mm = te.op(te.r.matmul(ps_a[:, 0:512] if n2 == 0 else ps_b[:, 0:512],
                                           hT[:, f * SC + sm * P: f * SC + sm * P + P],
                                           w2q[:, f * DM + n2 * 512: f * DM + (n2 + 1) * 512],
                                           start=False, stop=(f == FC - 1)))
                sc.w(mm)
                if ecnt[0]:
                    sc.w((est, ecnt[0]))
                m_eev = sc.op(sc.r.activation(out=eot[:, n2 * 512:(n2 + 1) * 512],
                                              in_=ps_a[:, 0:512] if n2 == 0 else ps_b[:, 0:512],
                                              func=AF.Copy))
            sy.w(m_eev)
            sy.r.dma_start(out=io["eo_d"][s * SC + sm * P: s * SC + (sm + 1) * P, :],
                           in_=eot[:]).then_inc(est, 16)
            ecnt[0] += 16
    sy.w((est, ecnt[0]))


def emit_moeC(k, pfx, sctx, io):
    nc = k.nc
    gp, te, sc, ve, sy = k.gp, k.te, k.sc, k.ve, k.sy
    for _e in (gp, te, sc, ve, sy):
        _e.w(*io.get('after', ()))
    oix = k.sbuf(pfx + "oix", [P, 1], I32, ctx=sctx) if "ownidx" in io else None
    x_s = k.sbuf(pfx + "x_s", [P, DM], F32, ctx=sctx)
    g_s = k.sbuf(pfx + "g_s", [P, DM], BF16, ctx=sctx)
    o_s = k.sbuf(pfx + "o_s", [P, DM], F32, ctx=sctx)
    gi = k.sbuf(pfx + "gi", [P, 1], I32, ctx=sctx)
    kp = k.sbuf(pfx + "kp", [P, 1], F32, ctx=sctx)
    for t in range(TILES_PC):
        sy.w(ve.mark())
        m_x = sy.dma(sy.r.dma_start(out=x_s[:], in_=io["xown"](t)))
        if oix is not None:
            gp.r.wait_ge(k.ve.sem, k.ve.cnt)
            m_oix = gp.dma(gp.r.dma_start(out=oix[:], in_=io["ownidx"][t * P:(t + 1) * P, :]))
            gp.r.wait_ge(k.gp.dsem, m_oix[1])
            gp.dma(gp.r.indirect_dma_start(
                out=gi[:, :], out_offset=None,
                in_=io["gidx_full"][:, :], in_offset=IndirectOffsetOnAxis(oix[:, :], 0)))
            m_meta = gp.dma(gp.r.indirect_dma_start(
                out=kp[:, :], out_offset=None,
                in_=io["keep2_full"][:, :], in_offset=IndirectOffsetOnAxis(oix[:, :], 0)))
            gp.r.wait_ge(k.gp.dsem, m_meta[1])
        else:
            sy.dma(sy.r.dma_start(out=gi[:], in_=io["gidx"](t)))
            m_meta = sy.dma(sy.r.dma_start(out=kp[:], in_=io["keep2"](t)))
            gp.r.wait_ge(k.sy.dsem, m_meta[1])
            gp.r.wait_ge(k.ve.sem, k.ve.cnt)
        m_g = gp.dma(gp.r.indirect_dma_start(
            out=g_s[:, :], out_offset=None,
            in_=io["eoall"][:, :], in_offset=IndirectOffsetOnAxis(gi[:, :], 0)))
        ve.w(m_g, m_x)
        ve.w(m_meta if oix is None else (k.gp.dsem, k.gp.dcnt))
        ve.w(sy.dmark())
        m_s = ve.op(ve.r.tensor_scalar_mul(o_s[:], g_s[:], kp[:, 0:1]))
        ve.w(m_s)
        m_o = ve.op(ve.r.tensor_tensor(o_s[:], o_s[:, 0:DM], x_s[:], AL.add)) if False else None
        m_o = ve.op(ve.r.tensor_tensor(x_s[:], o_s[:], x_s[:], AL.add))
        sy.w(m_o)
        sy.dma(sy.r.dma_start(out=io["xo"](t), in_=x_s[:]))


def barrier(k):
    marks = [k.te.mark(), k.sc.mark(), k.ve.mark(), k.gp.mark(), k.gp.dmark(),
             k.sy.dmark(), k.sy.mark()]
    for e in (k.te, k.sc, k.ve, k.gp, k.sy):
        e.w(*marks)


def emit_ag(k, in_ap, out_ap, groups):
    s = k.ccsem()
    k.gp.r.collective_compute("AllGather", AL.bypass, ins=[in_ap.opt()],
                              outs=[out_ap.opt()], replica_groups=groups).then_inc(s)
    return (s, 1)


def emit_prep(k, pfx, sctx, io):
    gp, te, sc, ve, sy = k.gp, k.te, k.sc, k.ve, k.sy
    for _e in (gp, te, sc, ve, sy):
        _e.w(*io.get('after', ()))
    xs = k.sbuf(pfx + "xs", [P, DI1], F32, ctx=sctx)
    qo = k.sbuf(pfx + "qo", [P, DI1], BF16, ctx=sctx)
    t = {
        "xn": k.sbuf(pfx + "t_xn", [P, DI1], F32, ctx=sctx),
        "a": k.sbuf(pfx + "t_a", [P, DI1], F32, ctx=sctx),
        "sgn": k.sbuf(pfx + "t_sgn", [P, DI1], BF16, ctx=sctx),
        "acc": k.sbuf(pfx + "t_acc", [P, DI1], BF16, ctx=sctx),
        "acc2": k.sbuf(pfx + "t_acc2", [P, DI1], BF16, ctx=sctx),
        "am": k.sbuf(pfx + "t_am", [P, NB_DI], F32, ctx=sctx),
        "scs": k.sbuf(pfx + "t_scs", [P, NB_DI], F32, ctx=sctx),
        "rcs": k.sbuf(pfx + "t_rcs", [P, NB_DI], F32, ctx=sctx),
    }
    jobs = []
    for s_ in range(3):
        jobs.append((io["p_wg"][s_, :, :], DM, NB_DM, s_ * DM))
    for s_ in range(3):
        jobs.append((io["p_wu"][s_, :, :], DM, NB_DM, (3 + s_) * DM))
    jobs.append((io["p_wd"][0, :, :], DI1, NB_DI, 6 * DM))
    for (srcap, width, nb, col0) in jobs:
        sy.w(ve.mark())
        m_ld = sy.dma(sy.r.dma_start(out=xs[:, 0:width], in_=srcap))
        ve.w(sy.dmark())
        sc.w(sy.dmark())
        tt = {kk2: (v[:, 0:nb] if kk2 in ("am", "scs", "rcs") else v[:, 0:width])
              for kk2, v in t.items()}
        m_q = e_quant(k, xs[:, 0:width], qo[:, 0:width], nb, tt, after=(m_ld,))
        sy.w(m_q)
        sy.dma(sy.r.dma_start(out=io["pack"][:, col0:col0 + width], in_=qo[:, 0:width]))


def nc_moeB():
    k = K("moeB")
    nc = k.nc
    io = {
        "eo_d": nc.dram_tensor("eo", [CAP, DM], BF16, kind="ExternalOutput").ap(),
        "gidx_d": nc.dram_tensor("gidx", [NTOK, 1], I32, kind="ExternalOutput").ap(),
        "keep2_d": nc.dram_tensor("keep2", [NTOK, 1], F32, kind="ExternalOutput").ap(),
        "rw": nc.dram_tensor("rw", [NEXP, DM], F32, kind="ExternalInput").ap(),
        "w1": nc.dram_tensor("w1", [DM, DFF0], F32, kind="ExternalInput").ap(),
        "b1": nc.dram_tensor("b1", [DFF0], F32, kind="ExternalInput").ap(),
        "w2": nc.dram_tensor("w2", [DFF0, DM], F32, kind="ExternalInput").ap(),
        "b2": nc.dram_tensor("b2", [DM], F32, kind="ExternalInput").ap(),
        "triu": nc.dram_tensor("triu", [P, P], F32, kind="ExternalInput").ap(),
        "esel": nc.dram_tensor("esel", [1, 4 * NEXP], F32, kind="ExternalInput").ap(),
        "e8": nc.dram_tensor("e8", [1, 4 * NEXP], F32, kind="ExternalInput").ap(),
        "buf_d": nc.dram_tensor("buf_scr", [CAP + 1, DM], BF16).ap(),
    }
    x_d = nc.dram_tensor("xf", [NTOK, DM], F32, kind="ExternalInput").ap()
    ident_d = nc.dram_tensor("ident", [P, P], F32, kind="ExternalInput").ap()
    io["xf"] = lambda t: x_d[t * P:(t + 1) * P, :]
    with k.ctx:
        k.start()
        k.ident = k.sbuf("g_ident", [P, P], BF16)
        k.ones1f = k.sbuf("g_ones1f", [1, P], F32)
        k.ones1b = k.sbuf("g_ones1b", [1, P], BF16)
        with nc.Block() as block:
            @block.sync
            def _(_h):
                m_id = k.gp.dma(k.gp.r.dma_start(out=k.ident[:], in_=ident_d[:, :]))
                k.ve.op(k.ve.r.memset(k.ones1f[:], 1.0))
                m_on = k.ve.op(k.ve.r.memset(k.ones1b[:], 1.0))
                io["after"] = (m_id, m_on)
                emit_moeB(k, "mb_", k.ctx, io)
                k.sy.w(k.sy.dmark())
    return k.nc


def nc_moeC():
    k = K("moeC")
    nc = k.nc
    xo_d = nc.dram_tensor("xo", [TPC, DM], F32, kind="ExternalOutput").ap()
    xown_d = nc.dram_tensor("xown", [TPC, DM], F32, kind="ExternalInput").ap()
    eo_d = nc.dram_tensor("eoall", [NEXP * CAP, DM], BF16, kind="ExternalInput").ap()
    gidx_d = nc.dram_tensor("gidx", [TPC, 1], I32, kind="ExternalInput").ap()
    keep2_d = nc.dram_tensor("keep2", [TPC, 1], F32, kind="ExternalInput").ap()
    io = {
        "xown": lambda t: xown_d[t * P:(t + 1) * P, :],
        "gidx": lambda t: gidx_d[t * P:(t + 1) * P, :],
        "keep2": lambda t: keep2_d[t * P:(t + 1) * P, :],
        "eoall": eo_d,
        "xo": lambda t: xo_d[t * P:(t + 1) * P, :],
    }
    with k.ctx:
        k.start()
        with nc.Block() as block:
            @block.sync
            def _(_h):
                emit_moeC(k, "mc_", k.ctx, io)
                k.sy.w(k.sy.dmark())
    return k.nc


def run_moe(x, rw, w1, b1, w2, b2, ident, triu):
    ncB = _get("moeB", nc_moeB)
    ncC = _get("moeC", nc_moeC)
    in_maps = []
    for c in range(NCORES):
        in_maps.append({
            "xf": x, "rw": rw,
            "w1": np.ascontiguousarray(w1[c]), "b1": np.ascontiguousarray(b1[c]),
            "w2": np.ascontiguousarray(w2[c]), "b2": np.ascontiguousarray(b2[c]),
            "ident": ident, "triu": triu,
            "esel": np.tile(np.eye(NEXP, dtype=np.float32)[c:c + 1], (1, 4)),
            "e8": np.tile(np.arange(NEXP, dtype=np.float32).reshape(1, NEXP), (1, 4)),
        })
    res = run_bass_kernel_spmd(ncB, in_maps, core_ids=list(range(NCORES)))
    eo_all = np.concatenate([res.results[c]["eo"] for c in range(NCORES)], axis=0)
    gidx_all = res.results[0]["gidx"]
    keep2_all = res.results[0]["keep2"]
    in_maps2 = []
    for c in range(NCORES):
        own = own_rows(c)
        in_maps2.append({
            "xown": np.ascontiguousarray(x[own]),
            "eoall": eo_all,
            "gidx": np.ascontiguousarray(gidx_all[own]),
            "keep2": np.ascontiguousarray(keep2_all[own]),
        })
    res2 = run_bass_kernel_spmd(ncC, in_maps2, core_ids=list(range(NCORES)))
    out = np.empty((NTOK, DM), np.float32)
    for c in range(NCORES):
        out[own_rows(c)] = res2.results[c]["xo"]
    return out


def run_attn1(x, W, ident):
    nc = _get("attn1", nc_attn1)
    in_maps = []
    for c in range(NCORES):
        b, r = c // 4, c % 4
        own = own_rows(c)
        in_maps.append({
            "xown": np.ascontiguousarray(x[own]),
            "xb": np.ascontiguousarray(x[b * SEQ:(b + 1) * SEQ]),
            "ident": ident,
            "w_dq": W["W_DQ"], "b_dq": W["b_DQ"], "w_uq": W["W_UQ"], "b_uq": W["b_UQ"],
            "w_dkv": W["W_DKV"], "b_dkv": W["b_DKV"], "w_uk": W["W_UK"], "b_uk": W["b_UK"],
            "w_uv": W["W_UV"], "b_uv": W["b_UV"], "w_qr": W["W_QR"], "b_qr": W["b_QR"],
            "w_kr": W["W_KR"], "b_kr": W["b_KR"], "w_o": W["W_O"], "b_o": W["b_O"],
        })
    res = run_bass_kernel_spmd(nc, in_maps, core_ids=list(range(NCORES)))
    out = np.empty((NTOK, DM), np.float32)
    for c in range(NCORES):
        out[own_rows(c)] = res.results[c]["xo"]
    return out


def own_rows(c):
    b, r = c // 4, c % 4
    rows = []
    for j in range(TILES_PC):
        t = r + 4 * j
        start = b * SEQ + t * P
        rows.append(np.arange(start, start + P))
    return np.concatenate(rows)


def run_ffn1(x, wgq, wuq, wdq, ident):
    nc = _get("ffn1", nc_ffn1)
    in_maps = []
    for c in range(NCORES):
        in_maps.append({
            "xs": np.ascontiguousarray(x[c * TPC:(c + 1) * TPC]),
            "wgq": wgq, "wuq": wuq, "wdq": wdq, "ident": ident,
        })
    res = run_bass_kernel_spmd(nc, in_maps, core_ids=list(range(NCORES)))
    return np.concatenate([res.results[c]["xo"] for c in range(NCORES)], axis=0)


def run_ffn1_prep(wg, wu, wd):
    nc = _get("ffn1prep", nc_ffn1_prep)
    RG, RD = DI1 // NCORES, DM // NCORES
    in_maps = [{"wg": np.ascontiguousarray(wg[c * RG:(c + 1) * RG]),
                "wu": np.ascontiguousarray(wu[c * RG:(c + 1) * RG]),
                "wd": np.ascontiguousarray(wd[c * RD:(c + 1) * RD])} for c in range(NCORES)]
    res = run_bass_kernel_spmd(nc, in_maps, core_ids=list(range(NCORES)))
    wgq = np.concatenate([res.results[c]["wgq"] for c in range(NCORES)], axis=0)
    wuq = np.concatenate([res.results[c]["wuq"] for c in range(NCORES)], axis=0)
    wdq = np.concatenate([res.results[c]["wdq"] for c in range(NCORES)], axis=0)
    return wgq, wuq, wdq



GROUPS8 = [[0, 1, 2, 3, 4, 5, 6, 7]]
GROUPS24 = [[0, 1, 2, 3], [4, 5, 6, 7]]
PACKW = 6 * DM + DI1


def _rm_global(g):
    b, t = g // 64, g % 64
    rank = b * 4 + (t % 4)
    slot = t // 4
    return rank * TPC + slot * P


def _rm_batch(t):
    return (t % 4) * TPC + (t // 4) * P


def nc_mega(stages=6):
    k = K("mega")
    nc = k.nc
    xout_d = nc.dram_tensor("xout", [TPC, DM], F32, kind="ExternalOutput").ap()
    xmy0_d = nc.dram_tensor("xmy0", [TPC, DM], F32, kind="ExternalInput").ap()
    xfull_d = nc.dram_tensor("xfull", [NTOK, DM], F32, kind="ExternalInput").ap()
    ident_d = nc.dram_tensor("ident", [P, P], F32, kind="ExternalInput").ap()
    ownidx_d = nc.dram_tensor("ownidx", [TPC, 1], I32, kind="ExternalInput").ap()
    moe_io = {
        "rw": nc.dram_tensor("rw", [NEXP, DM], F32, kind="ExternalInput").ap(),
        "w1": nc.dram_tensor("w1", [DM, DFF0], F32, kind="ExternalInput").ap(),
        "b1": nc.dram_tensor("b1", [DFF0], F32, kind="ExternalInput").ap(),
        "w2": nc.dram_tensor("w2", [DFF0, DM], F32, kind="ExternalInput").ap(),
        "b2": nc.dram_tensor("b2", [DM], F32, kind="ExternalInput").ap(),
        "triu": nc.dram_tensor("triu", [P, P], F32, kind="ExternalInput").ap(),
        "esel": nc.dram_tensor("esel", [1, 4 * NEXP], F32, kind="ExternalInput").ap(),
        "e8": nc.dram_tensor("e8", [1, 4 * NEXP], F32, kind="ExternalInput").ap(),
    }
    a0_io = {
        "wq": nc.dram_tensor("wq", [H0, DM, E0], F32, kind="ExternalInput").ap(),
        "wk": nc.dram_tensor("wk", [H0, DM, E0], F32, kind="ExternalInput").ap(),
        "wv": nc.dram_tensor("wv", [H0, DM, E0], F32, kind="ExternalInput").ap(),
        "wo": nc.dram_tensor("wo", [DM, DM], F32, kind="ExternalInput").ap(),
        "gam": nc.dram_tensor("gam", [DM], F32, kind="ExternalInput").ap(),
        "bet": nc.dram_tensor("bet", [DM], F32, kind="ExternalInput").ap(),
        "m16": nc.dram_tensor("m16", [16, P, 512], F32, kind="ExternalInput").ap(),
    }
    a1w = [("w_dq", [DM, DC1]), ("b_dq", [DC1]), ("w_uq", [DC1, DM]), ("b_uq", [DM]),
           ("w_dkv", [DM, DC1]), ("b_dkv", [DC1]), ("w_uk", [DC1, DM]), ("b_uk", [DM]),
           ("w_uv", [DC1, DM]), ("b_uv", [DM]), ("w_qr", [DC1, 256]), ("b_qr", [256]),
           ("w_kr", [DM, DR1]), ("b_kr", [DR1]), ("w_o", [DM, DM]), ("b_o", [DM])]
    a1_io = {nm: nc.dram_tensor(nm, sh, F32, kind="ExternalInput").ap() for nm, sh in a1w}
    prep_io = {
        "p_wg": nc.dram_tensor("p_wg", [3, P, DM], F32, kind="ExternalInput").ap(),
        "p_wu": nc.dram_tensor("p_wu", [3, P, DM], F32, kind="ExternalInput").ap(),
        "p_wd": nc.dram_tensor("p_wd", [1, P, DI1], F32, kind="ExternalInput").ap(),
    }
    xmy_d = nc.dram_tensor("xmy", [TPC, DM], F32).ap()
    xagB_d = nc.dram_tensor("xagB", [4 * TPC, DM], F32).ap()
    xagF_d = nc.dram_tensor("xagF", [NTOK, DM], F32).ap()
    eomy_d = nc.dram_tensor("eomy", [CAP, DM], BF16).ap()
    eoag_d = nc.dram_tensor("eoag", [NEXP * CAP, DM], BF16).ap()
    buf_d = nc.dram_tensor("buf_scr", [CAP + 1, DM], BF16).ap()
    gidx_d = nc.dram_tensor("gidx_scr", [NTOK, 1], I32).ap()
    keep2_d = nc.dram_tensor("keep2_scr", [NTOK, 1], F32).ap()
    hq_d = nc.dram_tensor("hq_scr", [TPC, DI1], BF16).ap()
    kT_d = nc.dram_tensor("kT_scr", [DM, SEQ], BF16).ap()
    v_d = nc.dram_tensor("v_scr", [SEQ, DM], BF16).ap()
    pack_d = nc.dram_tensor("pack_scr", [P, PACKW], BF16).ap()
    packag_d = nc.dram_tensor("packag_scr", [NCORES * P, PACKW], BF16).ap()

    moe_io.update({
        "eo_d": eomy_d, "gidx_d": gidx_d, "keep2_d": keep2_d, "buf_d": buf_d,
    })
    prep_io["pack"] = pack_d

    with k.ctx:
        k.start()
        k.ident = k.sbuf("g_ident", [P, P], BF16)
        k.ones1f = k.sbuf("g_ones1f", [1, P], F32)
        k.ones1b = k.sbuf("g_ones1b", [1, P], BF16)
        k.ones_col = k.sbuf("g_ones_col", [P, 1], BF16)
        with nc.Block() as block:
            @block.sync
            def _(_h):
                m_id = k.gp.dma(k.gp.r.dma_start(out=k.ident[:], in_=ident_d[:, :]))
                k.ve.op(k.ve.r.memset(k.ones1f[:], 1.0))
                k.ve.op(k.ve.r.memset(k.ones1b[:], 1.0))
                m_on = k.ve.op(k.ve.r.memset(k.ones_col[:], 1.0))
                base_after = (m_id, m_on)

                with contextlib.ExitStack() as sctx:
                    prep_io["after"] = base_after
                    emit_prep(k, "pp_", sctx, prep_io)
                    barrier(k)
                m_wq = emit_ag(k, pack_d[:, :], packag_d[:, :], GROUPS8)

                def moe_pass(xf_fn, xown_fn, tag, after=()):
                    with contextlib.ExitStack() as sctx:
                        mio = dict(moe_io)
                        mio["xf"] = xf_fn
                        mio["after"] = base_after + tuple(after)
                        emit_moeB(k, tag + "b_", sctx, mio)
                        barrier(k)
                    m_eo = emit_ag(k, eomy_d[:, :], eoag_d[:, :], GROUPS8)
                    with contextlib.ExitStack() as sctx:
                        cio = {
                            "xown": xown_fn, "eoall": eoag_d,
                            "ownidx": ownidx_d, "gidx_full": gidx_d, "keep2_full": keep2_d,
                            "gidx": None, "keep2": None,
                            "xo": lambda t: xmy_d[t * P:(t + 1) * P, :],
                            "after": base_after + (m_eo,),
                        }
                        emit_moeC(k, tag + "c_", sctx, cio)
                        barrier(k)

                moe_pass(lambda g: xfull_d[g * P:(g + 1) * P, :],
                         lambda t: xmy0_d[t * P:(t + 1) * P, :], "m1")
                if stages >= 2:
                    m_b1 = emit_ag(k, xmy_d[:, :], xagB_d[:, :], GROUPS24)
                if stages >= 2:
                  with contextlib.ExitStack() as sctx:
                    aio = dict(a0_io)
                    aio["kT_d"] = kT_d
                    aio["v_d"] = v_d
                    aio["xb"] = lambda t: xagB_d[_rm_batch(t):_rm_batch(t) + P, :]
                    aio["xown"] = lambda t: xmy_d[t * P:(t + 1) * P, :]
                    aio["xo"] = lambda t: xmy_d[t * P:(t + 1) * P, :]
                    aio["after"] = base_after + (m_b1,)
                    emit_attn0(k, "a0_", sctx, aio)
                    barrier(k)
                if stages >= 3:
                    m_f1 = emit_ag(k, xmy_d[:, :], xagF_d[:, :], GROUPS8)
                    moe_pass(lambda g: xagF_d[_rm_global(g):_rm_global(g) + P, :],
                             lambda t: xmy_d[t * P:(t + 1) * P, :], "m2", after=(m_f1,))

                if stages >= 4:
                  with contextlib.ExitStack() as sctx:
                    fio = {
                        "x": lambda t: xmy_d[t * P:(t + 1) * P, :],
                        "xo": lambda t: xmy_d[t * P:(t + 1) * P, :],
                        "wgq_t": lambda t: packag_d[(t % 8) * P:(t % 8 + 1) * P, (t // 8) * DM:(t // 8 + 1) * DM],
                        "wuq_t": lambda t: packag_d[(t % 8) * P:(t % 8 + 1) * P, (3 + t // 8) * DM:(4 + t // 8) * DM],
                        "wdq_t": lambda t: packag_d[(t % 8) * P:(t % 8 + 1) * P, 6 * DM:6 * DM + DI1],
                        "hq_d": hq_d,
                        "after": base_after + (m_wq,),
                    }
                    emit_ffn1(k, "f1_", sctx, fio)
                    barrier(k)
                if stages >= 5:
                  m_b2 = emit_ag(k, xmy_d[:, :], xagB_d[:, :], GROUPS24)
                  with contextlib.ExitStack() as sctx:
                    aio = dict(a1_io)
                    aio["xb"] = lambda t: xagB_d[_rm_batch(t):_rm_batch(t) + P, :]
                    aio["xown"] = lambda t: xmy_d[t * P:(t + 1) * P, :]
                    aio["xo"] = lambda t: xmy_d[t * P:(t + 1) * P, :]
                    aio["after"] = base_after + (m_b2,)
                    emit_attn1(k, "a1_", sctx, aio)
                    barrier(k)
                if stages >= 6:
                  with contextlib.ExitStack() as sctx:
                    fio = {
                        "x": lambda t: xmy_d[t * P:(t + 1) * P, :],
                        "xo": lambda t: xout_d[t * P:(t + 1) * P, :],
                        "wgq_t": lambda t: packag_d[(t % 8) * P:(t % 8 + 1) * P, (t // 8) * DM:(t // 8 + 1) * DM],
                        "wuq_t": lambda t: packag_d[(t % 8) * P:(t % 8 + 1) * P, (3 + t // 8) * DM:(4 + t // 8) * DM],
                        "wdq_t": lambda t: packag_d[(t % 8) * P:(t % 8 + 1) * P, 6 * DM:6 * DM + DI1],
                        "hq_d": hq_d,
                        "after": base_after,
                    }
                    emit_ffn1(k, "f2_", sctx, fio)
                if stages < 6:
                    barrier(k)
                    for t in range(TILES_PC):
                        k.sy.dma(k.sy.r.dma_start(out=xout_d[t * P:(t + 1) * P, :],
                                                  in_=xmy_d[t * P:(t + 1) * P, :]))
                k.sy.w(k.sy.dmark())
                k.sy.w(k.gp.dmark())
    return k.nc


def run_mega(inputs, stages=6):
    nc = _get("mega%d" % stages, lambda: nc_mega(stages))
    ins = inputs
    x = ins["L0_attn_x"].reshape(NTOK, DM)
    ident = np.eye(P, dtype=np.float32)
    triu = np.triu(np.ones((P, P), np.float32), 1)
    wg, wu, wd = ins["L1_ffn_w_gate"], ins["L1_ffn_w_up"], ins["L1_ffn_w_down"]
    in_maps = []
    for c in range(NCORES):
        b, r = c // 4, c % 4
        own = own_rows(c)
        gt = [min(c, 21), min(8 + c, 21), min(16 + c, 21)]
        im = {
            "xmy0": np.ascontiguousarray(x[own]),
            "xfull": x,
            "ident": ident,
            "ownidx": own.reshape(TPC, 1).astype(np.int32),
            "rw": ins["L0_ffn_router_w"],
            "w1": np.ascontiguousarray(ins["L0_ffn_w1"][c]),
            "b1": np.ascontiguousarray(ins["L0_ffn_b1"][c]),
            "w2": np.ascontiguousarray(ins["L0_ffn_w2"][c]),
            "b2": np.ascontiguousarray(ins["L0_ffn_b2"][c]),
            "triu": triu,
            "esel": np.tile(np.eye(NEXP, dtype=np.float32)[c:c + 1], (1, 4)),
            "e8": np.tile(np.arange(NEXP, dtype=np.float32).reshape(1, NEXP), (1, 4)),
            "wq": ins["L0_attn_Wq"], "wk": ins["L0_attn_Wk"], "wv": ins["L0_attn_Wv"],
            "wo": ins["L0_attn_Wo"], "gam": ins["L0_attn_gamma"], "bet": ins["L0_attn_beta"],
            "m16": build_mask16(r),
            "p_wg": np.ascontiguousarray(np.stack([wg[t * P:(t + 1) * P] for t in gt])),
            "p_wu": np.ascontiguousarray(np.stack([wu[t * P:(t + 1) * P] for t in gt])),
            "p_wd": np.ascontiguousarray(wd[c * P:(c + 1) * P][None]),
        }
        for nm in ("W_DQ", "b_DQ", "W_UQ", "b_UQ", "W_DKV", "b_DKV", "W_UK", "b_UK",
                   "W_UV", "b_UV", "W_QR", "b_QR", "W_KR", "b_KR", "W_O", "b_O"):
            im[nm.lower() if nm.startswith("b") else nm.lower()] = ins["L1_attn_" + nm]
        in_maps.append(im)
    res = run_bass_kernel_spmd(nc, in_maps, core_ids=list(range(NCORES)))
    out = np.empty((NTOK, DM), np.float32)
    for c in range(NCORES):
        out[own_rows(c)] = res.results[c]["xout"]
    return out.reshape(BATCH, SEQ, DM)


def kernel(**inputs):
    ins = {kk: np.ascontiguousarray(np.asarray(v)) for kk, v in inputs.items()}
    return run_mega(ins)


def kernel_unfused(**inputs):
    ins = {kk: np.ascontiguousarray(np.asarray(v)) for kk, v in inputs.items()}
    x = ins["L0_attn_x"].reshape(NTOK, DM).astype(np.float32, copy=True)
    ident = np.eye(P, dtype=np.float32)
    triu = np.triu(np.ones((P, P), np.float32), 1)
    W1 = {kk.replace("L1_attn_", ""): ins[kk] for kk in ins if kk.startswith("L1_attn_")}
    wgq, wuq, wdq = run_ffn1_prep(ins["L1_ffn_w_gate"], ins["L1_ffn_w_up"], ins["L1_ffn_w_down"])
    moe_args = (ins["L0_ffn_router_w"], ins["L0_ffn_w1"], ins["L0_ffn_b1"],
                ins["L0_ffn_w2"], ins["L0_ffn_b2"], ident, triu)
    x = run_moe(x, *moe_args)
    x = run_attn0(x, ins["L0_attn_Wq"], ins["L0_attn_Wk"], ins["L0_attn_Wv"],
                  ins["L0_attn_Wo"], ins["L0_attn_gamma"], ins["L0_attn_beta"], ident)
    x = run_moe(x, *moe_args)
    x = run_ffn1(x, wgq, wuq, wdq, ident)
    x = run_attn1(x, W1, ident)
    x = run_ffn1(x, wgq, wuq, wdq, ident)
    return x.reshape(BATCH, SEQ, DM)
